# version 11
# baseline (speedup 1.0000x reference)
import numpy as np
import ml_dtypes
from contextlib import ExitStack
import concourse.bass as bass
import concourse.mybir as mybir
from concourse.bass_utils import run_bass_kernel_spmd

F32 = mybir.dt.float32
BF16 = mybir.dt.bfloat16
AF = mybir.ActivationFunctionType
ALU = mybir.AluOpType
AX = mybir.AxisListType

T = 2048
D = 1024
NT = T // 128
ND = D // 128
ROUND_C = float(1.5 * 2.0**23)
PI = float(np.pi)
LN_EPS = 1e-5
QEPS = 1e-5


class SemCtr:
    def __init__(self, stack, nc, name):
        self.h = stack.enter_context(nc.semaphore(name))
        self.n = 0
        self.name = name

    def inc(self, instr, k=1):
        instr.then_inc(self.h, k)
        self.n += k
        return instr

    @property
    def t(self):
        return (self, self.n)


def wt(eng, *tickets):
    for t in tickets:
        if t is None:
            continue
        ctr, val = t
        if val > 0:
            eng.wait_ge(ctr.h, val)


class Bank:
    def __init__(self, tens):
        self.t = tens
        self.mm_t = None
        self.free_t = None


class Ctx:
    pass


def make_ctx(stack, nc, block):
    c = Ctx()
    c.nc = nc
    c.block = block
    sb = lambda name, shape, dt: stack.enter_context(nc.sbuf_tensor(name, shape, dt))
    ps = lambda name, shape, dt: stack.enter_context(nc.psum_tensor(name, shape, dt))
    c.xT = sb("xT", [128, ND * T], F32)
    c.xln = sb("xln", [128, ND * T], BF16)
    c.arenaA = sb("arenaA", [128, 32768], BF16)
    c.wbuf = sb("wbuf", [128, 8192], BF16)
    c.tmp0 = sb("tmp0", [128, 512], F32)
    c.tmp1 = sb("tmp1", [128, 512], F32)
    c.ones1 = sb("ones1", [1, 128], F32)
    c.onescol = sb("onescol", [128, 1], BF16)
    c.onescol32 = sb("onescol32", [128, 1], F32)
    c.id_bf = sb("id_bf", [128, 128], BF16)
    c.id_f32 = sb("id_f32", [128, 128], F32)
    c.consts = sb("consts", [1, 16], F32)
    c.gacc = sb("gacc", [128, 8], F32)
    c.bias = sb("bias", [128, 96], F32)
    c.moe_w = sb("moe_w", [128, 128], BF16)
    c.mask = sb("mask", [128, 128], F32)
    c.mask_bf = sb("mask_bf", [128, 128], BF16)
    c.poscol = sb("poscol", [128, 32], F32)
    c.cwt = sb("cwt", [128, 32], F32)
    c.cws = sb("cws", [128, 8], F32)
    c.cw8 = sb("cw8", [8, 2048], F32)
    c.cwrow = sb("cwrow", [1, 1024], F32)
    c.cwB = sb("cwB", [128, 2048], F32)
    c.pbig_t = stack.enter_context(nc.psum_tensor("pbig", [128, 2048], F32))
    c.pbig = c.pbig_t[:, :]
    c.pb = [Bank(c.pbig_t[:, 512 * i:512 * (i + 1)]) for i in range(4)]
    c.pb.append(Bank(ps("pb4", [128, 512], F32)))
    c.pb.append(Bank(ps("pb5", [128, 512], F32)))
    c.pt = [Bank(ps(f"pt{i}", [128, 1024], BF16)) for i in range(2)]
    c.pbigB = Bank(None)
    c.pe = SemCtr(stack, nc, "pe")
    c.sA = SemCtr(stack, nc, "sA")
    c.sA2 = SemCtr(stack, nc, "sA2")
    c.sV = SemCtr(stack, nc, "sV")
    c.sP = SemCtr(stack, nc, "sP")
    c.dx = [SemCtr(stack, nc, f"dx{i}") for i in range(8)]
    c.dw = [SemCtr(stack, nc, f"dw{i}") for i in range(16)]
    c.arenaAf = c.arenaA[:, :].bitcast(F32)
    return c


def emit_consts(c):
    @c.block.gpsimd
    def _(g):
        g.memset(c.ones1[:], 1.0)
        g.memset(c.onescol[:], 1.0)
        g.memset(c.onescol32[:], 1.0)
        g.memset(c.consts[0:1, 0:1], LN_EPS)
        g.memset(c.consts[0:1, 1:2], PI / 2)
        g.memset(c.bias[:, 56:57], PI / 2)
        c.sP.inc(g.memset(c.id_bf[:], 0.0))
        c.sP.inc(g.memset(c.id_f32[:], 0.0))
        g.wait_ge(c.sP.h, c.sP.n)
        c.sP.inc(g.affine_select(out=c.id_bf[:], in_=c.id_bf[:], compare_op=ALU.not_equal,
                                 fill=1.0, base=0, pattern=[[-1, 128]], channel_multiplier=1))
        c.sP.inc(g.affine_select(out=c.id_f32[:], in_=c.id_f32[:], compare_op=ALU.not_equal,
                                 fill=1.0, base=0, pattern=[[-1, 128]], channel_multiplier=1))
        c.sP.inc(g.memset(c.mask[:], 0.0))
        c.sP.inc(g.memset(c.mask_bf[:], 0.0))
        g.wait_ge(c.sP.h, c.sP.n)
        c.sP.inc(g.affine_select(out=c.mask[:], in_=c.mask[:], compare_op=ALU.is_ge,
                                 fill=-10000.0, base=0, pattern=[[-1, 128]],
                                 channel_multiplier=1))
        c.sP.inc(g.affine_select(out=c.mask_bf[:], in_=c.mask_bf[:], compare_op=ALU.is_ge,
                                 fill=-10000.0, base=0, pattern=[[-1, 128]],
                                 channel_multiplier=1))
    return c.sP.t


def _transpose_pipe(c, pairs, src_of, dst_of, ident, banks, pre_wait=None,
                    evac="scalar", quad_wait=None):
    nb = len(banks)
    ctr = c.sA if evac == "scalar" else c.sV
    pe0, e0 = c.pe.n, ctr.n
    n = len(pairs)
    quads = [list(range(q, min(q + 4, n))) for q in range(0, n, 4)]
    free0 = [b.free_t for b in banks]
    pe_wait = []
    mm_t = []
    ev_cnt = 0
    for g, q in enumerate(quads):
        pe_wait.append(free0[g % nb] if g < nb else (ctr, e0 + (g - nb + 1) * 4))
        mm_t.append((c.pe, pe0 + g + 1))

    @c.block.tensor
    def _(t):
        if pre_wait is not None:
            pre_wait(t)
        for g, q in enumerate(quads):
            bank = banks[g % nb]
            if quad_wait is not None:
                quad_wait(t, g)
            wt(t, pe_wait[g])
            for j, i in enumerate(q):
                ins = t.transpose(bank.t[:, 128 * j:128 * (j + 1)], src_of(pairs[i]), ident)
            c.pe.inc(ins)

    sec = getattr(c.block, evac)

    @sec
    def _(e):
        for g, q in enumerate(quads):
            bank = banks[g % nb]
            wt(e, mm_t[g])
            for j, i in enumerate(q):
                ins = e.copy(out=dst_of(pairs[i]), in_=bank.t[:, 128 * j:128 * (j + 1)]) \
                    if evac == "scalar" else \
                    e.tensor_copy(out=dst_of(pairs[i]), in_=bank.t[:, 128 * j:128 * (j + 1)])
                ctr.inc(ins)

    for g, q in enumerate(quads):
        banks[g % nb].mm_t = mm_t[g]
        banks[g % nb].free_t = (ctr, e0 + (g + 1) * 4 if len(q) == 4 else e0 + g * 4 + len(q))
    return ctr.t


def emit_load_x_transposed(c, x_hbm, const_t):
    stage = c.arenaAf
    xr = x_hbm.rearrange("(ti p) d -> p ti d", p=128)

    ld_sems = [c.dx[0], c.dx[3], c.dx[4], c.dx[5]]
    ld_tk = []

    @c.block.sync
    def _(s):
        for i in range(4):
            ld_sems[i].inc(s.dma_start(out=stage[:, 4096 * i:4096 * (i + 1)],
                                       in_=xr[:, 4 * i:4 * (i + 1), :]), 16)
            ld_tk.append(ld_sems[i].t)

    pairs = [(ti, k) for ti in range(NT) for k in range(ND)]

    def pre(t):
        wt(t, const_t)

    def quad_wait(t, g):
        ti = pairs[4 * g][0]
        wt(t, ld_tk[ti // 4])

    e0 = c.sA.n
    full_t = _transpose_pipe(
        c, pairs,
        src_of=lambda it: stage[:, 1024 * it[0] + 128 * it[1]:1024 * it[0] + 128 * (it[1] + 1)],
        dst_of=lambda it: c.xT[:, T * it[1] + 128 * it[0]:T * it[1] + 128 * (it[0] + 1)],
        ident=c.id_f32[:], banks=c.pb[0:4], pre_wait=pre, quad_wait=quad_wait)
    x_ch = [(c.sA, e0 + 32 * (ch + 1)) for ch in range(4)]
    return full_t, x_ch


def emit_ln(c, ready_t, ready_chunk=None, apply_extra=None, tscratch=None,
            sq_engine="scalar", rows_in_quarters=False):
    af = c.arenaAf
    if rows_in_quarters:
        mrow = [af[0:1, 4096 * ch:4096 * ch + 512] for ch in range(4)]
        vrow = [af[0:1, 4096 * ch + 512:4096 * ch + 1024] for ch in range(4)]
        trow = [af[0:1, 4096 * ch + 1024:4096 * ch + 1536] for ch in range(4)]
    else:
        mrow = [af[0:1, 4096 + 512 * ch:4096 + 512 * (ch + 1)] for ch in range(4)]
        vrow = [af[0:1, 6144 + 512 * ch:6144 + 512 * (ch + 1)] for ch in range(4)]
        trow = [af[0:1, 8192 + 512 * ch:8192 + 512 * (ch + 1)] for ch in range(4)]
    if tscratch is None:
        tscratch = [c.arenaAf[:, 10240 + 512 * i:10240 + 512 * (i + 1)]
                    for i in range(4)]
    rdy = [ready_chunk[ch] if ready_chunk else ready_t for ch in range(4)]

    a0, p0, v0 = c.sA.n, c.pe.n, c.sV.n
    on_act = sq_engine == "scalar"
    if on_act:
        sq_t = [(c.sA, a0 + 8 * (ch + 1)) for ch in range(4)]
        a2_t = [(c.sA, a0 + 32 + 2 * (ch + 1)) for ch in range(4)]
        vbase = v0
    else:
        sq_t = [(c.sV, v0 + 8 * (ch + 1)) for ch in range(4)]
        a2_t = [(c.sA, a0 + 2 * (ch + 1)) for ch in range(4)]
        vbase = v0 + 32
    sums_mm = [(c.pe, p0 + n) for n in (2, 4, 8, 12)]
    rep_mm = [(c.pe, p0 + n) for n in (6, 10, 14, 16)]
    stats_t = [(c.sV, vbase + n) for n in (6, 12, 34, 56)]
    sum_free = [(c.sV, vbase + n) for n in (2, 8, 30, 52)]
    apply_t = [(c.sV, vbase + n) for n in (28, 50, 72, 88)]

    fs0, fq0 = c.pb[4].free_t, c.pb[5].free_t
    pair_free = [[c.pb[0].free_t, c.pb[1].free_t],
                 [c.pb[2].free_t, c.pb[3].free_t]]

    sq_sec = c.block.scalar if on_act else c.block.vector
    ctr = c.sA if on_act else c.sV
    for ch in range(4):
        sl = slice(512 * ch, 512 * (ch + 1))

        @sq_sec
        def _(e, ch=ch, sl=sl):
            wt(e, rdy[ch])
            for k in range(ND):
                if on_act:
                    ctr.inc(e.activation(out=c.xln[:, T * k + sl.start:T * k + sl.stop],
                                         in_=c.xT[:, T * k + sl.start:T * k + sl.stop],
                                         func=AF.Square))
                else:
                    ctr.inc(e.tensor_tensor(out=c.xln[:, T * k + sl.start:T * k + sl.stop],
                                            in0=c.xT[:, T * k + sl.start:T * k + sl.stop],
                                            in1=c.xT[:, T * k + sl.start:T * k + sl.stop],
                                            op=ALU.mult))
                    e.wait_ge(ctr.h, ctr.n)

    def emit_sums(ch):
        sl = slice(512 * ch, 512 * (ch + 1))

        @c.block.tensor
        def _(t, ch=ch, sl=sl):
            wt(t, rdy[ch], fs0 if ch == 0 else sum_free[ch - 1])
            for k in range(ND):
                ins = t.matmul(c.pb[4].t[0:1, :], c.onescol32[:, :],
                               c.xT[:, T * k + sl.start:T * k + sl.stop],
                               start=(k == 0), stop=(k == ND - 1))
            c.pe.inc(ins)
            wt(t, sq_t[ch], fq0 if ch == 0 else None)
            for k in range(ND):
                ins = t.matmul(c.pb[5].t[0:1, :], c.onescol[:, :],
                               c.xln[:, T * k + sl.start:T * k + sl.stop],
                               start=(k == 0), stop=(k == ND - 1))
            c.pe.inc(ins)

    def emit_rep(ch):
        sl = slice(512 * ch, 512 * (ch + 1))
        p = ch % 2
        bmean, brstd = c.pb[2 * p], c.pb[2 * p + 1]
        fm = pair_free[p][0] if ch < 2 else apply_t[ch - 2]
        fr = pair_free[p][1] if ch < 2 else apply_t[ch - 2]

        @c.block.tensor
        def _(t, ch=ch, sl=sl, bmean=bmean, brstd=brstd, fm=fm, fr=fr):
            wt(t, stats_t[ch], fm)
            c.pe.inc(t.matmul(bmean.t[:, :], c.ones1[:, :], mrow[ch],
                              start=True, stop=True))
            wt(t, a2_t[ch], fr)
            c.pe.inc(t.matmul(brstd.t[:, :], c.ones1[:, :], vrow[ch],
                              start=True, stop=True))
        bmean.mm_t = rep_mm[ch]
        brstd.mm_t = rep_mm[ch]
        bmean.free_t = apply_t[ch]
        brstd.free_t = apply_t[ch]

    for step in (emit_sums, 0), (emit_sums, 1), (emit_rep, 0), (emit_sums, 2),             (emit_rep, 1), (emit_sums, 3), (emit_rep, 2), (emit_rep, 3):
        step[0](step[1])

    for ch in range(4):
        sl = slice(512 * ch, 512 * (ch + 1))

        @c.block.scalar
        def _(a, ch=ch, sl=sl):
            wt(a, stats_t[ch])
            c.sA.inc(a.activation(out=vrow[ch], in_=vrow[ch], func=AF.Ln,
                                  bias=c.consts[0:1, 0:1]))
            a.wait_ge(c.sA.h, c.sA.n)
            c.sA.inc(a.activation(out=vrow[ch], in_=vrow[ch], func=AF.Exp,
                                  scale=-0.5))

    def emit_stats(ch):
        sl = slice(512 * ch, 512 * (ch + 1))

        @c.block.vector
        def _(v, ch=ch, sl=sl):
            wt(v, sums_mm[ch])
            c.sV.inc(v.tensor_copy(out=mrow[ch], in_=c.pb[4].t[0:1, :]))
            c.sV.inc(v.tensor_copy(out=vrow[ch], in_=c.pb[5].t[0:1, :]))
            v.wait_ge(c.sV.h, c.sV.n)
            c.sV.inc(v.tensor_scalar(out=mrow[ch], in0=mrow[ch],
                                     scalar1=1.0 / D, scalar2=None, op0=ALU.mult))
            c.sV.inc(v.tensor_scalar(out=vrow[ch], in0=vrow[ch],
                                     scalar1=1.0 / D, scalar2=None, op0=ALU.mult))
            v.wait_ge(c.sV.h, c.sV.n)
            c.sV.inc(v.scalar_tensor_tensor(out=trow[ch], in0=mrow[ch],
                                            scalar=-1.0, in1=mrow[ch],
                                            op0=ALU.mult, op1=ALU.mult))
            v.wait_ge(c.sV.h, c.sV.n)
            c.sV.inc(v.tensor_tensor(out=vrow[ch], in0=vrow[ch],
                                     in1=trow[ch], op=ALU.add))

    def emit_apply(ch):
        sl = slice(512 * ch, 512 * (ch + 1))
        p = ch % 2
        bmean, brstd = c.pb[2 * p], c.pb[2 * p + 1]
        tfull = tscratch[ch]

        @c.block.vector
        def _(v, ch=ch, sl=sl, bmean=bmean, brstd=brstd, tfull=tfull):
            wt(v, rep_mm[ch], apply_extra if ch < 2 else None)
            for k in range(ND):
                c.sV.inc(v.tensor_tensor(out=tfull[:, :],
                                         in0=c.xT[:, T * k + sl.start:T * k + sl.stop],
                                         in1=bmean.t[:, :], op=ALU.subtract))
                v.wait_ge(c.sV.h, c.sV.n)
                c.sV.inc(v.tensor_tensor(out=c.xln[:, T * k + sl.start:T * k + sl.stop],
                                         in0=tfull[:, :], in1=brstd.t[:, :], op=ALU.mult))
                v.wait_ge(c.sV.h, c.sV.n)

    for step in (emit_stats, 0), (emit_stats, 1), (emit_apply, 0), (emit_stats, 2),             (emit_apply, 1), (emit_stats, 3), (emit_apply, 2), (emit_apply, 3):
        step[0](step[1])

    c.pb[4].free_t = sum_free[3]
    c.pb[5].free_t = sum_free[3]
    c.pb[4].mm_t = (c.pe, p0 + 11)
    c.pb[5].mm_t = (c.pe, p0 + 12)
    assert c.sA.n == a0 + (40 if on_act else 8)
    assert c.pe.n == p0 + 16
    assert c.sV.n == vbase + 88
    return apply_t[3], list(apply_t)


def emit_store_x_tokenmajor(c, out_hbm, ready_t):
    stage = c.arenaAf
    pairs = [(ti, k) for ti in range(NT) for k in range(ND)]

    def pre(t):
        wt(t, ready_t)

    st_t = _transpose_pipe(
        c, pairs,
        src_of=lambda it: c.xT[:, T * it[1] + 128 * it[0]:T * it[1] + 128 * (it[0] + 1)],
        dst_of=lambda it: stage[:, 1024 * it[0] + 128 * it[1]:1024 * it[0] + 128 * (it[1] + 1)],
        ident=c.id_f32[:], banks=c.pb[0:4], pre_wait=pre)

    xr = out_hbm.rearrange("(ti p) d -> p ti d", p=128)

    @c.block.sync
    def _(s):
        wt(s, st_t)
        for i in range(4):
            c.dx[0].inc(s.dma_start(out=xr[:, 4 * i:4 * (i + 1), :],
                                    in_=stage[:, 4096 * i:4096 * (i + 1)]), 16)
    return c.dx[0].t


def emit_wait_store(c, st_t):
    @c.block.sync
    def _(s):
        wt(s, st_t)



def emit_gamma(c, w_hbm, n_elems, which, wbuf_free_t=None, use_tmp=False):
    eng_ctr = c.sA if use_tmp else c.sV
    sec_name = "scalar" if use_tmp else "vector"
    if use_tmp:
        chunk = 512
        bufs = [c.tmp0[:, :], c.tmp1[:, :]]
    else:
        chunk = 2048
        wf = c.wbuf[:, :].bitcast(F32)
        bufs = [wf[:, 0:2048], wf[:, 2048:4096]]
    per_part = n_elems // 128
    nchunks = per_part // chunk
    wr = w_hbm.rearrange("a b -> (a b)").rearrange("(p n) -> p n", p=128)

    e0 = eng_ctr.n
    dma_t = []

    @c.block.sync
    def _(s):
        for i in range(nchunks):
            if i >= 2:
                s.wait_ge(eng_ctr.h, e0 + 2 * (i - 2) + 1)
            else:
                wt(s, wbuf_free_t)
            sem = c.dx[2 + i % 2]
            sem.inc(s.dma_start(out=bufs[i % 2], in_=wr[:, chunk * i:chunk * (i + 1)]), 16)
            dma_t.append(sem.t)

    sec = getattr(c.block, sec_name)

    @sec
    def _(v):
        for i in range(nchunks):
            wt(v, dma_t[i])
            if i > 0:
                v.wait_ge(eng_ctr.h, e0 + 2 * i)
            if use_tmp:
                eng_ctr.inc(v.activation(out=bufs[i % 2], in_=bufs[i % 2], func=AF.Abs,
                                         accum_out=c.gacc[:, 4 + which:5 + which]))
                v.wait_ge(eng_ctr.h, eng_ctr.n)
                if i == 0:
                    ins = v.copy(out=c.gacc[:, which:which + 1],
                                 in_=c.gacc[:, 4 + which:5 + which])
                else:
                    ins = v.add(out=c.gacc[:, which:which + 1],
                                in_=c.gacc[:, 4 + which:5 + which],
                                add=c.gacc[:, which:which + 1])
                eng_ctr.inc(ins)
            else:
                eng_ctr.inc(v.tensor_reduce(out=c.gacc[:, 4 + which:5 + which],
                                            in_=bufs[i % 2],
                                            axis=AX.X, op=ALU.add, apply_absolute_value=True))
                v.wait_ge(eng_ctr.h, eng_ctr.n)
                if i == 0:
                    ins = v.tensor_copy(out=c.gacc[:, which:which + 1],
                                        in_=c.gacc[:, 4 + which:5 + which])
                else:
                    ins = v.tensor_tensor(out=c.gacc[:, which:which + 1],
                                          in0=c.gacc[:, which:which + 1],
                                          in1=c.gacc[:, 4 + which:5 + which], op=ALU.add)
                eng_ctr.inc(ins)
    acc_t = eng_ctr.t

    bank = c.pb[5]
    freew = bank.free_t

    @c.block.tensor
    def _(t):
        wt(t, acc_t, freew)
        c.pe.inc(t.matmul(bank.t[0:1, 0:1], c.gacc[:, which:which + 1],
                          c.onescol32[:, :], start=True, stop=True))
    mm1_t = c.pe.t

    @c.block.vector
    def _(v):
        wt(v, mm1_t)
        c.sV.inc(v.tensor_scalar(out=c.gacc[0:1, 6 + which:7 + which], in0=bank.t[0:1, 0:1],
                                 scalar1=1.0 / n_elems, scalar2=QEPS,
                                 op0=ALU.mult, op1=ALU.add))
        v.wait_ge(c.sV.h, c.sV.n)
        c.sV.inc(v.reciprocal(out=c.gacc[0:1, 6 + which:7 + which],
                              in_=c.gacc[0:1, 6 + which:7 + which]))
    rec_t = c.sV.t

    @c.block.tensor
    def _(t):
        wt(t, rec_t)
        c.pe.inc(t.matmul(bank.t[:, 0:1], c.ones1[:, :],
                          c.gacc[0:1, 6 + which:7 + which], start=True, stop=True))
    mm2_t = c.pe.t
    bank.mm_t = mm2_t

    @c.block.vector
    def _(v):
        wt(v, mm2_t)
        c.sV.inc(v.tensor_copy(out=c.gacc[:, 2 + which:3 + which], in_=bank.t[:, 0:1]))
    bank.free_t = c.sV.t
    return c.sV.t


def emit_bias_cols(c, col0, arr_hbm, ncols):
    @c.block.sync
    def _(s):
        c.dx[1].inc(s.dma_start(out=c.bias[:, col0:col0 + ncols], in_=arr_hbm), 16)
    return c.dx[1].t


def emit_l1ffn(c, xln_t, w1t, w2t, r1_t, r2_t, bias_t, wbuf_free_t, wbuf_free_pe_t,
               phases=("h", "out"), halves=(0, 1), xln_ch=None):
    wf = c.wbuf[:, :].bitcast(F32)
    h = c.arenaA

    for half in halves:
        hs0 = 1024 * half
        v0, pe0, a0 = c.sV.n, c.pe.n, c.sA.n
        dxa, dxb = c.dx[4].n, c.dx[5].n
        dma_t = [(c.dx[4 + m % 2], (dxa if m % 2 == 0 else dxb) + 16 * (m // 2 + 1))
                 for m in range(32)]
        qt_t = [(c.sV, v0 + 3 * (m + 1)) for m in range(32)]
        mm_t = {(m, c2): (c.pe, pe0 + 2 * m + c2 + 1)
                for m in range(32) for c2 in range(2)}
        ev_t = {(m, c2): (c.sA, a0 + 2 * m + c2 + 1)
                for m in range(32) for c2 in range(2)}
        bank_free0 = [c.pb[i].free_t for i in range(4)]

        @c.block.sync
        def _(s, v0=v0, wbuf_free_t=wbuf_free_t):
            for m in range(32):
                if m >= 2:
                    s.wait_ge(c.sV.h, v0 + 3 * (m - 1))
                else:
                    wt(s, wbuf_free_t)
                c.dx[4 + m % 2].inc(
                    s.dma_start(out=wf[:, 1024 * (m % 2):1024 * (m % 2 + 1)],
                                in_=w1t[:, 128 * m:128 * (m + 1)]
                                .rearrange("(k p) f -> p k f", p=128)), 16)

        @c.block.vector
        def _(v, wbuf_free_pe_t=wbuf_free_pe_t):
            wt(v, r1_t, wbuf_free_pe_t)
            for m in range(32):
                qsl = c.wbuf[:, 4096 + 1024 * (m % 2):4096 + 1024 * (m % 2 + 1)]
                wt(v, dma_t[m])
                if m >= 2:
                    wt(v, mm_t[(m - 2, 1)])
                raw = wf[:, 1024 * (m % 2):1024 * (m % 2 + 1)]
                c.sV.inc(v.tensor_scalar(out=raw, in0=raw, scalar1=c.gacc[:, 2:3],
                                         scalar2=None, op0=ALU.mult))
                v.wait_ge(c.sV.h, c.sV.n)
                c.sV.inc(v.tensor_scalar(out=raw, in0=raw, scalar1=ROUND_C, scalar2=ROUND_C,
                                         op0=ALU.add, op1=ALU.subtract))
                v.wait_ge(c.sV.h, c.sV.n)
                c.sV.inc(v.tensor_scalar(out=qsl, in0=raw, scalar1=1.0, scalar2=-1.0,
                                         op0=ALU.min, op1=ALU.max))

        @c.block.tensor
        def _(t, hs0=hs0, bank_free0=bank_free0, half=half):
            if xln_ch and half == halves[0]:
                wt(t, xln_ch[2 * half], xln_ch[2 * half + 1])
            elif half == halves[0]:
                wt(t, xln_t)
            for m in range(32):
                qsl = c.wbuf[:, 4096 + 1024 * (m % 2):4096 + 1024 * (m % 2 + 1)]
                wt(t, qt_t[m])
                for c2 in range(2):
                    i = 2 * m + c2
                    bank = c.pb[i % 4]
                    wt(t, bank_free0[i % 4] if i < 4 else ev_t[(m - 2, c2)])
                    for k in range(ND):
                        ins = t.matmul(bank.t[:, :], qsl[:, 128 * k:128 * (k + 1)],
                                       c.xln[:, T * k + hs0 + 512 * c2:T * k + hs0 + 512 * (c2 + 1)],
                                       start=(k == 0), stop=(k == ND - 1))
                    c.pe.inc(ins)

        @c.block.scalar
        def _(a, half=half, bias_t=bias_t):
            if half == halves[0]:
                wt(a, bias_t)
            for m in range(32):
                for c2 in range(2):
                    bank = c.pb[(2 * m + c2) % 4]
                    wt(a, mm_t[(m, c2)])
                    c.sA.inc(a.activation(
                        out=h[:, 1024 * m + 512 * c2:1024 * m + 512 * (c2 + 1)],
                        in_=bank.t[:, :], func=AF.Relu, bias=c.bias[:, m:m + 1]))
        for m in range(32):
            for c2 in range(2):
                bank = c.pb[(2 * m + c2) % 4]
                bank.mm_t = mm_t[(m, c2)]
                bank.free_t = ev_t[(m, c2)]
        h_t = c.sA.t
        wbuf_free_t = qt_t[31]
        wbuf_free_pe_t = mm_t[(31, 1)]
        if "out" not in phases:
            continue

        v0, pe0 = c.sV.n, c.pe.n
        dx60 = c.dx[6].n
        dma2_t = {(m, sub): (c.dx[6], dx60 + 32 * m + 16 * (sub + 1))
                  for m in range(8) for sub in range(2)}
        sched = []
        for m in range(8):
            sched += [("q", m, 0, j) for j in range(3)]
            sched += [("q", m, 1, j) for j in range(3)]
            if m >= 1:
                sched += [("a", m - 1, 0, 0), ("a", m - 1, 1, 0)]
        sched += [("a", 7, 0, 0), ("a", 7, 1, 0)]
        svt = {}
        for i, it in enumerate(sched):
            svt[it[:3]] = (c.sV, v0 + i + 1)
        q2_t = {(m, sub): svt[("q", m, sub)] for m in range(8) for sub in range(2)}
        add_t = {(m, c2): svt[("a", m, c2)] for m in range(8) for c2 in range(2)}
        mm2_h = {(m, c2): (c.pe, pe0 + 4 * m + 2 * c2 + 1)
                 for m in range(8) for c2 in range(2)}
        mm2_t = {(m, c2): (c.pe, pe0 + 4 * m + 2 * c2 + 2)
                 for m in range(8) for c2 in range(2)}
        bank_free0 = [c.pb[i].free_t for i in range(6)]

        @c.block.sync
        def _(s, wbuf_free_t=wbuf_free_t):
            for m in range(8):
                for sub in range(2):
                    if m + sub >= 1:
                        wt(s, q2_t[(m, 0)] if sub == 1 else q2_t[(m - 1, 1)])
                    else:
                        wt(s, wbuf_free_t)
                    c.dx[6].inc(s.dma_start(
                        out=wf[:, 0:2048],
                        in_=w2t[2048 * sub:2048 * (sub + 1), 128 * m:128 * (m + 1)]
                        .rearrange("(k p) f -> p k f", p=128)), 16)

        @c.block.vector
        def _(v, half=half, hs0=hs0, wbuf_free_pe_t=wbuf_free_pe_t, bias_t=bias_t):
            wt(v, r2_t, wbuf_free_pe_t)
            if half == halves[0]:
                wt(v, bias_t)
            for m in range(8):
                for sub in range(2):
                    wt(v, dma2_t[(m, sub)])
                    if m >= 1:
                        wt(v, mm2_h[(m - 1, 1)] if sub == 0 else mm2_t[(m - 1, 1)])
                    raw = wf[:, 0:2048]
                    qsub = c.wbuf[:, 4096 + 2048 * sub:4096 + 2048 * (sub + 1)]
                    c.sV.inc(v.tensor_scalar(out=raw, in0=raw, scalar1=c.gacc[:, 3:4],
                                             scalar2=None, op0=ALU.mult))
                    v.wait_ge(c.sV.h, c.sV.n)
                    c.sV.inc(v.tensor_scalar(out=raw, in0=raw, scalar1=ROUND_C,
                                             scalar2=ROUND_C, op0=ALU.add, op1=ALU.subtract))
                    v.wait_ge(c.sV.h, c.sV.n)
                    c.sV.inc(v.tensor_scalar(out=qsub, in0=raw, scalar1=1.0, scalar2=-1.0,
                                             op0=ALU.min, op1=ALU.max))
                if m >= 1:
                    for c2 in range(2):
                        bank = c.pb[4 + c2]
                        wt(v, mm2_t[(m - 1, c2)])
                        dst = c.xT[:, T * (m - 1) + hs0 + 512 * c2:T * (m - 1) + hs0 + 512 * (c2 + 1)]
                        c.sV.inc(v.scalar_tensor_tensor(out=dst, in0=bank.t[:, :],
                                                        scalar=c.bias[:, 32 + m - 1:33 + m - 1],
                                                        in1=dst, op0=ALU.add, op1=ALU.add))
            for c2 in range(2):
                bank = c.pb[4 + c2]
                wt(v, mm2_t[(7, c2)])
                dst = c.xT[:, T * 7 + hs0 + 512 * c2:T * 7 + hs0 + 512 * (c2 + 1)]
                c.sV.inc(v.scalar_tensor_tensor(out=dst, in0=bank.t[:, :],
                                                scalar=c.bias[:, 39:40],
                                                in1=dst, op0=ALU.add, op1=ALU.add))

        @c.block.tensor
        def _(t, hs0=hs0, bank_free0=bank_free0, h_t=h_t):
            wt(t, h_t)
            for m in range(8):
                for c2 in range(2):
                    i = 2 * m + c2
                    bank = c.pb[4 + c2]
                    if c2 == 0:
                        wt(t, q2_t[(m, 0)])
                    wt(t, bank_free0[4 + c2] if i < 2 else add_t[(m - 1, c2)])
                    for kk in range(16):
                        ins = t.matmul(bank.t[:, :],
                                       c.wbuf[:, 4096 + 128 * kk:4096 + 128 * (kk + 1)],
                                       h[:, 1024 * kk + 512 * c2:1024 * kk + 512 * (c2 + 1)],
                                       start=(kk == 0), stop=False)
                    c.pe.inc(ins)
                    if c2 == 0:
                        wt(t, q2_t[(m, 1)])
                    for kk in range(16, 32):
                        ins = t.matmul(bank.t[:, :],
                                       c.wbuf[:, 4096 + 128 * kk:4096 + 128 * (kk + 1)],
                                       h[:, 1024 * kk + 512 * c2:1024 * kk + 512 * (c2 + 1)],
                                       start=False, stop=(kk == 31))
                    c.pe.inc(ins)
        for m in range(8):
            for c2 in range(2):
                bank = c.pb[4 + c2]
                bank.mm_t = mm2_t[(m, c2)]
                bank.free_t = add_t[(m, c2)]
        wbuf_free_t = q2_t[(7, 1)]
        wbuf_free_pe_t = mm2_t[(7, 1)]
        if half == halves[-1]:
            final_add = {m: add_t[(m, 1)] for m in range(8)}
    return c.sV.t, wbuf_free_t, wbuf_free_pe_t, final_add


def build_l1ffn_test():
    nc = bass.Bass("TRN2", target_bir_lowering=False)
    x = nc.dram_tensor("x", [T, D], F32, kind="ExternalInput").ap()
    w1t = nc.dram_tensor("w1t", [D, 4096], F32, kind="ExternalInput").ap()
    w2t = nc.dram_tensor("w2t", [4096, D], F32, kind="ExternalInput").ap()
    b1c = nc.dram_tensor("b1c", [128, 32], F32, kind="ExternalInput").ap()
    b2c = nc.dram_tensor("b2c", [128, 8], F32, kind="ExternalInput").ap()
    y = nc.dram_tensor("y", [T, D], F32, kind="ExternalOutput").ap()
    with ExitStack() as stack:
        block = stack.enter_context(nc.Block())
        c = make_ctx(stack, nc, block)
        const_t = emit_consts(c)
        b1_t = emit_bias_cols(c, 0, b1c, 32)
        b2_t = emit_bias_cols(c, 32, b2c, 8)
        x_t, _ = emit_load_x_transposed(c, x, const_t)
        r1_t = emit_gamma(c, w1t, D * 4096, 0)
        r2_t = emit_gamma(c, w2t, D * 4096, 1, wbuf_free_t=r1_t)
        ln_t, _ = emit_ln(c, x_t)
        ffn_t, _, _, _ = emit_l1ffn(c, ln_t, w1t, w2t, r1_t, r2_t, b2_t,
                                    wbuf_free_t=r2_t, wbuf_free_pe_t=None)
        st = emit_store_x_tokenmajor(c, y, ffn_t)
        emit_wait_store(c, st)
    return nc


def kernel(**inputs):
    raise NotImplementedError("dev version")



def emit_moe_cw(c, xln_t, wrg_dma_t, xln_ch=None):
    pe0, a0, v0 = c.pe.n, c.sA.n, c.sV.n
    log_mm = {tt: (c.pe, pe0 + 2 * tt + 1) for tt in range(NT)}
    tr_mm = {tt: (c.pe, pe0 + 2 * tt + 2) for tt in range(NT)}
    exp_t = {}
    a = a0
    evac_t = {}
    for tt in range(NT):
        a += 2
        exp_t[tt] = (c.sA, a)
        if tt % 4 == 3:
            a += 1
            evac_t[tt // 4] = (c.sA, a)
    cw_t = {tt: (c.sV, v0 + 5 * (tt + 1)) for tt in range(NT)}

    lbank0 = [c.pb[4].free_t, c.pb[5].free_t]
    tbank0 = [c.pb[0].free_t, c.pb[1].free_t]

    @c.block.tensor
    def _(t):
        wt(t, wrg_dma_t)
        if xln_ch is None:
            wt(t, xln_t)
        for tt in range(NT):
            if xln_ch and tt % 4 == 0:
                wt(t, xln_ch[tt // 4])
            lb = c.pb[4 + tt % 2]
            if tt < 2:
                wt(t, lbank0[tt % 2])
            else:
                wt(t, exp_t[tt - 2])
            for k in range(ND):
                t.matmul(lb.t[0:128, 0:8], c.xln[:, T * k + 128 * tt:T * k + 128 * (tt + 1)],
                         c.moe_w[:, 8 * k:8 * (k + 1)], start=(k == 0), stop=False)
            for k in range(ND):
                ins = t.matmul(lb.t[0:128, 8:16], c.xln[:, T * k + 128 * tt:T * k + 128 * (tt + 1)],
                               c.moe_w[:, 64 + 8 * k:64 + 8 * (k + 1)],
                               start=False, stop=(k == ND - 1))
            c.pe.inc(ins)
            tb = c.pb[(tt // 4) % 2]
            wt(t, cw_t[tt])
            if tt < 8 and tt % 4 == 0:
                wt(t, tbank0[(tt // 4) % 2])
            if tt >= 8 and tt % 4 == 0:
                wt(t, evac_t[tt // 4 - 2])
            c.pe.inc(t.matmul(tb.t[0:8, 128 * (tt % 4):128 * (tt % 4 + 1)],
                              c.cwt[:, 16 + 8 * (tt % 2):16 + 8 * (tt % 2) + 8],
                              c.id_f32[:], start=True, stop=True))

    @c.block.scalar
    def _(a_):
        for tt in range(NT):
            lb = c.pb[4 + tt % 2]
            wt(a_, log_mm[tt])
            if tt >= 1:
                wt(a_, (c.sV, v0 + 5 * (tt - 1) + 2))
            c.sA.inc(a_.activation(out=c.cwt[:, 0:8], in_=lb.t[0:128, 0:8], func=AF.Exp,
                                   accum_out=c.cws[:, 0:1]))
            c.sA.inc(a_.activation(out=c.cwt[:, 8:16], in_=lb.t[0:128, 8:16], func=AF.Exp,
                                   scale=10.0, accum_out=c.cws[:, 1:2]))
            if tt % 4 == 3:
                g = tt // 4
                tb = c.pb[g % 2]
                wt(a_, tr_mm[tt])
                c.sA.inc(a_.copy(out=c.cw8[:, 512 * g:512 * (g + 1)], in_=tb.t[0:8, :]))

    @c.block.vector
    def _(v):
        for tt in range(NT):
            wt(v, exp_t[tt])
            if tt >= 2:
                wt(v, tr_mm[tt - 2])
            c.sV.inc(v.scalar_tensor_tensor(out=c.cwt[:, 16 + 8 * (tt % 2):16 + 8 * (tt % 2) + 8],
                                            in0=c.cwt[:, 0:8], scalar=1.0, in1=c.cwt[:, 8:16],
                                            op0=ALU.mult, op1=ALU.mult,
                                            accum_out=c.cws[:, 2:3]))
            c.sV.inc(v.scalar_tensor_tensor(out=c.cws[:, 3:4], in0=c.cws[:, 0:1],
                                            scalar=1e-9, in1=c.cws[:, 1:2],
                                            op0=ALU.mult, op1=ALU.mult))
            v.wait_ge(c.sV.h, c.sV.n)
            c.sV.inc(v.tensor_tensor(out=c.cws[:, 4:5], in0=c.cws[:, 2:3],
                                     in1=c.cws[:, 3:4], op=ALU.add))
            v.wait_ge(c.sV.h, c.sV.n)
            c.sV.inc(v.reciprocal(out=c.cws[:, 5:6], in_=c.cws[:, 4:5]))
            v.wait_ge(c.sV.h, c.sV.n)
            sl = c.cwt[:, 16 + 8 * (tt % 2):16 + 8 * (tt % 2) + 8]
            c.sV.inc(v.tensor_scalar(out=sl, in0=sl, scalar1=c.cws[:, 5:6],
                                     scalar2=None, op0=ALU.mult))

    for tt in range(NT):
        lb = c.pb[4 + tt % 2]
        lb.mm_t = log_mm[tt]
        lb.free_t = exp_t[tt]
    for g in range(4):
        tb = c.pb[g % 2]
        tb.mm_t = tr_mm[4 * g + 3]
        tb.free_t = evac_t[g]
    return evac_t[3]


def emit_l0ffn(c, xln_t, w_gate, w_up, w_down, cw8_t, arena_free_pe_t,
               wbuf_free_t, wbuf_free_pe_t, experts=range(8), xln_ch=None):
    wb = c.wbuf
    h = c.arenaA
    h_free_t = arena_free_pe_t
    cwrow_free_t = None
    cwB_free = [None, None]
    xt_add_t = {}
    g_slot_pe = [None, None]
    u_slot_pe = [None, None]
    d_slot_pe = None

    first = True
    seq = [(hf, ee) for hf in range(2) for ee in experts]

    def build_cwB(half_b, e_b, crow_wait):
        par_b = e_b % 2
        hs0b = 1024 * half_b

        @c.block.sync
        def _(s, e_b=e_b, hs0b=hs0b, crow_wait=crow_wait):
            wt(s, crow_wait, cw8_t)
            c.dx[7].inc(s.dma_start(out=c.cwrow[:, :],
                                    in_=c.cw8[e_b:e_b + 1, hs0b:hs0b + 1024]), 16)
        crow_t = c.dx[7].t

        bank = c.pb[5]
        pw = bank.free_t
        a0c = c.sA.n
        cwb_mm = []

        @c.block.tensor
        def _(t, crow_t=crow_t, pw=pw, cwB_free_par=cwB_free[par_b], a0c=a0c):
            wt(t, crow_t, pw, cwB_free_par)
            for cc in range(2):
                if cc == 1:
                    wt(t, (c.sA, a0c + 1))
                c.pe.inc(t.matmul(bank.t[:, :], c.ones1[:, :],
                                  c.cwrow[0:1, 512 * cc:512 * (cc + 1)],
                                  start=True, stop=True))
                cwb_mm.append(c.pe.t)
        bank.mm_t = cwb_mm[1]

        @c.block.scalar
        def _(a, par_b=par_b):
            for cc in range(2):
                wt(a, cwb_mm[cc])
                c.sA.inc(a.copy(out=c.cwB[:, 1024 * par_b + 512 * cc:1024 * par_b + 512 * (cc + 1)],
                                in_=bank.t[:, :]))
        bank.free_t = c.sA.t
        return c.sA.t, cwb_mm[1]

    cwB_pre, cwrow_free_t = build_cwB(seq[0][0], seq[0][1], None)
    for si, (half, e) in enumerate(seq):
            hs0 = 1024 * half
            par = e % 2
            cwB_t = cwB_pre
            v0, pe0, a0 = c.sV.n, c.pe.n, c.sA.n
            dwg0 = [c.dw[0].n, c.dw[1].n]
            dwu0 = [c.dw[2].n, c.dw[3].n]
            gdma_t = {}
            udma_t = {}
            for m in range(32):
                gdma_t[m] = (c.dw[m % 2], dwg0[m % 2] + 16 * (m // 2 + 1))
                udma_t[m] = (c.dw[2 + m % 2], dwu0[m % 2] + 16 * (m // 2 + 1))
            gmm = {(m, c2): (c.pe, pe0 + 4 * m + 2 * c2 + 1) for m in range(32) for c2 in range(2)}
            umm = {(m, c2): (c.pe, pe0 + 4 * m + 2 * c2 + 2) for m in range(32) for c2 in range(2)}
            silu_t = {(m, c2): (c.sA, a0 + 2 * m + c2 + 1) for m in range(32) for c2 in range(2)}
            hmul_t = {(m, c2): (c.sV, v0 + 2 * m + c2 + 1) for m in range(32) for c2 in range(2)}
            gslot_pe0 = list(g_slot_pe)
            uslot_pe0 = list(u_slot_pe)
            bank_free0 = [c.pb[i].free_t for i in range(4)]

            @c.block.gpsimd
            def _(g, e=e, gslot_pe0=gslot_pe0, uslot_pe0=uslot_pe0,
                  wbuf_free_t=wbuf_free_t, wbuf_free_pe_t=wbuf_free_pe_t, first=first):
                for m in range(32):
                    if m >= 2:
                        wt(g, gmm[(m - 2, 1)])
                    else:
                        wt(g, gslot_pe0[m % 2], wbuf_free_pe_t)
                        if first:
                            wt(g, wbuf_free_t)
                    c.dw[m % 2].inc(g.dma_start(
                        out=wb[:, 1024 * (m % 2):1024 * (m % 2 + 1)],
                        in_=w_gate[e, :, 128 * m:128 * (m + 1)]
                        .rearrange("(k p) f -> p k f", p=128)), 16)
                    if m >= 2:
                        wt(g, umm[(m - 2, 1)])
                    else:
                        wt(g, uslot_pe0[m % 2])
                    c.dw[2 + m % 2].inc(g.dma_start(
                        out=wb[:, 2048 + 1024 * (m % 2):2048 + 1024 * (m % 2 + 1)],
                        in_=w_up[e, :, 128 * m:128 * (m + 1)]
                        .rearrange("(k p) f -> p k f", p=128)), 16)

            @c.block.tensor
            def _(t, hs0=hs0, bank_free0=bank_free0, first=first, xln_t=xln_t, half=half):
                if first:
                    if xln_ch:
                        wt(t, xln_ch[2 * half], xln_ch[2 * half + 1])
                    else:
                        wt(t, xln_t)
                for m in range(32):
                    gsl = wb[:, 1024 * (m % 2):1024 * (m % 2 + 1)]
                    usl = wb[:, 2048 + 1024 * (m % 2):2048 + 1024 * (m % 2 + 1)]
                    wt(t, gdma_t[m], udma_t[m])
                    for c2 in range(2):
                        i = 4 * m + 2 * c2
                        gb = c.pb[(2 * m + c2) % 2]
                        ub = c.pb[2 + (2 * m + c2) % 2]
                        if (2 * m + c2) < 2:
                            wt(t, bank_free0[(2 * m + c2) % 2])
                        else:
                            wt(t, silu_t[((2 * m + c2 - 2) // 2, (2 * m + c2 - 2) % 2)])
                        for k in range(ND):
                            ins = t.matmul(gb.t[:, :], gsl[:, 128 * k:128 * (k + 1)],
                                           c.xln[:, T * k + hs0 + 512 * c2:T * k + hs0 + 512 * (c2 + 1)],
                                           start=(k == 0), stop=(k == ND - 1))
                        c.pe.inc(ins)
                        if (2 * m + c2) < 2:
                            wt(t, bank_free0[2 + (2 * m + c2) % 2])
                        else:
                            wt(t, hmul_t[((2 * m + c2 - 2) // 2, (2 * m + c2 - 2) % 2)])
                        for k in range(ND):
                            ins = t.matmul(ub.t[:, :], usl[:, 128 * k:128 * (k + 1)],
                                           c.xln[:, T * k + hs0 + 512 * c2:T * k + hs0 + 512 * (c2 + 1)],
                                           start=(k == 0), stop=(k == ND - 1))
                        c.pe.inc(ins)

            @c.block.scalar
            def _(a_, tmp_free_t=getattr(c, '_moe_tmp_free', None)):
                for m in range(32):
                    for c2 in range(2):
                        gb = c.pb[(2 * m + c2) % 2]
                        wt(a_, gmm[(m, c2)])
                        if 2 * m + c2 >= 2:
                            wt(a_, hmul_t[((2 * m + c2 - 2) // 2, (2 * m + c2 - 2) % 2)])
                        else:
                            wt(a_, tmp_free_t)
                        tdst = c.tmp0 if (2 * m + c2) % 2 == 0 else c.tmp1
                        c.sA.inc(a_.activation(out=tdst[:, :], in_=gb.t[:, :], func=AF.Silu))

            @c.block.vector
            def _(v, h_free_t=h_free_t):
                wt(v, h_free_t)
                for m in range(32):
                    for c2 in range(2):
                        ub = c.pb[2 + (2 * m + c2) % 2]
                        wt(v, silu_t[(m, c2)], umm[(m, c2)])
                        tsrc = c.tmp0 if (2 * m + c2) % 2 == 0 else c.tmp1
                        c.sV.inc(v.tensor_tensor(
                            out=h[:, 1024 * m + 512 * c2:1024 * m + 512 * (c2 + 1)],
                            in0=tsrc[:, :], in1=ub.t[:, :], op=ALU.mult))
            for m in range(32):
                for c2 in range(2):
                    gb = c.pb[(2 * m + c2) % 2]
                    ub = c.pb[2 + (2 * m + c2) % 2]
                    gb.mm_t = gmm[(m, c2)]
                    gb.free_t = silu_t[(m, c2)]
                    ub.mm_t = umm[(m, c2)]
                    ub.free_t = hmul_t[(m, c2)]
            h_t = hmul_t[(31, 1)]
            g_slot_pe = [gmm[(30, 1)], gmm[(31, 1)]]
            u_slot_pe = [umm[(30, 1)], umm[(31, 1)]]
            wbuf_free_t = None
            wbuf_free_pe_t = None

            if si + 1 < len(seq):
                cwB_pre, cwrow_free_t = build_cwB(seq[si + 1][0], seq[si + 1][1],
                                                  cwrow_free_t)

            v0b, pe0b = c.sV.n, c.pe.n
            dsub = [c.dw[4], c.dw[10], c.dw[11], c.dw[12]]
            dsub0 = [x.n for x in dsub]
            ddma_t = {(m, j): (dsub[j], dsub0[j] + 16 * (m + 1))
                      for m in range(8) for j in range(4)}
            qtr = {(m, c2, j): (c.pe, pe0b + 8 * m + 4 * c2 + j + 1)
                   for m in range(8) for c2 in range(2) for j in range(4)}
            dmm = {(m, c2): qtr[(m, c2, 3)] for m in range(8) for c2 in range(2)}
            dmul_t = {(m, c2): (c.sV, v0b + 4 * m + 2 * c2 + 1) for m in range(8) for c2 in range(2)}
            dadd_t = {(m, c2): (c.sV, v0b + 4 * m + 2 * c2 + 2) for m in range(8) for c2 in range(2)}
            dslot0 = d_slot_pe
            bank45_free0 = [c.pb[4].free_t, c.pb[5].free_t]

            @c.block.gpsimd
            def _(g, e=e, dslot0=dslot0):
                for m in range(8):
                    for j in range(4):
                        if m >= 1:
                            wt(g, qtr[(m - 1, 1, j)])
                        elif j == 0:
                            wt(g, dslot0)
                        dsub[j].inc(g.dma_start(
                            out=wb[:, 4096 + 1024 * j:4096 + 1024 * (j + 1)],
                            in_=w_down[e, 1024 * j:1024 * (j + 1), 128 * m:128 * (m + 1)]
                            .rearrange("(k p) f -> p k f", p=128)), 16)

            @c.block.tensor
            def _(t, hs0=hs0, h_t=h_t, bank45_free0=bank45_free0):
                wt(t, h_t)
                for m in range(8):
                    for c2 in range(2):
                        i = 2 * m + c2
                        db = c.pb[4 + c2]
                        wt(t, bank45_free0[c2] if i < 2 else dmul_t[(m - 1, c2)])
                        for j in range(4):
                            if c2 == 0:
                                wt(t, ddma_t[(m, j)])
                            for kk in range(8 * j, 8 * (j + 1)):
                                ins = t.matmul(
                                    db.t[:, :],
                                    wb[:, 4096 + 128 * kk:4096 + 128 * (kk + 1)],
                                    h[:, 1024 * kk + 512 * c2:1024 * kk + 512 * (c2 + 1)],
                                    start=(kk == 0), stop=(kk == 31))
                            c.pe.inc(ins)

            @c.block.vector
            def _(v, hs0=hs0, half=half, cwB_t=cwB_t, par=par):
                wt(v, cwB_t)
                for m in range(8):
                    for c2 in range(2):
                        db = c.pb[4 + c2]
                        wt(v, dmm[(m, c2)])
                        tdst = c.tmp0 if c2 == 0 else c.tmp1
                        c.sV.inc(v.tensor_tensor(out=tdst[:, :], in0=db.t[:, :],
                                                 in1=c.cwB[:, 1024 * par + 512 * c2:1024 * par + 512 * (c2 + 1)],
                                                 op=ALU.mult))
                        v.wait_ge(c.sV.h, c.sV.n)
                        prev = xt_add_t.get((m, half, c2))
                        wt(v, prev)
                        dst = c.xT[:, T * m + hs0 + 512 * c2:T * m + hs0 + 512 * (c2 + 1)]
                        c.sV.inc(v.tensor_tensor(out=dst, in0=dst, in1=tdst[:, :], op=ALU.add))
                        xt_add_t[(m, half, c2)] = (c.sV, c.sV.n)
            for m in range(8):
                for c2 in range(2):
                    db = c.pb[4 + c2]
                    db.mm_t = dmm[(m, c2)]
                    db.free_t = dmul_t[(m, c2)]
            cwB_free[par] = dmul_t[(7, 1)]
            h_free_t = dmm[(7, 1)]
            d_slot_pe = dmm[(7, 1)]
            c._moe_tmp_free = dadd_t[(7, 1)]
            first = False
    last_add = max(v[1] for v in xt_add_t.values())
    moe_ch = [xt_add_t[(7, ch // 2, ch % 2)] for ch in range(4)]
    return (c.sV, last_add), d_slot_pe, moe_ch


def emit_moe_wrg(c, wrg_hbm):
    @c.block.gpsimd
    def _(g):
        c.dw[5].inc(g.dma_start(out=c.moe_w[:, :], in_=wrg_hbm), 16)
    return c.dw[5].t


def build_l0ffn_test(experts=tuple(range(8))):
    nc = bass.Bass("TRN2", target_bir_lowering=False)
    x = nc.dram_tensor("x", [T, D], F32, kind="ExternalInput").ap()
    wrg = nc.dram_tensor("wrg", [128, 128], F32, kind="ExternalInput").ap()
    w_gate = nc.dram_tensor("w_gate", [8, D, 4096], F32, kind="ExternalInput").ap()
    w_up = nc.dram_tensor("w_up", [8, D, 4096], F32, kind="ExternalInput").ap()
    w_down = nc.dram_tensor("w_down", [8, 4096, D], F32, kind="ExternalInput").ap()
    y = nc.dram_tensor("y", [T, D], F32, kind="ExternalOutput").ap()
    with ExitStack() as stack:
        block = stack.enter_context(nc.Block())
        c = make_ctx(stack, nc, block)
        const_t = emit_consts(c)
        wrg_t = emit_moe_wrg(c, wrg)
        x_t, _ = emit_load_x_transposed(c, x, const_t)
        ln_t, _ = emit_ln(c, x_t)
        cw8_t = emit_moe_cw(c, ln_t, wrg_t)
        ffn_t, _, _ = emit_l0ffn(c, ln_t, w_gate, w_up, w_down, cw8_t,
                                 arena_free_pe_t=None, wbuf_free_t=None,
                                 wbuf_free_pe_t=None, experts=experts)
        st = emit_store_x_tokenmajor(c, y, ffn_t)
        emit_wait_store(c, st)
    return nc



def emit_l1attn(c, xln_t, wqkvT, woutT, xt_prev_t, wbuf_free_pe_t=None,
                scalar_hook=None, xln_ch=None):
    A = c.arenaA
    ctxT = A[:, 0:16384]
    qT = A[:, 16384:18432]
    kT = A[:, 18432:20480]
    vtok = A[:, 20480:22528]
    attnb = [[A[:, 22528:24576], A[:, 28672:30720]],
             [A[:, 26624:28672], A[:, 30720:32768]]]
    stg = [A[:, 24576:25600], A[:, 25600:26624]]
    rs_col = [c.cws[:, 6:7], c.cws[:, 4:5]]
    rs2_col = [c.cws[:, 2:3], c.cws[:, 0:1]]
    rc_col = [c.cws[:, 7:8], c.cws[:, 5:6]]

    last_exp = [c.pb[0].free_t, c.pb[2].free_t]
    pb4_free = c.pb[4].free_t
    pt_free = [c.pt[0].free_t, c.pt[1].free_t]
    stg_free = [None, None]
    attn_war = [[None, None], [None, None]]
    rs_war = [None, None]
    wslot_pe = [None, None]
    prev_core_end = None
    ctx_last = None
    gidx = 0

    for hp in range(8):
        par = hp % 2
        wq = c.wbuf[:, 3072 * par:3072 * par + 1024]
        wk = c.wbuf[:, 3072 * par + 1024:3072 * par + 2048]
        wv = c.wbuf[:, 3072 * par + 2048:3072 * par + 3072]
        dq, dk, dv = c.dw[0 + par], c.dw[2 + par], c.dw[6 + par]
        wdma = [(dq, dq.n + 16), (dk, dk.n + 16), (dv, dv.n + 16)]
        w_reuse = wslot_pe[par] if wslot_pe[par] is not None else wbuf_free_pe_t

        pe_n, a_n, v_n = c.pe.n, c.sA.n, c.sV.n
        qkv_mm, qkv_mm_w = {}, {}
        qkv_ev, qkv_ev_w = {}, {}
        bank_free = {0: last_exp[0], 1: last_exp[0], 2: last_exp[1], 3: last_exp[1]}
        for gi in range(24):
            b = gi % 4
            qkv_mm_w[gi] = [bank_free[b]] + ([wdma[0], wdma[1], wdma[2]] if gi == 0 else [])
            if hp == 0 and xln_ch:
                if gi < 8:
                    qkv_mm_w[gi].append(xln_ch[gi % 4])
                elif (gi - 8) % 4 == 0:
                    qkv_mm_w[gi].append(xln_ch[(gi - 8) // 4])
            pe_n += 1
            qkv_mm[gi] = (c.pe, pe_n)
            qkv_ev_w[gi] = [qkv_mm[gi]] + ([prev_core_end] if gi == 0 else [])
            a_n += 1
            qkv_ev[gi] = (c.sA, a_n)
            bank_free[b] = qkv_ev[gi]
        last_exp = [qkv_ev[20], qkv_ev[23]]

        def passes(qt):
            kext = 128 * (qt + 1)
            if kext <= 1024:
                return [(0, kext)]
            return [(0, 1024), (1024, kext)]
        sc_mm, sc_w = {}, {}
        exp_t, exp_w = {}, {}
        nrm_t = {}
        tr_mm, tr_w = {}, {}
        trev_t, trev_w = {}, {}
        avg_mm, avg_w = {}, {}
        avev_t, avev_w = {}, {}
        ktg = {}

        def plan_ktwork(qt):
            nonlocal pe_n, a_n, v_n, gidx, pb4_free
            units = []
            for hloc in range(2):
                kts = list(range(qt + 1))
                groups = [kts[i:i + 8] for i in range(0, len(kts), 8)]
                ktg[(hloc, qt)] = []
                for gl in groups:
                    slot = gidx % 2
                    gidx += 1
                    ktg[(hloc, qt)].append((gl, slot))
                    units.append((hloc, gl, slot))
            n_u = len(units)
            tr_tick, av_tick = {}, {}
            for i2 in range(n_u):
                pe_n += 1
                tr_tick[i2] = (c.pe, pe_n)
                if i2 >= 2:
                    pe_n += 1
                    av_tick[i2 - 2] = (c.pe, pe_n)
            for i2 in (n_u - 2, n_u - 1):
                if i2 >= 0 and i2 not in av_tick:
                    pe_n += 1
                    av_tick[i2] = (c.pe, pe_n)
            ev_tick = {}
            for i2 in range(n_u):
                v_n += 1
                ev_tick[i2] = (c.sV, v_n)
            for i2, (hloc, gl, slot) in enumerate(units):
                tr_mm[(hloc, qt, gl[0])] = tr_tick[i2]
                trev_t[(hloc, qt, gl[0])] = ev_tick[i2]
                avg_mm[(hloc, qt, gl[0])] = av_tick[i2]
                tr_w[(hloc, qt, gl[0])] = [nrm_t[(hloc, qt)] if gl[0] == 0 else None,
                                           (ev_tick[i2 - 2] if i2 >= 2 else pt_free[slot])]
                trev_w[(hloc, qt, gl[0])] = [tr_tick[i2],
                                             (av_tick[i2 - 2] if i2 >= 2 else stg_free[slot])]
                avg_w[(hloc, qt, gl[0])] = [ev_tick[i2]] + \
                    ([pb4_free] if gl[0] == 0 else [])
                if gl[0] + 8 > qt:
                    avev_w[(hloc, qt)] = [av_tick[i2]]
                    a_n += 1
                    avev_t[(hloc, qt)] = (c.sA, a_n)
                    pb4_free = avev_t[(hloc, qt)]
            for i2 in range(max(0, n_u - 2), n_u):
                _, _, slot = units[i2]
                pt_free[slot] = ev_tick[i2]
                stg_free[slot] = av_tick[i2]
            if n_u == 1:
                pt_free[units[0][2]] = ev_tick[0]
                stg_free[units[0][2]] = av_tick[0]
            for hloc in range(2):
                attn_war[hloc][qt % 2] = tr_mm[(hloc, qt, ktg[(hloc, qt)][-1][0][0])]

        qt_order = list(reversed(range(NT)))
        first_qt = qt_order[0]
        for qi, qt in enumerate(qt_order):
            for pi, (k0, k1) in enumerate(passes(qt)):
                for hloc in range(2):
                    sc_w[(hloc, qt, pi)] = [last_exp[hloc]] + \
                        ([qkv_ev[7], qkv_ev[23]]
                         if (qt == first_qt and pi == 0 and hloc == 0) else [])
                    pe_n += 1
                    sc_mm[(hloc, qt, pi)] = (c.pe, pe_n)
                for hloc in range(2):
                    exp_w[(hloc, qt, pi)] = [sc_mm[(hloc, qt, pi)], rs_war[hloc],
                                             attn_war[hloc][qt % 2] if pi == 0 else None]
                    a_n += 1
                    exp_t[(hloc, qt, pi)] = (c.sA, a_n)
                    last_exp[hloc] = exp_t[(hloc, qt, pi)]
            if qi > 0:
                plan_ktwork(qt_order[qi - 1])
            npass = len(passes(qt))
            for hloc in range(2):
                v_n += (1 if npass == 2 else 0) + 2
                nrm_t[(hloc, qt)] = (c.sV, v_n)
                rs_war[hloc] = (c.sV, v_n)
        plan_ktwork(qt_order[-1])
        prev_core_end = avg_mm[(1, 0, ktg[(1, 0)][-1][0][0])]
        wslot_pe[par] = qkv_mm[23]
        ctx_last = avev_t[(1, 0)]

        @c.block.gpsimd
        def _(g, hp=hp, wq=wq, wk=wk, wv=wv, w_reuse=w_reuse, dq=dq, dk=dk, dv=dv):
            wt(g, w_reuse)
            dq.inc(g.dma_start(out=wq, in_=wqkvT[:, 128 * hp:128 * (hp + 1)]
                               .rearrange("(k p) f -> p k f", p=128)), 16)
            dk.inc(g.dma_start(out=wk, in_=wqkvT[:, 1024 + 128 * hp:1024 + 128 * (hp + 1)]
                               .rearrange("(k p) f -> p k f", p=128)), 16)
            dv.inc(g.dma_start(out=wv, in_=wqkvT[:, 2048 + 128 * hp:2048 + 128 * (hp + 1)]
                               .rearrange("(k p) f -> p k f", p=128)), 16)

        @c.block.tensor
        def _(t, hp=hp, wq=wq, wk=wk, wv=wv, qkv_mm_w=qkv_mm_w, sc_w=sc_w,
              tr_w=tr_w, avg_w=avg_w, ktg=ktg, xln_t=xln_t):
            if hp == 0 and xln_ch is None:
                wt(t, xln_t)
            gi = 0
            for which, wsl in ((0, wq), (1, wk)):
                for ch in range(4):
                    bank = c.pb[ch]
                    wt(t, *qkv_mm_w[gi])
                    for k in range(ND):
                        ins = t.matmul(bank.t[:, :], wsl[:, 128 * k:128 * (k + 1)],
                                       c.xln[:, T * k + 512 * ch:T * k + 512 * (ch + 1)],
                                       start=(k == 0), stop=(k == ND - 1))
                    c.pe.inc(ins)
                    gi += 1
            for ti in range(NT):
                bank = c.pb[ti % 4]
                wt(t, *qkv_mm_w[gi])
                for k in range(ND):
                    ins = t.matmul(bank.t[:, 0:128],
                                   c.xln[:, T * k + 128 * ti:T * k + 128 * (ti + 1)],
                                   wv[:, 128 * k:128 * (k + 1)],
                                   start=(k == 0), stop=(k == ND - 1))
                c.pe.inc(ins)
                gi += 1

            def emit_ktwork(qt):
                kext = 128 * (qt + 1)
                units = [(hloc, gl, slot) for hloc in range(2)
                         for gl, slot in ktg[(hloc, qt)]]

                def emit_tr(u):
                    hloc, gl, slot = u
                    wt(t, *tr_w[(hloc, qt, gl[0])])
                    for j, kt in enumerate(gl):
                        ins = t.transpose(c.pt[slot].t[:, 128 * j:128 * (j + 1)],
                                          attnb[hloc][qt % 2][:, 128 * kt:128 * (kt + 1)],
                                          c.id_bf[:])
                    c.pe.inc(ins)

                def emit_av(u):
                    hloc, gl, slot = u
                    h0 = 64 * hloc
                    wt(t, *avg_w[(hloc, qt, gl[0])])
                    for j, kt in enumerate(gl):
                        ins = t.matmul(c.pb[4].t[h0:h0 + 64, 0:128],
                                       vtok[:, 128 * kt + h0:128 * kt + h0 + 64],
                                       stg[slot][:, 128 * j:128 * (j + 1)],
                                       start=(kt == 0), stop=(kt == qt),
                                       tile_position=(0, h0), skip_group_check=True)
                    c.pe.inc(ins)

                for gi2, u in enumerate(units):
                    emit_tr(u)
                    if gi2 >= 2:
                        emit_av(units[gi2 - 2])
                if len(units) >= 2:
                    emit_av(units[-2])
                emit_av(units[-1])

            for qi, qt in enumerate(reversed(range(NT))):
                kext = 128 * (qt + 1)
                for pi, (k0, k1) in enumerate(passes(qt)):
                    for hloc in range(2):
                        h0 = 64 * hloc
                        wt(t, *sc_w[(hloc, qt, pi)])
                        nch = (k1 - k0 + 511) // 512
                        for ci in range(nch):
                            cs = k0 + 512 * ci
                            n = min(512, k1 - cs)
                            has_diag = cs <= kext - 128 < cs + n
                            ins = t.matmul(c.pbig[:, 1024 * hloc + 512 * ci:1024 * hloc + 512 * ci + n],
                                           qT[h0:h0 + 64, 128 * qt:128 * (qt + 1)],
                                           kT[h0:h0 + 64, cs:cs + n],
                                           start=True, stop=not has_diag,
                                           tile_position=(h0, 0), skip_group_check=True)
                            if has_diag:
                                ins = t.matmul(
                                    c.pbig[:, 1024 * hloc + 512 * ci + (kext - 128 - cs):
                                           1024 * hloc + 512 * ci + (kext - 128 - cs) + 128],
                                    c.id_bf[:], c.mask_bf[:],
                                    start=False, stop=True, skip_group_check=True)
                        c.pe.inc(ins)
                if qi > 0:
                    emit_ktwork(qt + 1)
            emit_ktwork(0)

        @c.block.scalar
        def _(a, hp=hp, qkv_ev_w=qkv_ev_w, exp_w=exp_w, avev_w=avev_w, ktg=ktg):
            gi = 0
            for which in range(2):
                for ch in range(4):
                    bank = c.pb[ch]
                    wt(a, *qkv_ev_w[gi])
                    dst = qT if which == 0 else kT
                    if which == 0:
                        c.sA.inc(a.mul(out=dst[:, 512 * ch:512 * (ch + 1)],
                                       in_=bank.t[:, :], mul=0.125))
                    else:
                        c.sA.inc(a.copy(out=dst[:, 512 * ch:512 * (ch + 1)],
                                        in_=bank.t[:, :]))
                    gi += 1
            for ti in range(NT):
                bank = c.pb[ti % 4]
                wt(a, *qkv_ev_w[gi])
                c.sA.inc(a.copy(out=vtok[:, 128 * ti:128 * (ti + 1)],
                                in_=bank.t[:, 0:128]))
                gi += 1
            def emit_avev(qt):
                for hloc in range(2):
                    h0 = 64 * hloc
                    wt(a, *avev_w[(hloc, qt)])
                    c.sA.inc(a.copy(
                        out=A[h0:h0 + 64, 2048 * hp + 128 * qt:2048 * hp + 128 * (qt + 1)],
                        in_=c.pb[4].t[h0:h0 + 64, 0:128]))

            for qi, qt in enumerate(reversed(range(NT))):
                for pi, (k0, k1) in enumerate(passes(qt)):
                    for hloc in range(2):
                        wt(a, *exp_w[(hloc, qt, pi)])
                        c.sA.inc(a.activation(
                            out=attnb[hloc][qt % 2][:, k0:k1],
                            in_=c.pbig[:, 1024 * hloc:1024 * hloc + (k1 - k0)],
                            func=AF.Exp,
                            accum_out=(rs_col[hloc] if pi == 0 else rs2_col[hloc])))
                if qi > 0:
                    emit_avev(qt + 1)
                if scalar_hook is not None:
                    scalar_hook(a, hp, 15 - qt)
            emit_avev(0)

        @c.block.vector
        def _(v, hp=hp, exp_t=exp_t, nrm_t=nrm_t, trev_w=trev_w, ktg=ktg):
            def emit_ktev(qt):
                for hloc in range(2):
                    for gl, slot in ktg[(hloc, qt)]:
                        wt(v, *trev_w[(hloc, qt, gl[0])])
                        c.sV.inc(v.tensor_copy(out=stg[slot][:, 0:128 * len(gl)],
                                               in_=c.pt[slot].t[:, 0:128 * len(gl)]))

            for qi, qt in enumerate(reversed(range(NT))):
                kext = 128 * (qt + 1)
                npass = len(passes(qt))
                if qi > 0:
                    emit_ktev(qt + 1)
                for hloc in range(2):
                    wt(v, exp_t[(hloc, qt, npass - 1)])
                    if npass == 2:
                        c.sV.inc(v.tensor_tensor(out=rs_col[hloc], in0=rs_col[hloc],
                                                 in1=rs2_col[hloc], op=ALU.add))
                        v.wait_ge(c.sV.h, c.sV.n)
                    c.sV.inc(v.reciprocal(out=rc_col[hloc], in_=rs_col[hloc]))
                    v.wait_ge(c.sV.h, c.sV.n)
                    c.sV.inc(v.tensor_scalar(out=attnb[hloc][qt % 2][:, 0:kext],
                                             in0=attnb[hloc][qt % 2][:, 0:kext],
                                             scalar1=rc_col[hloc], scalar2=None,
                                             op0=ALU.mult))
            emit_ktev(0)

    for i in range(4):
        c.pb[i].free_t = last_exp[i // 2]
    c.pb[4].free_t = pb4_free
    c.pt[0].free_t = pt_free[0]
    c.pt[1].free_t = pt_free[1]

    v0 = c.sV.n
    pe0 = c.pe.n
    wo_sem = [c.dw[8], c.dw[9]]
    wo_n0 = [c.dw[8].n, c.dw[9].n]
    wodma = {j: (wo_sem[j % 2], wo_n0[j % 2] + 16 * (j // 2 + 1)) for j in range(8)}
    womm = {}
    pe_n = pe0
    wo_w = {}
    bank_free = [c.pb[i].free_t for i in range(4)]
    woev_t = {(j, ch): (c.sV, v0 + 4 * j + ch + 1) for j in range(8) for ch in range(4)}
    for j in range(8):
        for ch in range(4):
            u = 4 * j + ch
            wo_w[(j, ch)] = [bank_free[ch] if u < 4 else woev_t[(j - 1, ch)]]
            pe_n += 1
            womm[(j, ch)] = (c.pe, pe_n)

    @c.block.gpsimd
    def _(g, wsl0=wslot_pe[0]):
        for j in range(8):
            if j >= 2:
                wt(g, womm[(j - 2, 3)])
            else:
                wt(g, wsl0)
            wo_sem[j % 2].inc(g.dma_start(
                out=c.wbuf[:, 1024 * (j % 2):1024 * (j % 2 + 1)],
                in_=woutT[:, 128 * j:128 * (j + 1)]
                .rearrange("(k p) f -> p k f", p=128)), 16)

    @c.block.tensor
    def _(t, ctx_last=ctx_last):
        wt(t, ctx_last)
        for j in range(8):
            wsl = c.wbuf[:, 1024 * (j % 2):1024 * (j % 2 + 1)]
            wt(t, wodma[j])
            for ch in range(4):
                wt(t, *wo_w[(j, ch)])
                for e in range(8):
                    ins = t.matmul(c.pb[ch].t[:, :], wsl[:, 128 * e:128 * (e + 1)],
                                   ctxT[:, 2048 * e + 512 * ch:2048 * e + 512 * (ch + 1)],
                                   start=(e == 0), stop=(e == 7))
                c.pe.inc(ins)

    @c.block.vector
    def _(v, xt_prev_t=xt_prev_t):
        wt(v, xt_prev_t)
        for j in range(8):
            for ch in range(4):
                wt(v, womm[(j, ch)])
                dst = c.xT[:, T * j + 512 * ch:T * j + 512 * (ch + 1)]
                c.sV.inc(v.tensor_tensor(out=dst, in0=dst, in1=c.pb[ch].t[:, :],
                                         op=ALU.add))
    for j in range(8):
        for ch in range(4):
            c.pb[ch].mm_t = womm[(j, ch)]
            c.pb[ch].free_t = woev_t[(j, ch)]
    return c.sV.t, womm[(7, 3)], [woev_t[(7, ch)] for ch in range(4)]


def build_l1attn_test():
    nc = bass.Bass("TRN2", target_bir_lowering=False)
    x = nc.dram_tensor("x", [T, D], F32, kind="ExternalInput").ap()
    wqkvT = nc.dram_tensor("wqkvT", [D, 3072], F32, kind="ExternalInput").ap()
    woutT = nc.dram_tensor("woutT", [D, D], F32, kind="ExternalInput").ap()
    y = nc.dram_tensor("y", [T, D], F32, kind="ExternalOutput").ap()
    with ExitStack() as stack:
        block = stack.enter_context(nc.Block())
        c = make_ctx(stack, nc, block)
        const_t = emit_consts(c)
        x_t, _ = emit_load_x_transposed(c, x, const_t)
        ln_t, _ = emit_ln(c, x_t)
        at_t, _, _ = emit_l1attn(c, ln_t, wqkvT, woutT, xt_prev_t=None)
        st = emit_store_x_tokenmajor(c, y, at_t)
        emit_wait_store(c, st)
    return nc



def emit_l0attn_consts(c, ln_t, bkv_hbm):
    af = c.arenaAf
    stage = af[:, 10240:12288]
    A = c.arenaA
    cosB = A[:, 16384:18432]
    sinB = A[:, 18432:20480]

    @c.block.sync
    def _(s):
        c.dx[7].inc(s.dma_start(out=c.cw8[0:1, 0:2048], in_=bkv_hbm), 16)
    bkv_t = c.dx[7].t

    @c.block.gpsimd
    def _(g):
        wt(g, ln_t)
        c.sP.inc(g.iota(stage, pattern=[[1, 2048]], base=0, channel_multiplier=0,
                        allow_small_or_imprecise_dtypes=True))
        c.sP.inc(g.iota(c.tmp0[:, 0:16], pattern=[[128, 16]], base=0,
                        channel_multiplier=1, allow_small_or_imprecise_dtypes=True))
    iota_t = c.sP.t

    @c.block.scalar
    def _(a):
        wt(a, iota_t)
        c.sA.inc(a.activation(out=cosB, in_=stage, func=AF.Sin,
                              scale=-PI / 2 / T, bias=c.bias[:, 56:57]))
        c.sA.inc(a.activation(out=sinB, in_=stage, func=AF.Sin, scale=PI / 2 / T))
        c.sA.inc(a.activation(out=c.poscol[:, 0:16], in_=c.tmp0[:, 0:16], func=AF.Sin,
                              scale=-PI / 2 / T, bias=c.bias[:, 56:57]))
        c.sA.inc(a.activation(out=c.poscol[:, 16:32], in_=c.tmp0[:, 0:16], func=AF.Sin,
                              scale=PI / 2 / T))
    return c.sA.t, c.sA.t, bkv_t


def emit_l0attn(c, xln_t, wqkv, wout, b_dma_t, pos_t, bkv_t, xt_prev_t,
                xln_ch=None):
    A = c.arenaA
    ctxT = A[:, 0:16384]
    cosB = A[:, 16384:18432]
    sinB = A[:, 18432:20480]
    qcT = A[:, 20480:22528]
    qsT = A[:, 22528:24576]
    kc = A[:, 24576:26624]
    ks = A[:, 26624:28672]
    vtok = A[:, 28672:30720]
    kvh = A[:, 30720:30976]
    stg = [A[:, 30976:31104], A[:, 31104:31232]]
    bkv = c.cw8[0:1, 0:2048]

    bank_free = [c.pb[i].free_t for i in range(4)]
    pb4_free = c.pb[4].free_t
    pb5_free = c.pb[5].free_t
    pt_free = [c.pt[0].free_t, c.pt[1].free_t]
    tmp_free = [None, None]
    wslot_pe = [None, None]
    ctx_all = []

    for hp in range(8):
        par = hp % 2
        wq = c.wbuf[:, 3072 * par:3072 * par + 1024]
        wk = c.wbuf[:, 3072 * par + 1024:3072 * par + 2048]
        wv = c.wbuf[:, 3072 * par + 2048:3072 * par + 3072]
        dq, dkv = c.dw[0 + par], c.dw[2 + par]
        wdma = [(dq, dq.n + 16), (dkv, dkv.n + 32)]
        w_reuse = wslot_pe[par] if wslot_pe[par] is not None else xt_prev_t

        pe_n, a_n, v_n = c.pe.n, c.sA.n, c.sV.n
        q_mm, q_w = {}, {}
        for ch in range(4):
            q_w[ch] = [bank_free[ch]]
            if ch == 0:
                q_w[ch].append(wdma[0])
            if hp == 0:
                q_w[ch].append(xln_ch[ch] if xln_ch else xln_t)
            pe_n += 1
            q_mm[ch] = (c.pe, pe_n)
        qev_t, qmul_t = {}, {}
        for ch in range(4):
            a_n += 1
            qev_t[ch] = (c.sA, a_n)
            v_n += 2
            qmul_t[ch] = (c.sV, v_n)
            bank_free[ch] = qev_t[ch]
        qev_w = {}
        for ch in range(4):
            qev_w[ch] = [q_mm[ch], (qmul_t[ch - 2] if ch >= 2 else tmp_free[ch % 2]),
                         b_dma_t if (hp == 0 and ch == 0) else None,
                         pos_t if (hp == 0 and ch < 2) else None]
        kv_mm, kv_w = {}, {}
        kvev_a, kvev_v = {}, {}
        for ti in range(NT):
            b = ti % 4
            kv_w[ti] = [bank_free[b]] + ([wdma[1], bkv_t] if ti == 0 else []) + \
                ([xln_ch[ti // 4]] if (hp == 0 and xln_ch and ti % 4 == 0) else [])
            pe_n += 1
            kv_mm[ti] = (c.pe, pe_n)
            a_n += 2
            kvev_a[ti] = (c.sA, a_n)
            v_n += 2
            kvev_v[ti] = (c.sV, v_n)
            bank_free[b] = kvev_a[ti]
        kvev_aw = {}
        for ti in range(NT):
            kvev_aw[ti] = [kv_mm[ti], (kvev_v[ti - 2] if ti >= 2 else qmul_t[2 + ti % 2])]
        kvacc_mm, kvacc_w = {}, {}
        for ti in range(NT):
            kvacc_w[ti] = [kvev_v[ti], kvev_a[ti]] + \
                ([pb4_free, pb5_free] if ti == 0 else [])
            pe_n += 1
            kvacc_mm[ti] = (c.pe, pe_n)
        a_n += 8
        kvh_t = (c.sA, a_n)
        ctx_mm, ctx_w = {}, {}
        recip_t = {}
        mul_t, mul_w = {}, {}
        tr_mm, tr_w = {}, {}
        trev_t = {}
        for r in range(8):
            t0 = 2 * r
            ctx_w[r] = [kvh_t if r == 0 else None] +                 [bank_free[b] for b in range(4)] if r < 2 else                 [mul_t[(2 * (r - 2) + (b // 2), b % 2)] for b in range(4)]
            if r == 0:
                ctx_w[r] = [kvh_t] + [bank_free[b] for b in range(4)]
            elif r == 1:
                ctx_w[r] = [bank_free[b] for b in range(4)] if False else                     [mul_t.get((2 * (r - 1) + 0, 0))] and [None]
            pe_n += 1
            ctx_mm[r] = (c.pe, pe_n)
            v_n += 2
            recip_t[r] = (c.sV, v_n)
            for u in range(4):
                ti, hloc = t0 + u // 2, u % 2
                mul_w[(ti, hloc)] = [recip_t[r],
                                     (c.pe, tr_mm[ti - 2][1]) if ti >= 2 else None]
                a_n += 1
                mul_t[(ti, hloc)] = (c.sA, a_n)
            for ti in (t0, t0 + 1):
                tr_w[ti] = [mul_t[(ti, 0)], mul_t[(ti, 1)],
                            (pt_free[ti % 2] if ti < 2 else trev_t[ti - 2])]
                pe_n += 1
                tr_mm[ti] = (c.pe, pe_n)
                a_n += 1
                trev_t[ti] = (c.sA, a_n)
                pt_free[ti % 2] = trev_t[ti]
        for r in range(8):
            if r == 0:
                ctx_w[r] = [kvh_t] + [bank_free[b] for b in range(4)]
            elif r == 1:
                ctx_w[r] = []
            else:
                ctx_w[r] = [mul_t[(2 * (r - 2) + 1, 1)]]
        for b in range(4):
            bank_free[b] = mul_t[(2 * 7 + b // 2, b % 2)]
        tmp_free = [kvev_v[NT - 2], kvev_v[NT - 1]]
        wslot_pe[par] = kvacc_mm[NT - 1]
        wslot_pe[par] = kv_mm[NT - 1]
        pb4_free = kvh_t
        pb5_free = kvh_t
        ctx_all.append(trev_t[NT - 1])

        @c.block.gpsimd
        def _(g, hp=hp, wq=wq, wk=wk, wv=wv, w_reuse=w_reuse, dq=dq, dkv=dkv):
            wt(g, w_reuse)
            dq.inc(g.dma_start(out=wq, in_=wqkv[:, 128 * hp:128 * (hp + 1)]
                               .rearrange("(k p) f -> p k f", p=128)), 16)
            dkv.inc(g.dma_start(out=wk, in_=wqkv[:, 1024 + 128 * hp:1024 + 128 * (hp + 1)]
                                .rearrange("(k p) f -> p k f", p=128)), 16)
            dkv.inc(g.dma_start(out=wv, in_=wqkv[:, 2048 + 128 * hp:2048 + 128 * (hp + 1)]
                                .rearrange("(k p) f -> p k f", p=128)), 16)

        @c.block.tensor
        def _(t, hp=hp, wq=wq, wk=wk, wv=wv, q_w=q_w, kv_w=kv_w, kvacc_w=kvacc_w,
              ctx_w=ctx_w, tr_w=tr_w):
            for ch in range(4):
                wt(t, *q_w[ch])
                for k in range(ND):
                    ins = t.matmul(c.pb[ch].t[:, :], wq[:, 128 * k:128 * (k + 1)],
                                   c.xln[:, T * k + 512 * ch:T * k + 512 * (ch + 1)],
                                   start=(k == 0), stop=(k == ND - 1))
                c.pe.inc(ins)
            for ti in range(NT):
                b = ti % 4
                wt(t, *kv_w[ti])
                for k in range(ND):
                    t.matmul(c.pb[b].t[:, 0:128],
                             c.xln[:, T * k + 128 * ti:T * k + 128 * (ti + 1)],
                             wk[:, 128 * k:128 * (k + 1)],
                             start=(k == 0), stop=False)
                t.matmul(c.pb[b].t[:, 0:128], c.ones1[:, :],
                         bkv[0:1, 128 * hp:128 * (hp + 1)], start=False, stop=False)
                for k in range(ND):
                    t.matmul(c.pb[b].t[:, 128:256],
                             c.xln[:, T * k + 128 * ti:T * k + 128 * (ti + 1)],
                             wv[:, 128 * k:128 * (k + 1)],
                             start=False, stop=False)
                ins = t.matmul(c.pb[b].t[:, 128:256], c.ones1[:, :],
                               bkv[0:1, 1024 + 128 * hp:1024 + 128 * (hp + 1)],
                               start=False, stop=True)
                c.pe.inc(ins)
            for ti in range(NT):
                wt(t, *kvacc_w[ti])
                t.matmul(c.pb[4].t[:, 0:128], kc[:, 128 * ti:128 * (ti + 1)],
                         vtok[:, 128 * ti:128 * (ti + 1)],
                         start=(ti == 0), stop=False, skip_group_check=True)
                t.matmul(c.pb[4].t[:, 128:129], kc[:, 128 * ti:128 * (ti + 1)],
                         c.onescol[:, :], start=False, stop=(ti == NT - 1),
                         skip_group_check=True)
                t.matmul(c.pb[5].t[:, 0:128], ks[:, 128 * ti:128 * (ti + 1)],
                         vtok[:, 128 * ti:128 * (ti + 1)],
                         start=(ti == 0), stop=False, skip_group_check=True)
                ins = t.matmul(c.pb[5].t[:, 128:129], ks[:, 128 * ti:128 * (ti + 1)],
                               c.onescol[:, :], start=False, stop=(ti == NT - 1),
                               skip_group_check=True)
                c.pe.inc(ins)
            for r in range(8):
                t0 = 2 * r
                wt(t, *ctx_w[r])
                cb = 65 * (r % 2)
                for phase in range(2):
                    for u in range(4):
                        ti, hloc = t0 + u // 2, u % 2
                        b = 2 * (ti % 2) + hloc
                        h0 = 64 * hloc
                        src = qcT if phase == 0 else qsT
                        rhs = kvh[h0:h0 + 64, 0:65] if phase == 0 else kvh[h0:h0 + 64, 65:130]
                        ins = t.matmul(c.pb[b].t[:, cb:cb + 65],
                                       src[h0:h0 + 64, 128 * ti:128 * (ti + 1)],
                                       rhs, start=(phase == 0), stop=(phase == 1),
                                       tile_position=(h0, 0), skip_group_check=True)
                c.pe.inc(ins)
                for ti in (t0, t0 + 1):
                    wt(t, *tr_w[ti])
                    c.pe.inc(t.transpose(c.pt[ti % 2].t[:, 0:128], stg[ti % 2], c.id_bf[:]))

        @c.block.scalar
        def _(a, hp=hp, qev_w=qev_w, kvev_aw=kvev_aw, tr_mm=tr_mm, trev_t=trev_t,
              kvacc_mm=kvacc_mm):
            for ch in range(4):
                wt(a, *qev_w[ch])
                tdst = c.tmp0 if ch % 2 == 0 else c.tmp1
                c.sA.inc(a.activation(out=tdst[:, :], in_=c.pb[ch].t[:, :], func=AF.Relu,
                                      bias=c.bias[:, 40 + hp:41 + hp]))
            for ti in range(NT):
                b = ti % 4
                wt(a, *kvev_aw[ti])
                tdst = c.tmp0 if ti % 2 == 0 else c.tmp1
                c.sA.inc(a.activation(out=tdst[:, 0:128], in_=c.pb[b].t[:, 0:128],
                                      func=AF.Relu))
                c.sA.inc(a.copy(out=vtok[:, 128 * ti:128 * (ti + 1)],
                                in_=c.pb[b].t[:, 128:256]))
            wt(a, kvacc_mm[NT - 1])
            c.sA.inc(a.copy(out=kvh[0:64, 0:64], in_=c.pb[4].t[0:64, 0:64]))
            c.sA.inc(a.copy(out=kvh[64:128, 0:64], in_=c.pb[4].t[64:128, 64:128]))
            c.sA.inc(a.copy(out=kvh[0:64, 64:65], in_=c.pb[4].t[0:64, 128:129]))
            c.sA.inc(a.copy(out=kvh[64:128, 64:65], in_=c.pb[4].t[64:128, 128:129]))
            c.sA.inc(a.copy(out=kvh[0:64, 65:129], in_=c.pb[5].t[0:64, 0:64]))
            c.sA.inc(a.copy(out=kvh[64:128, 65:129], in_=c.pb[5].t[64:128, 64:128]))
            c.sA.inc(a.copy(out=kvh[0:64, 129:130], in_=c.pb[5].t[0:64, 128:129]))
            c.sA.inc(a.copy(out=kvh[64:128, 129:130], in_=c.pb[5].t[64:128, 128:129]))
            for r in range(8):
                t0 = 2 * r
                cb = 65 * (r % 2)
                rcd = c.cws if r % 2 == 0 else c.gacc
                for u in range(4):
                    ti, hloc = t0 + u // 2, u % 2
                    b = 2 * (ti % 2) + hloc
                    h0 = 64 * hloc
                    wt(a, *mul_w[(ti, hloc)])
                    c.sA.inc(a.mul(out=stg[ti % 2][:, 64 * hloc:64 * hloc + 64],
                                   in_=c.pb[b].t[:, cb:cb + 64], mul=rcd[:, 4 + u:5 + u]))
                for ti in (t0, t0 + 1):
                    wt(a, trev_t[ti] if False else tr_mm[ti])
                    c.sA.inc(a.copy(out=ctxT[:, 2048 * hp + 128 * ti:2048 * hp + 128 * (ti + 1)],
                                    in_=c.pt[ti % 2].t[:, 0:128]))

        @c.block.vector
        def _(v, hp=hp, qev_t=qev_t, kvev_a=kvev_a, ctx_mm=ctx_mm, mul_t=mul_t, pos_t=pos_t):
            if hp == 0:
                wt(v, pos_t)
            for ch in range(4):
                wt(v, qev_t[ch])
                tsrc = c.tmp0 if ch % 2 == 0 else c.tmp1
                c.sV.inc(v.tensor_tensor(out=qcT[:, 512 * ch:512 * (ch + 1)],
                                         in0=tsrc[:, :], in1=cosB[:, 512 * ch:512 * (ch + 1)],
                                         op=ALU.mult))
                c.sV.inc(v.tensor_tensor(out=qsT[:, 512 * ch:512 * (ch + 1)],
                                         in0=tsrc[:, :], in1=sinB[:, 512 * ch:512 * (ch + 1)],
                                         op=ALU.mult))
            for ti in range(NT):
                wt(v, kvev_a[ti])
                tsrc = c.tmp0 if ti % 2 == 0 else c.tmp1
                c.sV.inc(v.tensor_scalar(out=kc[:, 128 * ti:128 * (ti + 1)],
                                         in0=tsrc[:, 0:128], scalar1=c.poscol[:, ti:ti + 1],
                                         scalar2=None, op0=ALU.mult))
                c.sV.inc(v.tensor_scalar(out=ks[:, 128 * ti:128 * (ti + 1)],
                                         in0=tsrc[:, 0:128], scalar1=c.poscol[:, 16 + ti:17 + ti],
                                         scalar2=None, op0=ALU.mult))
            normsv = c.pbig_t[:, :].rearrange("p (b x) -> p b x", x=512)
            for r in range(8):
                cb = 65 * (r % 2)
                rcd = c.cws if r % 2 == 0 else c.gacc
                wt(v, ctx_mm[r])
                if r >= 2:
                    wt(v, mul_t[(2 * (r - 2) + 1, 1)])
                c.sV.inc(v.tensor_scalar(out=rcd[:, 0:4],
                                         in0=normsv[:, :, cb + 64:cb + 65],
                                         scalar1=1e-4, scalar2=None, op0=ALU.add))
                v.wait_ge(c.sV.h, c.sV.n)
                c.sV.inc(v.reciprocal(out=rcd[:, 4:8], in_=rcd[:, 0:4]))

    for i in range(4):
        c.pb[i].free_t = bank_free[i]
    c.pb[4].free_t = pb4_free
    c.pb[5].free_t = pb5_free
    c.pt[0].free_t = pt_free[0]
    c.pt[1].free_t = pt_free[1]
    ctx_last = ctx_all[-1]

    v0 = c.sV.n
    pe0 = c.pe.n
    wo_sem = [c.dw[8], c.dw[9]]
    wo_n0 = [c.dw[8].n, c.dw[9].n]
    wodma = {j: (wo_sem[j % 2], wo_n0[j % 2] + 16 * (j // 2 + 1)) for j in range(8)}
    woev_t = {(j, ch): (c.sV, v0 + 4 * j + ch + 1) for j in range(8) for ch in range(4)}
    womm = {}
    pe_n = pe0
    wo_w = {}
    bank_free = [c.pb[i].free_t for i in range(4)]
    for j in range(8):
        for ch in range(4):
            u = 4 * j + ch
            wo_w[(j, ch)] = [bank_free[ch] if u < 4 else woev_t[(j - 1, ch)]]
            pe_n += 1
            womm[(j, ch)] = (c.pe, pe_n)

    @c.block.gpsimd
    def _(g, wsl0=wslot_pe[0], wsl1=wslot_pe[1]):
        for j in range(8):
            if j >= 2:
                wt(g, womm[(j - 2, 3)])
            else:
                wt(g, wsl0, wsl1)
            wo_sem[j % 2].inc(g.dma_start(
                out=c.wbuf[:, 1024 * (j % 2):1024 * (j % 2 + 1)],
                in_=wout[:, 128 * j:128 * (j + 1)]
                .rearrange("(k p) f -> p k f", p=128)), 16)

    @c.block.tensor
    def _(t, ctx_last=ctx_last):
        wt(t, ctx_last)
        for j in range(8):
            wsl = c.wbuf[:, 1024 * (j % 2):1024 * (j % 2 + 1)]
            wt(t, wodma[j])
            for ch in range(4):
                wt(t, *wo_w[(j, ch)])
                for e in range(8):
                    ins = t.matmul(c.pb[ch].t[:, :], wsl[:, 128 * e:128 * (e + 1)],
                                   ctxT[:, 2048 * e + 512 * ch:2048 * e + 512 * (ch + 1)],
                                   start=(e == 0), stop=(e == 7))
                c.pe.inc(ins)

    @c.block.vector
    def _(v, xt_prev_t=xt_prev_t, b_dma_t=b_dma_t):
        wt(v, xt_prev_t, b_dma_t)
        for j in range(8):
            for ch in range(4):
                wt(v, womm[(j, ch)])
                dst = c.xT[:, T * j + 512 * ch:T * j + 512 * (ch + 1)]
                c.sV.inc(v.scalar_tensor_tensor(out=dst, in0=c.pb[ch].t[:, :],
                                                scalar=c.bias[:, 48 + j:49 + j],
                                                in1=dst, op0=ALU.add, op1=ALU.add))
    for j in range(8):
        for ch in range(4):
            c.pb[ch].mm_t = womm[(j, ch)]
            c.pb[ch].free_t = woev_t[(j, ch)]
    return c.sV.t, womm[(7, 3)], [woev_t[(7, ch)] for ch in range(4)]


def build_l0attn_test():
    nc = bass.Bass("TRN2", target_bir_lowering=False)
    x = nc.dram_tensor("x", [T, D], F32, kind="ExternalInput").ap()
    wqkv = nc.dram_tensor("wqkv", [D, 3072], F32, kind="ExternalInput").ap()
    wout = nc.dram_tensor("wout", [D, D], F32, kind="ExternalInput").ap()
    bq = nc.dram_tensor("bq", [128, 8], F32, kind="ExternalInput").ap()
    bo = nc.dram_tensor("bo", [128, 8], F32, kind="ExternalInput").ap()
    bkv = nc.dram_tensor("bkv", [1, 2048], F32, kind="ExternalInput").ap()
    y = nc.dram_tensor("y", [T, D], F32, kind="ExternalOutput").ap()
    with ExitStack() as stack:
        block = stack.enter_context(nc.Block())
        c = make_ctx(stack, nc, block)
        const_t = emit_consts(c)
        bq_t = emit_bias_cols(c, 40, bq, 8)
        bo_t = emit_bias_cols(c, 48, bo, 8)
        x_t, _ = emit_load_x_transposed(c, x, const_t)
        ln_t, _ = emit_ln(c, x_t)
        cos_t, pos_t, bkv_t = emit_l0attn_consts(c, ln_t, bkv)
        at_t, _, _ = emit_l0attn(c, ln_t, wqkv, wout, bo_t, (cos_t if True else pos_t),
                                 bkv_t, xt_prev_t=None)
        st = emit_store_x_tokenmajor(c, y, at_t)
        emit_wait_store(c, st)
    return nc



def build_full():
    nc = bass.Bass("TRN2", target_bir_lowering=False)
    dt = lambda n, sh: nc.dram_tensor(n, sh, F32, kind="ExternalInput").ap()
    x = dt("x", [T, D])
    wqkv0 = dt("wqkv0", [D, 3072])
    wout0 = dt("wout0", [D, D])
    bq0 = dt("bq0", [128, 8])
    bo0 = dt("bo0", [128, 8])
    bkv0 = dt("bkv0", [1, 2048])
    wrg = dt("wrg", [128, 128])
    w_gate = dt("w_gate", [8, D, 4096])
    w_up = dt("w_up", [8, D, 4096])
    w_down = dt("w_down", [8, 4096, D])
    wqkvT1 = dt("wqkvT1", [D, 3072])
    woutT1 = dt("woutT1", [D, D])
    w1t = dt("w1t", [D, 4096])
    w2t = dt("w2t", [4096, D])
    b1c = dt("b1c", [128, 32])
    b2c = dt("b2c", [128, 8])
    y = nc.dram_tensor("y", [T, D], F32, kind="ExternalOutput").ap()
    with ExitStack() as stack:
        block = stack.enter_context(nc.Block())
        c = make_ctx(stack, nc, block)
        with nc.named_scope("setup"):
            const_t = emit_consts(c)
            emit_bias_cols(c, 0, b1c, 32)
            emit_bias_cols(c, 32, b2c, 8)
            emit_bias_cols(c, 40, bq0, 8)
            bias_t = emit_bias_cols(c, 48, bo0, 8)
            wrg_t = emit_moe_wrg(c, wrg)
            x_t, x_ch = emit_load_x_transposed(c, x, const_t)
        with nc.named_scope("ln1"):
            ln1_t, ln1_ch = emit_ln(c, x_t, ready_chunk=x_ch,
                                    tscratch=[c.tmp0, c.tmp1, c.tmp0, c.tmp1],
                                    sq_engine="vector", rows_in_quarters=True)
            cos_t, pos_t, bkv_t = emit_l0attn_consts(c, ln1_t, bkv0)
        with nc.named_scope("l0attn"):
            at0_t, l0_pe, at0_ch = emit_l0attn(c, ln1_t, wqkv0, wout0, bias_t, cos_t,
                                               bkv_t, xt_prev_t=None, xln_ch=ln1_ch)
        with nc.named_scope("ln2"):
            ln2_t, ln2_ch = emit_ln(c, at0_t, at0_ch)
        with nc.named_scope("moecw"):
            cw8_t = emit_moe_cw(c, ln2_t, wrg_t, xln_ch=ln2_ch)
        with nc.named_scope("moe"):
            moe_t, moe_pe, moe_ch = emit_l0ffn(c, ln2_t, w_gate, w_up, w_down, cw8_t,
                                               arena_free_pe_t=l0_pe, wbuf_free_t=l0_pe,
                                               wbuf_free_pe_t=l0_pe, xln_ch=ln2_ch)
        with nc.named_scope("ln3"):
            ln3_t, ln3_ch = emit_ln(c, moe_t, moe_ch)
        with nc.named_scope("l1attn"):
            ghook, acc1_t, acc2_t = gamma_stream(c, w1t, w2t, D * 4096,
                                                 wbuf_free_t=moe_t)
            at1_t, l1_pe, at1_ch = emit_l1attn(c, ln3_t, wqkvT1, woutT1, xt_prev_t=moe_t,
                                               wbuf_free_pe_t=moe_pe, scalar_hook=ghook,
                                               xln_ch=ln3_ch)
        with nc.named_scope("gammas"):
            r1_t = emit_gamma_tail(c, D * 4096, 0, acc1_t)
            r2_t = emit_gamma_tail(c, D * 4096, 1, acc2_t)
        with nc.named_scope("ln4"):
            ln4_t, ln4_ch = emit_ln(c, at1_t, at1_ch, apply_extra=r2_t)
        with nc.named_scope("l1ffn"):
            f1_t, _, _, adds = emit_l1ffn(c, ln4_t, w1t, w2t, r1_t, r2_t, bias_t,
                                          wbuf_free_t=l1_pe, wbuf_free_pe_t=l1_pe,
                                          xln_ch=ln4_ch)
        with nc.named_scope("store"):
            st2 = emit_store_overlapped(c, y, adds, r2_t)
            emit_wait_store2(c, st2)
    return nc


def _layouts(inp):
    ac = np.ascontiguousarray
    b_qkv = inp["L0_attn_b_qkv"]
    wrg = np.concatenate([
        inp["L0_ffn_w_router"].reshape(8, 128, 8).transpose(1, 0, 2).reshape(128, 64),
        inp["L0_ffn_w_gating"].reshape(8, 128, 8).transpose(1, 0, 2).reshape(128, 64)],
        axis=1)
    return {
        "wqkv0": ac(inp["L0_attn_W_qkv"]),
        "wout0": ac(inp["L0_attn_W_out"]),
        "bq0": ac(b_qkv[0:1024].reshape(8, 128).T),
        "bo0": ac(inp["L0_attn_b_out"].reshape(8, 128).T),
        "bkv0": ac(b_qkv[1024:3072].reshape(1, 2048)),
        "wrg": ac(wrg),
        "w_gate": ac(inp["L0_ffn_w_gate"]),
        "w_up": ac(inp["L0_ffn_w_up"]),
        "w_down": ac(inp["L0_ffn_w_down"]),
        "wqkvT1": ac(inp["L1_attn_w_qkv"].T),
        "woutT1": ac(inp["L1_attn_w_out"].T),
        "w1t": ac(inp["L1_ffn_W1"].T),
        "w2t": ac(inp["L1_ffn_W2"].T),
        "b1c": ac(inp["L1_ffn_b1"].reshape(32, 128).T),
        "b2c": ac(inp["L1_ffn_b2"].reshape(8, 128).T),
    }


def kernel(**inputs):
    weights = _layouts(inputs)
    xfull = np.ascontiguousarray(inputs["L0_attn_x"], dtype=np.float32)
    nc = build_full()
    in_maps = []
    for b in range(8):
        m = dict(weights)
        m["x"] = np.ascontiguousarray(xfull[b])
        in_maps.append(m)
    res = run_bass_kernel_spmd(nc, in_maps, core_ids=list(range(8)))
    out = np.stack([res.results[b]["y"] for b in range(8)], axis=0)
    return out.astype(np.float32)


def emit_store_overlapped(c, out_hbm, adds, dma_prior_t):
    tmps = [c.tmp0, c.tmp1]
    yr = out_hbm.rearrange("(ti p) f -> p ti f", p=128)
    pe_n, a_n = c.pe.n, c.sA.n
    tr_w, tr_t = {}, {}
    ev_w, ev_t = {}, {}
    dma_t = {}
    bank_free = [c.pb[i].free_t for i in range(4)]
    dsem = [c.dx[0], c.dx[2]]
    d0 = [c.dx[0].n, c.dx[2].n]
    for m in range(8):
        for ti in range(NT):
            u = NT * m + ti
            Q = ti // 4
            prev_bank_ev = None
            if u >= 4:
                pm, pti = (m, ti - 4) if ti >= 4 else (m - 1, ti + 12)
                prev_bank_ev = ev_t[(pm, pti)]
            tr_w[(m, ti)] = [adds[m] if ti == 0 else None,
                             bank_free[ti % 4] if u < 4 else prev_bank_ev]
            pe_n += 1
            tr_t[(m, ti)] = (c.pe, pe_n)
            j = 2 * m + Q // 2
            prev_dma = (dsem[Q % 2], 16 * j + d0[Q % 2]) if j >= 1 else dma_prior_t
            ev_w[(m, ti)] = [tr_t[(m, ti)], prev_dma if ti % 4 == 0 else None]
            a_n += 1
            ev_t[(m, ti)] = (c.sA, a_n)
        for Q in range(4):
            dma_t[(m, Q)] = (dsem[Q % 2], d0[Q % 2] + 16 * (2 * m + Q // 2 + 1))

    @c.block.tensor
    def _(t):
        for m in range(8):
            for ti in range(NT):
                wt(t, *tr_w[(m, ti)])
                c.pe.inc(t.transpose(c.pb[ti % 4].t[:, 0:128],
                                     c.xT[:, T * m + 128 * ti:T * m + 128 * (ti + 1)],
                                     c.id_f32[:]))

    @c.block.scalar
    def _(a):
        for m in range(8):
            for ti in range(NT):
                wt(a, *ev_w[(m, ti)])
                tmp = tmps[(ti // 4) % 2]
                c.sA.inc(a.copy(out=tmp[:, 128 * (ti % 4):128 * (ti % 4 + 1)],
                                in_=c.pb[ti % 4].t[:, 0:128]))

    @c.block.sync
    def _(s):
        if dma_prior_t is not None:
            wt(s, dma_prior_t)
        for m in range(8):
            for Q in range(4):
                wt(s, ev_t[(m, 4 * Q + 3)])
                dsem[Q % 2].inc(s.dma_start(
                    out=yr[:, 4 * Q:4 * (Q + 1), 128 * m:128 * (m + 1)],
                    in_=tmps[Q % 2][:, :].rearrange("p (ti f) -> p ti f", f=128)), 16)
    for i in range(4):
        c.pb[i].free_t = ev_t[(7, 12 + i)]
    return [dma_t[(7, 2)], dma_t[(7, 3)]]


def emit_wait_store2(c, st2):
    @c.block.sync
    def _(s):
        for t in st2:
            wt(s, t)


def gamma_stream(c, w1_hbm, w2_hbm, n_elems, wbuf_free_t):
    chunk = 512
    wf = c.wbuf[:, :].bitcast(F32)
    bufs = [c.tmp0[:, :], c.tmp1[:, :], wf[:, 3072:3584], wf[:, 3584:4096]]
    nck = (n_elems // 128) // chunk
    wrs = [w.rearrange("a b -> (a b)").rearrange("(p n) -> p n", p=128)
           for w in (w1_hbm, w2_hbm)]
    a20 = c.sA2.n
    dma_t = []

    @c.block.sync
    def _(s):
        for i in range(2 * nck):
            if i >= 4:
                s.wait_ge(c.sA2.h, a20 + 2 * (i - 4) + 1)
            else:
                wt(s, wbuf_free_t)
            sem = [c.dx[2], c.dx[3], c.dx[5], c.dx[6]][i % 4]
            w, idx = i // nck, i % nck
            sem.inc(s.dma_start(out=bufs[i % 4],
                                in_=wrs[w][:, chunk * idx:chunk * (idx + 1)]), 16)
            dma_t.append(sem.t)

    def hook(a, hp, qt):
        for i in [16 * hp + qt]:
            which = i // nck
            i_w = i % nck
            wt(a, dma_t[i])
            if i > 0:
                a.wait_ge(c.sA2.h, a20 + 2 * i)
            c.sA2.inc(a.activation(out=bufs[i % 4], in_=bufs[i % 4], func=AF.Abs,
                                   accum_out=c.gacc[:, 4 + which:5 + which]))
            a.wait_ge(c.sA2.h, c.sA2.n)
            if i_w == 0:
                ins = a.copy(out=c.gacc[:, which:which + 1],
                             in_=c.gacc[:, 4 + which:5 + which])
            else:
                ins = a.add(out=c.gacc[:, which:which + 1],
                            in_=c.gacc[:, 4 + which:5 + which],
                            add=c.gacc[:, which:which + 1])
            c.sA2.inc(ins)
    acc1_t = (c.sA2, a20 + 2 * nck)
    acc2_t = (c.sA2, a20 + 4 * nck)
    return hook, acc1_t, acc2_t


def emit_gamma_tail(c, n_elems, which, acc_t):
    bank = c.pb[5]
    freew = bank.free_t

    @c.block.tensor
    def _(t):
        wt(t, acc_t, freew)
        c.pe.inc(t.matmul(bank.t[0:1, 0:1], c.gacc[:, which:which + 1],
                          c.onescol32[:, :], start=True, stop=True))
    mm1_t = c.pe.t

    @c.block.vector
    def _(v):
        wt(v, mm1_t)
        c.sV.inc(v.tensor_scalar(out=c.gacc[0:1, 6 + which:7 + which], in0=bank.t[0:1, 0:1],
                                 scalar1=1.0 / n_elems, scalar2=QEPS,
                                 op0=ALU.mult, op1=ALU.add))
        v.wait_ge(c.sV.h, c.sV.n)
        c.sV.inc(v.reciprocal(out=c.gacc[0:1, 6 + which:7 + which],
                              in_=c.gacc[0:1, 6 + which:7 + which]))
    rec_t = c.sV.t

    @c.block.tensor
    def _(t):
        wt(t, rec_t)
        c.pe.inc(t.matmul(bank.t[:, 0:1], c.ones1[:, :],
                          c.gacc[0:1, 6 + which:7 + which], start=True, stop=True))
    mm2_t = c.pe.t
    bank.mm_t = mm2_t

    @c.block.vector
    def _(v):
        wt(v, mm2_t)
        c.sV.inc(v.tensor_copy(out=c.gacc[:, 2 + which:3 + which], in_=bank.t[:, 0:1]))
    bank.free_t = c.sV.t
    return c.sV.t


# version 12
# speedup vs baseline: 1.1811x; 1.1811x over previous
import numpy as np
import ml_dtypes
from contextlib import ExitStack
import concourse.bass as bass
import concourse.mybir as mybir
from concourse.bass_utils import run_bass_kernel_spmd

F32 = mybir.dt.float32
BF16 = mybir.dt.bfloat16
AF = mybir.ActivationFunctionType
ALU = mybir.AluOpType
AX = mybir.AxisListType

T = 2048
D = 1024
NT = T // 128
ND = D // 128
ROUND_C = float(1.5 * 2.0**23)
PI = float(np.pi)
LN_EPS = 1e-5
QEPS = 1e-5


class SemCtr:
    def __init__(self, stack, nc, name):
        self.h = stack.enter_context(nc.semaphore(name))
        self.n = 0
        self.name = name

    def inc(self, instr, k=1):
        instr.then_inc(self.h, k)
        self.n += k
        return instr

    @property
    def t(self):
        return (self, self.n)


def wt(eng, *tickets):
    for t in tickets:
        if t is None:
            continue
        ctr, val = t
        if val > 0:
            eng.wait_ge(ctr.h, val)


class Bank:
    def __init__(self, tens):
        self.t = tens
        self.mm_t = None
        self.free_t = None


class Ctx:
    pass


def make_ctx(stack, nc, block):
    c = Ctx()
    c.nc = nc
    c.block = block
    sb = lambda name, shape, dt: stack.enter_context(nc.sbuf_tensor(name, shape, dt))
    ps = lambda name, shape, dt: stack.enter_context(nc.psum_tensor(name, shape, dt))
    c.xT = sb("xT", [128, ND * T], F32)
    c.xln = sb("xln", [128, ND * T], BF16)
    c.arenaA = sb("arenaA", [128, 32768], BF16)
    c.wbuf = sb("wbuf", [128, 8192], BF16)
    c.tmp0 = sb("tmp0", [128, 512], F32)
    c.tmp1 = sb("tmp1", [128, 512], F32)
    c.ones1 = sb("ones1", [1, 128], F32)
    c.onescol = sb("onescol", [128, 1], BF16)
    c.onescol32 = sb("onescol32", [128, 1], F32)
    c.id_bf = sb("id_bf", [128, 128], BF16)
    c.id_f32 = sb("id_f32", [128, 128], F32)
    c.consts = sb("consts", [1, 16], F32)
    c.gacc = sb("gacc", [128, 8], F32)
    c.bias = sb("bias", [128, 96], F32)
    c.moe_w = sb("moe_w", [128, 128], BF16)
    c.mask = sb("mask", [128, 128], F32)
    c.mask_bf = sb("mask_bf", [128, 128], BF16)
    c.poscol = sb("poscol", [128, 32], F32)
    c.cwt = sb("cwt", [128, 32], F32)
    c.cws = sb("cws", [128, 8], F32)
    c.cw8 = sb("cw8", [8, 2048], F32)
    c.cwrow = sb("cwrow", [1, 1024], F32)
    c.cwB = sb("cwB", [128, 2048], F32)
    c.pbig_t = stack.enter_context(nc.psum_tensor("pbig", [128, 2048], F32))
    c.pbig = c.pbig_t[:, :]
    c.pb = [Bank(c.pbig_t[:, 512 * i:512 * (i + 1)]) for i in range(4)]
    c.pb.append(Bank(ps("pb4", [128, 512], F32)))
    c.pb.append(Bank(ps("pb5", [128, 512], F32)))
    c.pt = [Bank(ps(f"pt{i}", [128, 1024], BF16)) for i in range(2)]
    c.pbigB = Bank(None)
    c.pe = SemCtr(stack, nc, "pe")
    c.sA = SemCtr(stack, nc, "sA")
    c.sA2 = SemCtr(stack, nc, "sA2")
    c.sV = SemCtr(stack, nc, "sV")
    c.sP = SemCtr(stack, nc, "sP")
    c.dx = [SemCtr(stack, nc, f"dx{i}") for i in range(8)]
    c.dw = [SemCtr(stack, nc, f"dw{i}") for i in range(16)]
    c.arenaAf = c.arenaA[:, :].bitcast(F32)
    return c


def emit_consts(c):
    @c.block.gpsimd
    def _(g):
        g.memset(c.ones1[:], 1.0)
        g.memset(c.onescol[:], 1.0)
        g.memset(c.onescol32[:], 1.0)
        g.memset(c.consts[0:1, 0:1], LN_EPS)
        g.memset(c.consts[0:1, 1:2], PI / 2)
        g.memset(c.bias[:, 56:57], PI / 2)
        c.sP.inc(g.memset(c.id_bf[:], 0.0))
        c.sP.inc(g.memset(c.id_f32[:], 0.0))
        g.wait_ge(c.sP.h, c.sP.n)
        c.sP.inc(g.affine_select(out=c.id_bf[:], in_=c.id_bf[:], compare_op=ALU.not_equal,
                                 fill=1.0, base=0, pattern=[[-1, 128]], channel_multiplier=1))
        c.sP.inc(g.affine_select(out=c.id_f32[:], in_=c.id_f32[:], compare_op=ALU.not_equal,
                                 fill=1.0, base=0, pattern=[[-1, 128]], channel_multiplier=1))
        c.sP.inc(g.memset(c.mask[:], 0.0))
        c.sP.inc(g.memset(c.mask_bf[:], 0.0))
        g.wait_ge(c.sP.h, c.sP.n)
        c.sP.inc(g.affine_select(out=c.mask[:], in_=c.mask[:], compare_op=ALU.is_ge,
                                 fill=-10000.0, base=0, pattern=[[-1, 128]],
                                 channel_multiplier=1))
        c.sP.inc(g.affine_select(out=c.mask_bf[:], in_=c.mask_bf[:], compare_op=ALU.is_ge,
                                 fill=-10000.0, base=0, pattern=[[-1, 128]],
                                 channel_multiplier=1))
    return c.sP.t


def _transpose_pipe(c, pairs, src_of, dst_of, ident, banks, pre_wait=None,
                    evac="scalar", quad_wait=None):
    nb = len(banks)
    ctr = c.sA if evac == "scalar" else c.sV
    pe0, e0 = c.pe.n, ctr.n
    n = len(pairs)
    quads = [list(range(q, min(q + 4, n))) for q in range(0, n, 4)]
    free0 = [b.free_t for b in banks]
    pe_wait = []
    mm_t = []
    ev_cnt = 0
    for g, q in enumerate(quads):
        pe_wait.append(free0[g % nb] if g < nb else (ctr, e0 + (g - nb + 1) * 4))
        mm_t.append((c.pe, pe0 + g + 1))

    @c.block.tensor
    def _(t):
        if pre_wait is not None:
            pre_wait(t)
        for g, q in enumerate(quads):
            bank = banks[g % nb]
            if quad_wait is not None:
                quad_wait(t, g)
            wt(t, pe_wait[g])
            for j, i in enumerate(q):
                ins = t.transpose(bank.t[:, 128 * j:128 * (j + 1)], src_of(pairs[i]), ident)
            c.pe.inc(ins)

    sec = getattr(c.block, evac)

    @sec
    def _(e):
        for g, q in enumerate(quads):
            bank = banks[g % nb]
            wt(e, mm_t[g])
            for j, i in enumerate(q):
                ins = e.copy(out=dst_of(pairs[i]), in_=bank.t[:, 128 * j:128 * (j + 1)]) \
                    if evac == "scalar" else \
                    e.tensor_copy(out=dst_of(pairs[i]), in_=bank.t[:, 128 * j:128 * (j + 1)])
                ctr.inc(ins)

    for g, q in enumerate(quads):
        banks[g % nb].mm_t = mm_t[g]
        banks[g % nb].free_t = (ctr, e0 + (g + 1) * 4 if len(q) == 4 else e0 + g * 4 + len(q))
    return ctr.t


def emit_load_x_transposed(c, x_hbm, const_t):
    stage = c.arenaAf
    xr = x_hbm.rearrange("(ti p) d -> p ti d", p=128)

    ld_sems = [c.dx[0], c.dx[3], c.dx[4], c.dx[5]]
    ld_tk = []

    @c.block.sync
    def _(s):
        for i in range(4):
            ld_sems[i].inc(s.dma_start(out=stage[:, 4096 * i:4096 * (i + 1)],
                                       in_=xr[:, 4 * i:4 * (i + 1), :]), 16)
            ld_tk.append(ld_sems[i].t)

    pairs = [(ti, k) for ti in range(NT) for k in range(ND)]

    def pre(t):
        wt(t, const_t)

    def quad_wait(t, g):
        ti = pairs[4 * g][0]
        wt(t, ld_tk[ti // 4])

    e0 = c.sA.n
    full_t = _transpose_pipe(
        c, pairs,
        src_of=lambda it: stage[:, 1024 * it[0] + 128 * it[1]:1024 * it[0] + 128 * (it[1] + 1)],
        dst_of=lambda it: c.xT[:, T * it[1] + 128 * it[0]:T * it[1] + 128 * (it[0] + 1)],
        ident=c.id_f32[:], banks=c.pb[0:4], pre_wait=pre, quad_wait=quad_wait)
    x_ch = [(c.sA, e0 + 32 * (ch + 1)) for ch in range(4)]
    return full_t, x_ch


def emit_ln(c, ready_t, ready_chunk=None, apply_extra=None, tscratch=None,
            sq_engine="scalar", rows_in_quarters=False):
    af = c.arenaAf
    if rows_in_quarters:
        mrow = [af[0:1, 4096 * ch:4096 * ch + 512] for ch in range(4)]
        vrow = [af[0:1, 4096 * ch + 512:4096 * ch + 1024] for ch in range(4)]
        trow = [af[0:1, 4096 * ch + 1024:4096 * ch + 1536] for ch in range(4)]
    else:
        mrow = [af[0:1, 4096 + 512 * ch:4096 + 512 * (ch + 1)] for ch in range(4)]
        vrow = [af[0:1, 6144 + 512 * ch:6144 + 512 * (ch + 1)] for ch in range(4)]
        trow = [af[0:1, 8192 + 512 * ch:8192 + 512 * (ch + 1)] for ch in range(4)]
    if tscratch is None:
        tscratch = [c.arenaAf[:, 10240 + 512 * i:10240 + 512 * (i + 1)]
                    for i in range(4)]
    rdy = [ready_chunk[ch] if ready_chunk else ready_t for ch in range(4)]

    a0, p0, v0 = c.sA.n, c.pe.n, c.sV.n
    on_act = sq_engine == "scalar"
    if on_act:
        sq_t = [(c.sA, a0 + 8 * (ch + 1)) for ch in range(4)]
        a2_t = [(c.sA, a0 + 32 + 2 * (ch + 1)) for ch in range(4)]
        vbase = v0
    else:
        sq_t = [(c.sV, v0 + 8 * (ch + 1)) for ch in range(4)]
        a2_t = [(c.sA, a0 + 2 * (ch + 1)) for ch in range(4)]
        vbase = v0 + 32
    sums_mm = [(c.pe, p0 + n) for n in (2, 4, 8, 12)]
    rep_mm = [(c.pe, p0 + n) for n in (6, 10, 14, 16)]
    stats_t = [(c.sV, vbase + n) for n in (6, 12, 34, 56)]
    sum_free = [(c.sV, vbase + n) for n in (2, 8, 30, 52)]
    apply_t = [(c.sV, vbase + n) for n in (28, 50, 72, 88)]

    fs0, fq0 = c.pb[4].free_t, c.pb[5].free_t
    pair_free = [[c.pb[0].free_t, c.pb[1].free_t],
                 [c.pb[2].free_t, c.pb[3].free_t]]

    sq_sec = c.block.scalar if on_act else c.block.vector
    ctr = c.sA if on_act else c.sV
    for ch in range(4):
        sl = slice(512 * ch, 512 * (ch + 1))

        @sq_sec
        def _(e, ch=ch, sl=sl):
            wt(e, rdy[ch])
            for k in range(ND):
                if on_act:
                    ctr.inc(e.activation(out=c.xln[:, T * k + sl.start:T * k + sl.stop],
                                         in_=c.xT[:, T * k + sl.start:T * k + sl.stop],
                                         func=AF.Square))
                else:
                    ctr.inc(e.tensor_tensor(out=c.xln[:, T * k + sl.start:T * k + sl.stop],
                                            in0=c.xT[:, T * k + sl.start:T * k + sl.stop],
                                            in1=c.xT[:, T * k + sl.start:T * k + sl.stop],
                                            op=ALU.mult))
                    e.wait_ge(ctr.h, ctr.n)

    def emit_sums(ch):
        sl = slice(512 * ch, 512 * (ch + 1))

        @c.block.tensor
        def _(t, ch=ch, sl=sl):
            wt(t, rdy[ch], fs0 if ch == 0 else sum_free[ch - 1])
            for k in range(ND):
                ins = t.matmul(c.pb[4].t[0:1, :], c.onescol32[:, :],
                               c.xT[:, T * k + sl.start:T * k + sl.stop],
                               start=(k == 0), stop=(k == ND - 1))
            c.pe.inc(ins)
            wt(t, sq_t[ch], fq0 if ch == 0 else None)
            for k in range(ND):
                ins = t.matmul(c.pb[5].t[0:1, :], c.onescol[:, :],
                               c.xln[:, T * k + sl.start:T * k + sl.stop],
                               start=(k == 0), stop=(k == ND - 1))
            c.pe.inc(ins)

    def emit_rep(ch):
        sl = slice(512 * ch, 512 * (ch + 1))
        p = ch % 2
        bmean, brstd = c.pb[2 * p], c.pb[2 * p + 1]
        fm = pair_free[p][0] if ch < 2 else apply_t[ch - 2]
        fr = pair_free[p][1] if ch < 2 else apply_t[ch - 2]

        @c.block.tensor
        def _(t, ch=ch, sl=sl, bmean=bmean, brstd=brstd, fm=fm, fr=fr):
            wt(t, stats_t[ch], fm)
            c.pe.inc(t.matmul(bmean.t[:, :], c.ones1[:, :], mrow[ch],
                              start=True, stop=True))
            wt(t, a2_t[ch], fr)
            c.pe.inc(t.matmul(brstd.t[:, :], c.ones1[:, :], vrow[ch],
                              start=True, stop=True))
        bmean.mm_t = rep_mm[ch]
        brstd.mm_t = rep_mm[ch]
        bmean.free_t = apply_t[ch]
        brstd.free_t = apply_t[ch]

    for step in (emit_sums, 0), (emit_sums, 1), (emit_rep, 0), (emit_sums, 2),             (emit_rep, 1), (emit_sums, 3), (emit_rep, 2), (emit_rep, 3):
        step[0](step[1])

    for ch in range(4):
        sl = slice(512 * ch, 512 * (ch + 1))

        @c.block.scalar
        def _(a, ch=ch, sl=sl):
            wt(a, stats_t[ch])
            c.sA.inc(a.activation(out=vrow[ch], in_=vrow[ch], func=AF.Ln,
                                  bias=c.consts[0:1, 0:1]))
            a.wait_ge(c.sA.h, c.sA.n)
            c.sA.inc(a.activation(out=vrow[ch], in_=vrow[ch], func=AF.Exp,
                                  scale=-0.5))

    def emit_stats(ch):
        sl = slice(512 * ch, 512 * (ch + 1))

        @c.block.vector
        def _(v, ch=ch, sl=sl):
            wt(v, sums_mm[ch])
            c.sV.inc(v.tensor_copy(out=mrow[ch], in_=c.pb[4].t[0:1, :]))
            c.sV.inc(v.tensor_copy(out=vrow[ch], in_=c.pb[5].t[0:1, :]))
            v.wait_ge(c.sV.h, c.sV.n)
            c.sV.inc(v.tensor_scalar(out=mrow[ch], in0=mrow[ch],
                                     scalar1=1.0 / D, scalar2=None, op0=ALU.mult))
            c.sV.inc(v.tensor_scalar(out=vrow[ch], in0=vrow[ch],
                                     scalar1=1.0 / D, scalar2=None, op0=ALU.mult))
            v.wait_ge(c.sV.h, c.sV.n)
            c.sV.inc(v.scalar_tensor_tensor(out=trow[ch], in0=mrow[ch],
                                            scalar=-1.0, in1=mrow[ch],
                                            op0=ALU.mult, op1=ALU.mult))
            v.wait_ge(c.sV.h, c.sV.n)
            c.sV.inc(v.tensor_tensor(out=vrow[ch], in0=vrow[ch],
                                     in1=trow[ch], op=ALU.add))

    def emit_apply(ch):
        sl = slice(512 * ch, 512 * (ch + 1))
        p = ch % 2
        bmean, brstd = c.pb[2 * p], c.pb[2 * p + 1]
        tfull = tscratch[ch]

        @c.block.vector
        def _(v, ch=ch, sl=sl, bmean=bmean, brstd=brstd, tfull=tfull):
            wt(v, rep_mm[ch], apply_extra if ch < 2 else None)
            for k in range(ND):
                c.sV.inc(v.tensor_tensor(out=tfull[:, :],
                                         in0=c.xT[:, T * k + sl.start:T * k + sl.stop],
                                         in1=bmean.t[:, :], op=ALU.subtract))
                v.wait_ge(c.sV.h, c.sV.n)
                c.sV.inc(v.tensor_tensor(out=c.xln[:, T * k + sl.start:T * k + sl.stop],
                                         in0=tfull[:, :], in1=brstd.t[:, :], op=ALU.mult))
                v.wait_ge(c.sV.h, c.sV.n)

    for step in (emit_stats, 0), (emit_stats, 1), (emit_apply, 0), (emit_stats, 2),             (emit_apply, 1), (emit_stats, 3), (emit_apply, 2), (emit_apply, 3):
        step[0](step[1])

    c.pb[4].free_t = sum_free[3]
    c.pb[5].free_t = sum_free[3]
    c.pb[4].mm_t = (c.pe, p0 + 11)
    c.pb[5].mm_t = (c.pe, p0 + 12)
    assert c.sA.n == a0 + (40 if on_act else 8)
    assert c.pe.n == p0 + 16
    assert c.sV.n == vbase + 88
    return apply_t[3], list(apply_t)


def emit_store_x_tokenmajor(c, out_hbm, ready_t):
    stage = c.arenaAf
    pairs = [(ti, k) for ti in range(NT) for k in range(ND)]

    def pre(t):
        wt(t, ready_t)

    st_t = _transpose_pipe(
        c, pairs,
        src_of=lambda it: c.xT[:, T * it[1] + 128 * it[0]:T * it[1] + 128 * (it[0] + 1)],
        dst_of=lambda it: stage[:, 1024 * it[0] + 128 * it[1]:1024 * it[0] + 128 * (it[1] + 1)],
        ident=c.id_f32[:], banks=c.pb[0:4], pre_wait=pre)

    xr = out_hbm.rearrange("(ti p) d -> p ti d", p=128)

    @c.block.sync
    def _(s):
        wt(s, st_t)
        for i in range(4):
            c.dx[0].inc(s.dma_start(out=xr[:, 4 * i:4 * (i + 1), :],
                                    in_=stage[:, 4096 * i:4096 * (i + 1)]), 16)
    return c.dx[0].t


def emit_wait_store(c, st_t):
    @c.block.sync
    def _(s):
        wt(s, st_t)



def emit_gamma(c, w_hbm, n_elems, which, wbuf_free_t=None, use_tmp=False):
    eng_ctr = c.sA if use_tmp else c.sV
    sec_name = "scalar" if use_tmp else "vector"
    if use_tmp:
        chunk = 512
        bufs = [c.tmp0[:, :], c.tmp1[:, :]]
    else:
        chunk = 2048
        wf = c.wbuf[:, :].bitcast(F32)
        bufs = [wf[:, 0:2048], wf[:, 2048:4096]]
    per_part = n_elems // 128
    nchunks = per_part // chunk
    wr = w_hbm.rearrange("a b -> (a b)").rearrange("(p n) -> p n", p=128)

    e0 = eng_ctr.n
    dma_t = []

    @c.block.sync
    def _(s):
        for i in range(nchunks):
            if i >= 2:
                s.wait_ge(eng_ctr.h, e0 + 2 * (i - 2) + 1)
            else:
                wt(s, wbuf_free_t)
            sem = c.dx[2 + i % 2]
            sem.inc(s.dma_start(out=bufs[i % 2], in_=wr[:, chunk * i:chunk * (i + 1)]), 16)
            dma_t.append(sem.t)

    sec = getattr(c.block, sec_name)

    @sec
    def _(v):
        for i in range(nchunks):
            wt(v, dma_t[i])
            if i > 0:
                v.wait_ge(eng_ctr.h, e0 + 2 * i)
            if use_tmp:
                eng_ctr.inc(v.activation(out=bufs[i % 2], in_=bufs[i % 2], func=AF.Abs,
                                         accum_out=c.gacc[:, 4 + which:5 + which]))
                v.wait_ge(eng_ctr.h, eng_ctr.n)
                if i == 0:
                    ins = v.copy(out=c.gacc[:, which:which + 1],
                                 in_=c.gacc[:, 4 + which:5 + which])
                else:
                    ins = v.add(out=c.gacc[:, which:which + 1],
                                in_=c.gacc[:, 4 + which:5 + which],
                                add=c.gacc[:, which:which + 1])
                eng_ctr.inc(ins)
            else:
                eng_ctr.inc(v.tensor_reduce(out=c.gacc[:, 4 + which:5 + which],
                                            in_=bufs[i % 2],
                                            axis=AX.X, op=ALU.add, apply_absolute_value=True))
                v.wait_ge(eng_ctr.h, eng_ctr.n)
                if i == 0:
                    ins = v.tensor_copy(out=c.gacc[:, which:which + 1],
                                        in_=c.gacc[:, 4 + which:5 + which])
                else:
                    ins = v.tensor_tensor(out=c.gacc[:, which:which + 1],
                                          in0=c.gacc[:, which:which + 1],
                                          in1=c.gacc[:, 4 + which:5 + which], op=ALU.add)
                eng_ctr.inc(ins)
    acc_t = eng_ctr.t

    bank = c.pb[5]
    freew = bank.free_t

    @c.block.tensor
    def _(t):
        wt(t, acc_t, freew)
        c.pe.inc(t.matmul(bank.t[0:1, 0:1], c.gacc[:, which:which + 1],
                          c.onescol32[:, :], start=True, stop=True))
    mm1_t = c.pe.t

    @c.block.vector
    def _(v):
        wt(v, mm1_t)
        c.sV.inc(v.tensor_scalar(out=c.gacc[0:1, 6 + which:7 + which], in0=bank.t[0:1, 0:1],
                                 scalar1=1.0 / n_elems, scalar2=QEPS,
                                 op0=ALU.mult, op1=ALU.add))
        v.wait_ge(c.sV.h, c.sV.n)
        c.sV.inc(v.reciprocal(out=c.gacc[0:1, 6 + which:7 + which],
                              in_=c.gacc[0:1, 6 + which:7 + which]))
    rec_t = c.sV.t

    @c.block.tensor
    def _(t):
        wt(t, rec_t)
        c.pe.inc(t.matmul(bank.t[:, 0:1], c.ones1[:, :],
                          c.gacc[0:1, 6 + which:7 + which], start=True, stop=True))
    mm2_t = c.pe.t
    bank.mm_t = mm2_t

    @c.block.vector
    def _(v):
        wt(v, mm2_t)
        c.sV.inc(v.tensor_copy(out=c.gacc[:, 2 + which:3 + which], in_=bank.t[:, 0:1]))
    bank.free_t = c.sV.t
    return c.sV.t


def emit_bias_cols(c, col0, arr_hbm, ncols):
    @c.block.sync
    def _(s):
        c.dx[1].inc(s.dma_start(out=c.bias[:, col0:col0 + ncols], in_=arr_hbm), 16)
    return c.dx[1].t


def emit_l1ffn(c, xln_t, w1t, w2t, r1_t, r2_t, bias_t, wbuf_free_t, wbuf_free_pe_t,
               phases=("h", "out"), halves=(0, 1), xln_ch=None):
    wf = c.wbuf[:, :].bitcast(F32)
    h = c.arenaA

    for half in halves:
        hs0 = 1024 * half
        v0, pe0, a0 = c.sV.n, c.pe.n, c.sA.n
        dxa, dxb = c.dx[4].n, c.dx[5].n
        dma_t = [(c.dx[4 + m % 2], (dxa if m % 2 == 0 else dxb) + 16 * (m // 2 + 1))
                 for m in range(32)]
        qt_t = [(c.sV, v0 + 3 * (m + 1)) for m in range(32)]
        mm_t = {(m, c2): (c.pe, pe0 + 2 * m + c2 + 1)
                for m in range(32) for c2 in range(2)}
        ev_t = {(m, c2): (c.sA, a0 + 2 * m + c2 + 1)
                for m in range(32) for c2 in range(2)}
        bank_free0 = [c.pb[i].free_t for i in range(4)]

        @c.block.sync
        def _(s, v0=v0, wbuf_free_t=wbuf_free_t):
            for m in range(32):
                if m >= 2:
                    s.wait_ge(c.sV.h, v0 + 3 * (m - 1))
                else:
                    wt(s, wbuf_free_t)
                c.dx[4 + m % 2].inc(
                    s.dma_start(out=wf[:, 1024 * (m % 2):1024 * (m % 2 + 1)],
                                in_=w1t[:, 1024 * m:1024 * (m + 1)]), 16)

        @c.block.vector
        def _(v, wbuf_free_pe_t=wbuf_free_pe_t):
            wt(v, r1_t, wbuf_free_pe_t)
            for m in range(32):
                qsl = c.wbuf[:, 4096 + 1024 * (m % 2):4096 + 1024 * (m % 2 + 1)]
                wt(v, dma_t[m])
                if m >= 2:
                    wt(v, mm_t[(m - 2, 1)])
                raw = wf[:, 1024 * (m % 2):1024 * (m % 2 + 1)]
                c.sV.inc(v.tensor_scalar(out=raw, in0=raw, scalar1=c.gacc[:, 2:3],
                                         scalar2=None, op0=ALU.mult))
                v.wait_ge(c.sV.h, c.sV.n)
                c.sV.inc(v.tensor_scalar(out=raw, in0=raw, scalar1=ROUND_C, scalar2=ROUND_C,
                                         op0=ALU.add, op1=ALU.subtract))
                v.wait_ge(c.sV.h, c.sV.n)
                c.sV.inc(v.tensor_scalar(out=qsl, in0=raw, scalar1=1.0, scalar2=-1.0,
                                         op0=ALU.min, op1=ALU.max))

        @c.block.tensor
        def _(t, hs0=hs0, bank_free0=bank_free0, half=half):
            if xln_ch and half == halves[0]:
                wt(t, xln_ch[2 * half], xln_ch[2 * half + 1])
            elif half == halves[0]:
                wt(t, xln_t)
            for m in range(32):
                qsl = c.wbuf[:, 4096 + 1024 * (m % 2):4096 + 1024 * (m % 2 + 1)]
                wt(t, qt_t[m])
                for c2 in range(2):
                    i = 2 * m + c2
                    bank = c.pb[i % 4]
                    wt(t, bank_free0[i % 4] if i < 4 else ev_t[(m - 2, c2)])
                    for k in range(ND):
                        ins = t.matmul(bank.t[:, :], qsl[:, 128 * k:128 * (k + 1)],
                                       c.xln[:, T * k + hs0 + 512 * c2:T * k + hs0 + 512 * (c2 + 1)],
                                       start=(k == 0), stop=(k == ND - 1))
                    c.pe.inc(ins)

        @c.block.scalar
        def _(a, half=half, bias_t=bias_t):
            if half == halves[0]:
                wt(a, bias_t)
            for m in range(32):
                for c2 in range(2):
                    bank = c.pb[(2 * m + c2) % 4]
                    wt(a, mm_t[(m, c2)])
                    c.sA.inc(a.activation(
                        out=h[:, 1024 * m + 512 * c2:1024 * m + 512 * (c2 + 1)],
                        in_=bank.t[:, :], func=AF.Relu, bias=c.bias[:, m:m + 1]))
        for m in range(32):
            for c2 in range(2):
                bank = c.pb[(2 * m + c2) % 4]
                bank.mm_t = mm_t[(m, c2)]
                bank.free_t = ev_t[(m, c2)]
        h_t = c.sA.t
        wbuf_free_t = qt_t[31]
        wbuf_free_pe_t = mm_t[(31, 1)]
        if "out" not in phases:
            continue

        v0, pe0 = c.sV.n, c.pe.n
        qsems = [c.dx[4], c.dx[5], c.dx[6], c.dx[7]]
        qbase = [sem.n for sem in qsems]
        dma2_t = {}
        for i in range(64):
            dma2_t[i] = (qsems[i % 4], qbase[i % 4] + 16 * (i // 4 + 1))
        sched = []
        for m in range(8):
            sched += [("q", m, qq, j) for qq in range(8) for j in range(3)]
            if m >= 1:
                sched += [("a", m - 1, 0, 0), ("a", m - 1, 1, 0)]
        sched += [("a", 7, 0, 0), ("a", 7, 1, 0)]
        svt = {}
        for i, it in enumerate(sched):
            svt[it[:3]] = (c.sV, v0 + i + 1)
        q4_t = {(m, qq): svt[("q", m, qq)] for m in range(8) for qq in range(8)}
        add_t = {(m, c2): svt[("a", m, c2)] for m in range(8) for c2 in range(2)}
        mm2_h = {(m, c2): (c.pe, pe0 + 4 * m + 2 * c2 + 1)
                 for m in range(8) for c2 in range(2)}
        mm2_t = {(m, c2): (c.pe, pe0 + 4 * m + 2 * c2 + 2)
                 for m in range(8) for c2 in range(2)}
        bank_free0 = [c.pb[i].free_t for i in range(6)]

        @c.block.sync
        def _(s, wbuf_free_t=wbuf_free_t):
            for i in range(64):
                if i >= 4:
                    wt(s, q4_t[((i - 4) // 8, (i - 4) % 8)])
                else:
                    wt(s, wbuf_free_t)
                m, qq = i // 8, i % 8
                qsems[i % 4].inc(s.dma_start(
                    out=wf[:, 512 * (i % 4):512 * (i % 4 + 1)],
                    in_=w2t[:, 4096 * m + 512 * qq:4096 * m + 512 * (qq + 1)]), 16)

        @c.block.vector
        def _(v, half=half, hs0=hs0, wbuf_free_pe_t=wbuf_free_pe_t, bias_t=bias_t):
            wt(v, r2_t, wbuf_free_pe_t)
            if half == halves[0]:
                wt(v, bias_t)
            for m in range(8):
                for qq in range(8):
                    i = 8 * m + qq
                    wt(v, dma2_t[i])
                    if m >= 2 and qq == 0:
                        wt(v, mm2_t[(m - 2, 1)])
                    raw = wf[:, 512 * (i % 4):512 * (i % 4 + 1)]
                    if m % 2 == 0:
                        qsub = c.wbuf[:, 4096 + 512 * qq:4096 + 512 * (qq + 1)]
                    else:
                        qsub = c.xln[:, 2048 * qq + 1024 * half:2048 * qq + 1024 * half + 512]
                    c.sV.inc(v.tensor_scalar(out=raw, in0=raw, scalar1=c.gacc[:, 3:4],
                                             scalar2=None, op0=ALU.mult))
                    v.wait_ge(c.sV.h, c.sV.n)
                    c.sV.inc(v.tensor_scalar(out=raw, in0=raw, scalar1=ROUND_C,
                                             scalar2=ROUND_C, op0=ALU.add, op1=ALU.subtract))
                    v.wait_ge(c.sV.h, c.sV.n)
                    c.sV.inc(v.tensor_scalar(out=qsub, in0=raw, scalar1=1.0, scalar2=-1.0,
                                             op0=ALU.min, op1=ALU.max))
                    v.wait_ge(c.sV.h, c.sV.n)
                if m >= 1:
                    for c2 in range(2):
                        bank = c.pb[4 + c2]
                        wt(v, mm2_t[(m - 1, c2)])
                        dst = c.xT[:, T * (m - 1) + hs0 + 512 * c2:T * (m - 1) + hs0 + 512 * (c2 + 1)]
                        c.sV.inc(v.scalar_tensor_tensor(out=dst, in0=bank.t[:, :],
                                                        scalar=c.bias[:, 32 + m - 1:33 + m - 1],
                                                        in1=dst, op0=ALU.add, op1=ALU.add))
                        v.wait_ge(c.sV.h, c.sV.n)
            for c2 in range(2):
                bank = c.pb[4 + c2]
                wt(v, mm2_t[(7, c2)])
                dst = c.xT[:, T * 7 + hs0 + 512 * c2:T * 7 + hs0 + 512 * (c2 + 1)]
                c.sV.inc(v.scalar_tensor_tensor(out=dst, in0=bank.t[:, :],
                                                scalar=c.bias[:, 39:40],
                                                in1=dst, op0=ALU.add, op1=ALU.add))
                v.wait_ge(c.sV.h, c.sV.n)

        @c.block.tensor
        def _(t, hs0=hs0, bank_free0=bank_free0, h_t=h_t, half=half):
            wt(t, h_t)

            def qtile(m, kk):
                if m % 2 == 0:
                    return c.wbuf[:, 4096 + 128 * kk:4096 + 128 * (kk + 1)]
                col = 2048 * (kk // 4) + 1024 * half + 128 * (kk % 4)
                return c.xln[:, col:col + 128]
            for m in range(8):
                for c2 in range(2):
                    i = 2 * m + c2
                    bank = c.pb[4 + c2]
                    if c2 == 0:
                        wt(t, q4_t[(m, 3)])
                    wt(t, bank_free0[4 + c2] if i < 2 else add_t[(m - 1, c2)])
                    for kk in range(16):
                        ins = t.matmul(bank.t[:, :], qtile(m, kk),
                                       h[:, 1024 * kk + 512 * c2:1024 * kk + 512 * (c2 + 1)],
                                       start=(kk == 0), stop=False)
                    c.pe.inc(ins)
                    if c2 == 0:
                        wt(t, q4_t[(m, 7)])
                    for kk in range(16, 32):
                        ins = t.matmul(bank.t[:, :], qtile(m, kk),
                                       h[:, 1024 * kk + 512 * c2:1024 * kk + 512 * (c2 + 1)],
                                       start=False, stop=(kk == 31))
                    c.pe.inc(ins)
        for m in range(8):
            for c2 in range(2):
                bank = c.pb[4 + c2]
                bank.mm_t = mm2_t[(m, c2)]
                bank.free_t = add_t[(m, c2)]
        wbuf_free_t = q4_t[(7, 7)]
        wbuf_free_pe_t = mm2_t[(7, 1)]
        if half == halves[-1]:
            final_add = {m: add_t[(m, 1)] for m in range(8)}
    return c.sV.t, wbuf_free_t, wbuf_free_pe_t, final_add


def build_l1ffn_test():
    nc = bass.Bass("TRN2", target_bir_lowering=False)
    x = nc.dram_tensor("x", [T, D], F32, kind="ExternalInput").ap()
    w1t = nc.dram_tensor("w1t", [128, 32768], F32, kind="ExternalInput").ap()
    w2t = nc.dram_tensor("w2t", [128, 32768], F32, kind="ExternalInput").ap()
    b1c = nc.dram_tensor("b1c", [128, 32], F32, kind="ExternalInput").ap()
    b2c = nc.dram_tensor("b2c", [128, 8], F32, kind="ExternalInput").ap()
    y = nc.dram_tensor("y", [T, D], F32, kind="ExternalOutput").ap()
    with ExitStack() as stack:
        block = stack.enter_context(nc.Block())
        c = make_ctx(stack, nc, block)
        const_t = emit_consts(c)
        b1_t = emit_bias_cols(c, 0, b1c, 32)
        b2_t = emit_bias_cols(c, 32, b2c, 8)
        x_t, _ = emit_load_x_transposed(c, x, const_t)
        r1_t = emit_gamma(c, w1t, D * 4096, 0)
        r2_t = emit_gamma(c, w2t, D * 4096, 1, wbuf_free_t=r1_t)
        ln_t, _ = emit_ln(c, x_t)
        ffn_t, _, _, _ = emit_l1ffn(c, ln_t, w1t, w2t, r1_t, r2_t, b2_t,
                                    wbuf_free_t=r2_t, wbuf_free_pe_t=None)
        st = emit_store_x_tokenmajor(c, y, ffn_t)
        emit_wait_store(c, st)
    return nc


def kernel(**inputs):
    raise NotImplementedError("dev version")



def emit_moe_cw(c, xln_t, wrg_dma_t, xln_ch=None):
    pe0, a0, v0 = c.pe.n, c.sA.n, c.sV.n
    log_mm = {tt: (c.pe, pe0 + 2 * tt + 1) for tt in range(NT)}
    tr_mm = {tt: (c.pe, pe0 + 2 * tt + 2) for tt in range(NT)}
    exp_t = {}
    a = a0
    evac_t = {}
    for tt in range(NT):
        a += 2
        exp_t[tt] = (c.sA, a)
        if tt % 4 == 3:
            a += 1
            evac_t[tt // 4] = (c.sA, a)
    cw_t = {tt: (c.sV, v0 + 5 * (tt + 1)) for tt in range(NT)}

    lbank0 = [c.pb[4].free_t, c.pb[5].free_t]
    tbank0 = [c.pb[0].free_t, c.pb[1].free_t]

    @c.block.tensor
    def _(t):
        wt(t, wrg_dma_t)
        if xln_ch is None:
            wt(t, xln_t)
        for tt in range(NT):
            if xln_ch and tt % 4 == 0:
                wt(t, xln_ch[tt // 4])
            lb = c.pb[4 + tt % 2]
            if tt < 2:
                wt(t, lbank0[tt % 2])
            else:
                wt(t, exp_t[tt - 2])
            for k in range(ND):
                t.matmul(lb.t[0:128, 0:8], c.xln[:, T * k + 128 * tt:T * k + 128 * (tt + 1)],
                         c.moe_w[:, 8 * k:8 * (k + 1)], start=(k == 0), stop=False)
            for k in range(ND):
                ins = t.matmul(lb.t[0:128, 8:16], c.xln[:, T * k + 128 * tt:T * k + 128 * (tt + 1)],
                               c.moe_w[:, 64 + 8 * k:64 + 8 * (k + 1)],
                               start=False, stop=(k == ND - 1))
            c.pe.inc(ins)
            tb = c.pb[(tt // 4) % 2]
            wt(t, cw_t[tt])
            if tt < 8 and tt % 4 == 0:
                wt(t, tbank0[(tt // 4) % 2])
            if tt >= 8 and tt % 4 == 0:
                wt(t, evac_t[tt // 4 - 2])
            c.pe.inc(t.matmul(tb.t[0:8, 128 * (tt % 4):128 * (tt % 4 + 1)],
                              c.cwt[:, 16 + 8 * (tt % 2):16 + 8 * (tt % 2) + 8],
                              c.id_f32[:], start=True, stop=True))

    @c.block.scalar
    def _(a_):
        for tt in range(NT):
            lb = c.pb[4 + tt % 2]
            wt(a_, log_mm[tt])
            if tt >= 1:
                wt(a_, (c.sV, v0 + 5 * (tt - 1) + 2))
            c.sA.inc(a_.activation(out=c.cwt[:, 0:8], in_=lb.t[0:128, 0:8], func=AF.Exp,
                                   accum_out=c.cws[:, 0:1]))
            c.sA.inc(a_.activation(out=c.cwt[:, 8:16], in_=lb.t[0:128, 8:16], func=AF.Exp,
                                   scale=10.0, accum_out=c.cws[:, 1:2]))
            if tt % 4 == 3:
                g = tt // 4
                tb = c.pb[g % 2]
                wt(a_, tr_mm[tt])
                c.sA.inc(a_.copy(out=c.cw8[:, 512 * g:512 * (g + 1)], in_=tb.t[0:8, :]))

    @c.block.vector
    def _(v):
        for tt in range(NT):
            wt(v, exp_t[tt])
            if tt >= 2:
                wt(v, tr_mm[tt - 2])
            c.sV.inc(v.scalar_tensor_tensor(out=c.cwt[:, 16 + 8 * (tt % 2):16 + 8 * (tt % 2) + 8],
                                            in0=c.cwt[:, 0:8], scalar=1.0, in1=c.cwt[:, 8:16],
                                            op0=ALU.mult, op1=ALU.mult,
                                            accum_out=c.cws[:, 2:3]))
            c.sV.inc(v.scalar_tensor_tensor(out=c.cws[:, 3:4], in0=c.cws[:, 0:1],
                                            scalar=1e-9, in1=c.cws[:, 1:2],
                                            op0=ALU.mult, op1=ALU.mult))
            v.wait_ge(c.sV.h, c.sV.n)
            c.sV.inc(v.tensor_tensor(out=c.cws[:, 4:5], in0=c.cws[:, 2:3],
                                     in1=c.cws[:, 3:4], op=ALU.add))
            v.wait_ge(c.sV.h, c.sV.n)
            c.sV.inc(v.reciprocal(out=c.cws[:, 5:6], in_=c.cws[:, 4:5]))
            v.wait_ge(c.sV.h, c.sV.n)
            sl = c.cwt[:, 16 + 8 * (tt % 2):16 + 8 * (tt % 2) + 8]
            c.sV.inc(v.tensor_scalar(out=sl, in0=sl, scalar1=c.cws[:, 5:6],
                                     scalar2=None, op0=ALU.mult))

    for tt in range(NT):
        lb = c.pb[4 + tt % 2]
        lb.mm_t = log_mm[tt]
        lb.free_t = exp_t[tt]
    for g in range(4):
        tb = c.pb[g % 2]
        tb.mm_t = tr_mm[4 * g + 3]
        tb.free_t = evac_t[g]
    return evac_t[3]


def emit_l0ffn(c, xln_t, w_gate, w_up, w_down, cw8_t, arena_free_pe_t,
               wbuf_free_t, wbuf_free_pe_t, experts=range(8), xln_ch=None):
    wb = c.wbuf
    h = c.arenaA
    h_free_t = arena_free_pe_t
    cwrow_free_t = None
    cwB_free = [None, None]
    xt_add_t = {}
    g_slot_pe = [None, None]
    u_slot_pe = [None, None]
    d_slot_pe = None

    first = True
    seq = [(hf, ee) for hf in range(2) for ee in experts]

    def build_cwB(half_b, e_b, crow_wait):
        par_b = e_b % 2
        hs0b = 1024 * half_b

        @c.block.sync
        def _(s, e_b=e_b, hs0b=hs0b, crow_wait=crow_wait):
            wt(s, crow_wait, cw8_t)
            c.dx[7].inc(s.dma_start(out=c.cwrow[:, :],
                                    in_=c.cw8[e_b:e_b + 1, hs0b:hs0b + 1024]), 16)
        crow_t = c.dx[7].t

        bank = c.pb[5]
        pw = bank.free_t
        a0c = c.sA.n
        cwb_mm = []

        @c.block.tensor
        def _(t, crow_t=crow_t, pw=pw, cwB_free_par=cwB_free[par_b], a0c=a0c):
            wt(t, crow_t, pw, cwB_free_par)
            for cc in range(2):
                if cc == 1:
                    wt(t, (c.sA, a0c + 1))
                c.pe.inc(t.matmul(bank.t[:, :], c.ones1[:, :],
                                  c.cwrow[0:1, 512 * cc:512 * (cc + 1)],
                                  start=True, stop=True))
                cwb_mm.append(c.pe.t)
        bank.mm_t = cwb_mm[1]

        @c.block.scalar
        def _(a, par_b=par_b):
            for cc in range(2):
                wt(a, cwb_mm[cc])
                c.sA.inc(a.copy(out=c.cwB[:, 1024 * par_b + 512 * cc:1024 * par_b + 512 * (cc + 1)],
                                in_=bank.t[:, :]))
        bank.free_t = c.sA.t
        return c.sA.t, cwb_mm[1]

    cwB_pre, cwrow_free_t = build_cwB(seq[0][0], seq[0][1], None)
    for si, (half, e) in enumerate(seq):
            hs0 = 1024 * half
            par = e % 2
            cwB_t = cwB_pre
            v0, pe0, a0 = c.sV.n, c.pe.n, c.sA.n
            dwg0 = [c.dw[0].n, c.dw[1].n]
            dwu0 = [c.dw[2].n, c.dw[3].n]
            gdma_t = {}
            udma_t = {}
            for m in range(32):
                gdma_t[m] = (c.dw[m % 2], dwg0[m % 2] + 16 * (m // 2 + 1))
                udma_t[m] = (c.dw[2 + m % 2], dwu0[m % 2] + 16 * (m // 2 + 1))
            gmm = {(m, c2): (c.pe, pe0 + 4 * m + 2 * c2 + 1) for m in range(32) for c2 in range(2)}
            umm = {(m, c2): (c.pe, pe0 + 4 * m + 2 * c2 + 2) for m in range(32) for c2 in range(2)}
            silu_t = {(m, c2): (c.sA, a0 + 2 * m + c2 + 1) for m in range(32) for c2 in range(2)}
            hmul_t = {(m, c2): (c.sV, v0 + 2 * m + c2 + 1) for m in range(32) for c2 in range(2)}
            gslot_pe0 = list(g_slot_pe)
            uslot_pe0 = list(u_slot_pe)
            bank_free0 = [c.pb[i].free_t for i in range(4)]

            @c.block.gpsimd
            def _(g, e=e, gslot_pe0=gslot_pe0, uslot_pe0=uslot_pe0,
                  wbuf_free_t=wbuf_free_t, wbuf_free_pe_t=wbuf_free_pe_t, first=first):
                for m in range(32):
                    if m >= 2:
                        wt(g, gmm[(m - 2, 1)])
                    else:
                        wt(g, gslot_pe0[m % 2], wbuf_free_pe_t)
                        if first:
                            wt(g, wbuf_free_t)
                    c.dw[m % 2].inc(g.dma_start(
                        out=wb[:, 1024 * (m % 2):1024 * (m % 2 + 1)],
                        in_=w_gate[e, :, 128 * m:128 * (m + 1)]
                        .rearrange("(k p) f -> p k f", p=128)), 16)
                    if m >= 2:
                        wt(g, umm[(m - 2, 1)])
                    else:
                        wt(g, uslot_pe0[m % 2])
                    c.dw[2 + m % 2].inc(g.dma_start(
                        out=wb[:, 2048 + 1024 * (m % 2):2048 + 1024 * (m % 2 + 1)],
                        in_=w_up[e, :, 128 * m:128 * (m + 1)]
                        .rearrange("(k p) f -> p k f", p=128)), 16)

            @c.block.tensor
            def _(t, hs0=hs0, bank_free0=bank_free0, first=first, xln_t=xln_t, half=half):
                if first:
                    if xln_ch:
                        wt(t, xln_ch[2 * half], xln_ch[2 * half + 1])
                    else:
                        wt(t, xln_t)
                for m in range(32):
                    gsl = wb[:, 1024 * (m % 2):1024 * (m % 2 + 1)]
                    usl = wb[:, 2048 + 1024 * (m % 2):2048 + 1024 * (m % 2 + 1)]
                    wt(t, gdma_t[m], udma_t[m])
                    for c2 in range(2):
                        i = 4 * m + 2 * c2
                        gb = c.pb[(2 * m + c2) % 2]
                        ub = c.pb[2 + (2 * m + c2) % 2]
                        if (2 * m + c2) < 2:
                            wt(t, bank_free0[(2 * m + c2) % 2])
                        else:
                            wt(t, silu_t[((2 * m + c2 - 2) // 2, (2 * m + c2 - 2) % 2)])
                        for k in range(ND):
                            ins = t.matmul(gb.t[:, :], gsl[:, 128 * k:128 * (k + 1)],
                                           c.xln[:, T * k + hs0 + 512 * c2:T * k + hs0 + 512 * (c2 + 1)],
                                           start=(k == 0), stop=(k == ND - 1))
                        c.pe.inc(ins)
                        if (2 * m + c2) < 2:
                            wt(t, bank_free0[2 + (2 * m + c2) % 2])
                        else:
                            wt(t, hmul_t[((2 * m + c2 - 2) // 2, (2 * m + c2 - 2) % 2)])
                        for k in range(ND):
                            ins = t.matmul(ub.t[:, :], usl[:, 128 * k:128 * (k + 1)],
                                           c.xln[:, T * k + hs0 + 512 * c2:T * k + hs0 + 512 * (c2 + 1)],
                                           start=(k == 0), stop=(k == ND - 1))
                        c.pe.inc(ins)

            @c.block.scalar
            def _(a_, tmp_free_t=getattr(c, '_moe_tmp_free', None)):
                for m in range(32):
                    for c2 in range(2):
                        gb = c.pb[(2 * m + c2) % 2]
                        wt(a_, gmm[(m, c2)])
                        if 2 * m + c2 >= 2:
                            wt(a_, hmul_t[((2 * m + c2 - 2) // 2, (2 * m + c2 - 2) % 2)])
                        else:
                            wt(a_, tmp_free_t)
                        tdst = c.tmp0 if (2 * m + c2) % 2 == 0 else c.tmp1
                        c.sA.inc(a_.activation(out=tdst[:, :], in_=gb.t[:, :], func=AF.Silu))

            @c.block.vector
            def _(v, h_free_t=h_free_t):
                wt(v, h_free_t)
                for m in range(32):
                    for c2 in range(2):
                        ub = c.pb[2 + (2 * m + c2) % 2]
                        wt(v, silu_t[(m, c2)], umm[(m, c2)])
                        tsrc = c.tmp0 if (2 * m + c2) % 2 == 0 else c.tmp1
                        c.sV.inc(v.tensor_tensor(
                            out=h[:, 1024 * m + 512 * c2:1024 * m + 512 * (c2 + 1)],
                            in0=tsrc[:, :], in1=ub.t[:, :], op=ALU.mult))
            for m in range(32):
                for c2 in range(2):
                    gb = c.pb[(2 * m + c2) % 2]
                    ub = c.pb[2 + (2 * m + c2) % 2]
                    gb.mm_t = gmm[(m, c2)]
                    gb.free_t = silu_t[(m, c2)]
                    ub.mm_t = umm[(m, c2)]
                    ub.free_t = hmul_t[(m, c2)]
            h_t = hmul_t[(31, 1)]
            g_slot_pe = [gmm[(30, 1)], gmm[(31, 1)]]
            u_slot_pe = [umm[(30, 1)], umm[(31, 1)]]
            wbuf_free_t = None
            wbuf_free_pe_t = None

            if si + 1 < len(seq):
                cwB_pre, cwrow_free_t = build_cwB(seq[si + 1][0], seq[si + 1][1],
                                                  cwrow_free_t)

            v0b, pe0b = c.sV.n, c.pe.n
            dsub = [c.dw[4], c.dw[10], c.dw[11], c.dw[12]]
            dsub0 = [x.n for x in dsub]
            ddma_t = {(m, j): (dsub[j], dsub0[j] + 16 * (m + 1))
                      for m in range(8) for j in range(4)}
            qtr = {(m, c2, j): (c.pe, pe0b + 8 * m + 4 * c2 + j + 1)
                   for m in range(8) for c2 in range(2) for j in range(4)}
            dmm = {(m, c2): qtr[(m, c2, 3)] for m in range(8) for c2 in range(2)}
            dmul_t = {(m, c2): (c.sV, v0b + 4 * m + 2 * c2 + 1) for m in range(8) for c2 in range(2)}
            dadd_t = {(m, c2): (c.sV, v0b + 4 * m + 2 * c2 + 2) for m in range(8) for c2 in range(2)}
            dslot0 = d_slot_pe
            bank45_free0 = [c.pb[4].free_t, c.pb[5].free_t]

            @c.block.gpsimd
            def _(g, e=e, dslot0=dslot0):
                for m in range(8):
                    for j in range(4):
                        if m >= 1:
                            wt(g, qtr[(m - 1, 1, j)])
                        elif j == 0:
                            wt(g, dslot0)
                        dsub[j].inc(g.dma_start(
                            out=wb[:, 4096 + 1024 * j:4096 + 1024 * (j + 1)],
                            in_=w_down[e, 1024 * j:1024 * (j + 1), 128 * m:128 * (m + 1)]
                            .rearrange("(k p) f -> p k f", p=128)), 16)

            @c.block.tensor
            def _(t, hs0=hs0, h_t=h_t, bank45_free0=bank45_free0):
                wt(t, h_t)
                for m in range(8):
                    for c2 in range(2):
                        i = 2 * m + c2
                        db = c.pb[4 + c2]
                        wt(t, bank45_free0[c2] if i < 2 else dmul_t[(m - 1, c2)])
                        for j in range(4):
                            if c2 == 0:
                                wt(t, ddma_t[(m, j)])
                            for kk in range(8 * j, 8 * (j + 1)):
                                ins = t.matmul(
                                    db.t[:, :],
                                    wb[:, 4096 + 128 * kk:4096 + 128 * (kk + 1)],
                                    h[:, 1024 * kk + 512 * c2:1024 * kk + 512 * (c2 + 1)],
                                    start=(kk == 0), stop=(kk == 31))
                            c.pe.inc(ins)

            @c.block.vector
            def _(v, hs0=hs0, half=half, cwB_t=cwB_t, par=par):
                wt(v, cwB_t)
                for m in range(8):
                    for c2 in range(2):
                        db = c.pb[4 + c2]
                        wt(v, dmm[(m, c2)])
                        tdst = c.tmp0 if c2 == 0 else c.tmp1
                        c.sV.inc(v.tensor_tensor(out=tdst[:, :], in0=db.t[:, :],
                                                 in1=c.cwB[:, 1024 * par + 512 * c2:1024 * par + 512 * (c2 + 1)],
                                                 op=ALU.mult))
                        v.wait_ge(c.sV.h, c.sV.n)
                        prev = xt_add_t.get((m, half, c2))
                        wt(v, prev)
                        dst = c.xT[:, T * m + hs0 + 512 * c2:T * m + hs0 + 512 * (c2 + 1)]
                        c.sV.inc(v.tensor_tensor(out=dst, in0=dst, in1=tdst[:, :], op=ALU.add))
                        xt_add_t[(m, half, c2)] = (c.sV, c.sV.n)
            for m in range(8):
                for c2 in range(2):
                    db = c.pb[4 + c2]
                    db.mm_t = dmm[(m, c2)]
                    db.free_t = dmul_t[(m, c2)]
            cwB_free[par] = dmul_t[(7, 1)]
            h_free_t = dmm[(7, 1)]
            d_slot_pe = dmm[(7, 1)]
            c._moe_tmp_free = dadd_t[(7, 1)]
            first = False
    last_add = max(v[1] for v in xt_add_t.values())
    moe_ch = [xt_add_t[(7, ch // 2, ch % 2)] for ch in range(4)]
    return (c.sV, last_add), d_slot_pe, moe_ch


def emit_moe_wrg(c, wrg_hbm):
    @c.block.gpsimd
    def _(g):
        c.dw[5].inc(g.dma_start(out=c.moe_w[:, :], in_=wrg_hbm), 16)
    return c.dw[5].t


def build_l0ffn_test(experts=tuple(range(8))):
    nc = bass.Bass("TRN2", target_bir_lowering=False)
    x = nc.dram_tensor("x", [T, D], F32, kind="ExternalInput").ap()
    wrg = nc.dram_tensor("wrg", [128, 128], F32, kind="ExternalInput").ap()
    w_gate = nc.dram_tensor("w_gate", [8, D, 4096], F32, kind="ExternalInput").ap()
    w_up = nc.dram_tensor("w_up", [8, D, 4096], F32, kind="ExternalInput").ap()
    w_down = nc.dram_tensor("w_down", [8, 4096, D], F32, kind="ExternalInput").ap()
    y = nc.dram_tensor("y", [T, D], F32, kind="ExternalOutput").ap()
    with ExitStack() as stack:
        block = stack.enter_context(nc.Block())
        c = make_ctx(stack, nc, block)
        const_t = emit_consts(c)
        wrg_t = emit_moe_wrg(c, wrg)
        x_t, _ = emit_load_x_transposed(c, x, const_t)
        ln_t, _ = emit_ln(c, x_t)
        cw8_t = emit_moe_cw(c, ln_t, wrg_t)
        ffn_t, _, _ = emit_l0ffn(c, ln_t, w_gate, w_up, w_down, cw8_t,
                                 arena_free_pe_t=None, wbuf_free_t=None,
                                 wbuf_free_pe_t=None, experts=experts)
        st = emit_store_x_tokenmajor(c, y, ffn_t)
        emit_wait_store(c, st)
    return nc



def emit_l1attn(c, xln_t, wqkvT, woutT, xt_prev_t, wbuf_free_pe_t=None,
                scalar_hook=None, xln_ch=None):
    A = c.arenaA
    ctxT = A[:, 0:16384]
    qT = A[:, 16384:18432]
    kT = A[:, 18432:20480]
    vtok = A[:, 20480:22528]
    attnb = [[A[:, 22528:24576], A[:, 28672:30720]],
             [A[:, 26624:28672], A[:, 30720:32768]]]
    stg = [A[:, 24576:25600], A[:, 25600:26624]]
    rs_col = [c.cws[:, 6:7], c.cws[:, 4:5]]
    rs2_col = [c.cws[:, 2:3], c.cws[:, 0:1]]
    rc_col = [c.cws[:, 7:8], c.cws[:, 5:6]]

    last_exp = [c.pb[0].free_t, c.pb[2].free_t]
    pb4_free = c.pb[4].free_t
    pt_free = [c.pt[0].free_t, c.pt[1].free_t]
    stg_free = [None, None]
    attn_war = [[None, None], [None, None]]
    rs_war = [None, None]
    wslot_pe = [None, None]
    prev_core_end = None
    ctx_last = None
    gidx = 0

    for hp in range(8):
        par = hp % 2
        wq = c.wbuf[:, 3072 * par:3072 * par + 1024]
        wk = c.wbuf[:, 3072 * par + 1024:3072 * par + 2048]
        wv = c.wbuf[:, 3072 * par + 2048:3072 * par + 3072]
        dq, dk, dv = c.dw[0 + par], c.dw[2 + par], c.dw[6 + par]
        wdma = [(dq, dq.n + 16), (dk, dk.n + 16), (dv, dv.n + 16)]
        w_reuse = wslot_pe[par] if wslot_pe[par] is not None else wbuf_free_pe_t

        pe_n, a_n, v_n = c.pe.n, c.sA.n, c.sV.n
        qkv_mm, qkv_mm_w = {}, {}
        qkv_ev, qkv_ev_w = {}, {}
        bank_free = {0: last_exp[0], 1: last_exp[0], 2: last_exp[1], 3: last_exp[1]}
        for gi in range(24):
            b = gi % 4
            qkv_mm_w[gi] = [bank_free[b]] + ([wdma[0], wdma[1], wdma[2]] if gi == 0 else [])
            if hp == 0 and xln_ch:
                if gi < 8:
                    qkv_mm_w[gi].append(xln_ch[gi % 4])
                elif (gi - 8) % 4 == 0:
                    qkv_mm_w[gi].append(xln_ch[(gi - 8) // 4])
            pe_n += 1
            qkv_mm[gi] = (c.pe, pe_n)
            qkv_ev_w[gi] = [qkv_mm[gi]] + ([prev_core_end] if gi == 0 else [])
            a_n += 1
            qkv_ev[gi] = (c.sA, a_n)
            bank_free[b] = qkv_ev[gi]
        last_exp = [qkv_ev[20], qkv_ev[23]]

        def passes(qt):
            kext = 128 * (qt + 1)
            if kext <= 1024:
                return [(0, kext)]
            return [(0, 1024), (1024, kext)]
        sc_mm, sc_w = {}, {}
        exp_t, exp_w = {}, {}
        nrm_t = {}
        tr_mm, tr_w = {}, {}
        trev_t, trev_w = {}, {}
        avg_mm, avg_w = {}, {}
        avev_t, avev_w = {}, {}
        ktg = {}

        def plan_ktwork(qt):
            nonlocal pe_n, a_n, v_n, gidx, pb4_free
            units = []
            for hloc in range(2):
                kts = list(range(qt + 1))
                groups = [kts[i:i + 8] for i in range(0, len(kts), 8)]
                ktg[(hloc, qt)] = []
                for gl in groups:
                    slot = gidx % 2
                    gidx += 1
                    ktg[(hloc, qt)].append((gl, slot))
                    units.append((hloc, gl, slot))
            n_u = len(units)
            tr_tick, av_tick = {}, {}
            for i2 in range(n_u):
                pe_n += 1
                tr_tick[i2] = (c.pe, pe_n)
                if i2 >= 2:
                    pe_n += 1
                    av_tick[i2 - 2] = (c.pe, pe_n)
            for i2 in (n_u - 2, n_u - 1):
                if i2 >= 0 and i2 not in av_tick:
                    pe_n += 1
                    av_tick[i2] = (c.pe, pe_n)
            ev_tick = {}
            for i2 in range(n_u):
                v_n += 1
                ev_tick[i2] = (c.sV, v_n)
            for i2, (hloc, gl, slot) in enumerate(units):
                tr_mm[(hloc, qt, gl[0])] = tr_tick[i2]
                trev_t[(hloc, qt, gl[0])] = ev_tick[i2]
                avg_mm[(hloc, qt, gl[0])] = av_tick[i2]
                tr_w[(hloc, qt, gl[0])] = [nrm_t[(hloc, qt)] if gl[0] == 0 else None,
                                           (ev_tick[i2 - 2] if i2 >= 2 else pt_free[slot])]
                trev_w[(hloc, qt, gl[0])] = [tr_tick[i2],
                                             (av_tick[i2 - 2] if i2 >= 2 else stg_free[slot])]
                avg_w[(hloc, qt, gl[0])] = [ev_tick[i2]] + \
                    ([pb4_free] if gl[0] == 0 else [])
                if gl[0] + 8 > qt:
                    avev_w[(hloc, qt)] = [av_tick[i2]]
                    a_n += 1
                    avev_t[(hloc, qt)] = (c.sA, a_n)
                    pb4_free = avev_t[(hloc, qt)]
            for i2 in range(max(0, n_u - 2), n_u):
                _, _, slot = units[i2]
                pt_free[slot] = ev_tick[i2]
                stg_free[slot] = av_tick[i2]
            if n_u == 1:
                pt_free[units[0][2]] = ev_tick[0]
                stg_free[units[0][2]] = av_tick[0]
            for hloc in range(2):
                attn_war[hloc][qt % 2] = tr_mm[(hloc, qt, ktg[(hloc, qt)][-1][0][0])]

        qt_order = list(reversed(range(NT)))
        first_qt = qt_order[0]
        for qi, qt in enumerate(qt_order):
            for pi, (k0, k1) in enumerate(passes(qt)):
                for hloc in range(2):
                    sc_w[(hloc, qt, pi)] = [last_exp[hloc]] + \
                        ([qkv_ev[7], qkv_ev[23]]
                         if (qt == first_qt and pi == 0 and hloc == 0) else [])
                    pe_n += 1
                    sc_mm[(hloc, qt, pi)] = (c.pe, pe_n)
                for hloc in range(2):
                    exp_w[(hloc, qt, pi)] = [sc_mm[(hloc, qt, pi)], rs_war[hloc],
                                             attn_war[hloc][qt % 2] if pi == 0 else None]
                    a_n += 1
                    exp_t[(hloc, qt, pi)] = (c.sA, a_n)
                    last_exp[hloc] = exp_t[(hloc, qt, pi)]
            if qi > 0:
                plan_ktwork(qt_order[qi - 1])
            npass = len(passes(qt))
            for hloc in range(2):
                v_n += (1 if npass == 2 else 0) + 2
                nrm_t[(hloc, qt)] = (c.sV, v_n)
                rs_war[hloc] = (c.sV, v_n)
        plan_ktwork(qt_order[-1])
        prev_core_end = avg_mm[(1, 0, ktg[(1, 0)][-1][0][0])]
        wslot_pe[par] = qkv_mm[23]
        ctx_last = avev_t[(1, 0)]

        @c.block.gpsimd
        def _(g, hp=hp, wq=wq, wk=wk, wv=wv, w_reuse=w_reuse, dq=dq, dk=dk, dv=dv):
            wt(g, w_reuse)
            dq.inc(g.dma_start(out=wq, in_=wqkvT[:, 128 * hp:128 * (hp + 1)]
                               .rearrange("(k p) f -> p k f", p=128)), 16)
            dk.inc(g.dma_start(out=wk, in_=wqkvT[:, 1024 + 128 * hp:1024 + 128 * (hp + 1)]
                               .rearrange("(k p) f -> p k f", p=128)), 16)
            dv.inc(g.dma_start(out=wv, in_=wqkvT[:, 2048 + 128 * hp:2048 + 128 * (hp + 1)]
                               .rearrange("(k p) f -> p k f", p=128)), 16)

        @c.block.tensor
        def _(t, hp=hp, wq=wq, wk=wk, wv=wv, qkv_mm_w=qkv_mm_w, sc_w=sc_w,
              tr_w=tr_w, avg_w=avg_w, ktg=ktg, xln_t=xln_t):
            if hp == 0 and xln_ch is None:
                wt(t, xln_t)
            gi = 0
            for which, wsl in ((0, wq), (1, wk)):
                for ch in range(4):
                    bank = c.pb[ch]
                    wt(t, *qkv_mm_w[gi])
                    for k in range(ND):
                        ins = t.matmul(bank.t[:, :], wsl[:, 128 * k:128 * (k + 1)],
                                       c.xln[:, T * k + 512 * ch:T * k + 512 * (ch + 1)],
                                       start=(k == 0), stop=(k == ND - 1))
                    c.pe.inc(ins)
                    gi += 1
            for ti in range(NT):
                bank = c.pb[ti % 4]
                wt(t, *qkv_mm_w[gi])
                for k in range(ND):
                    ins = t.matmul(bank.t[:, 0:128],
                                   c.xln[:, T * k + 128 * ti:T * k + 128 * (ti + 1)],
                                   wv[:, 128 * k:128 * (k + 1)],
                                   start=(k == 0), stop=(k == ND - 1))
                c.pe.inc(ins)
                gi += 1

            def emit_ktwork(qt):
                kext = 128 * (qt + 1)
                units = [(hloc, gl, slot) for hloc in range(2)
                         for gl, slot in ktg[(hloc, qt)]]

                def emit_tr(u):
                    hloc, gl, slot = u
                    wt(t, *tr_w[(hloc, qt, gl[0])])
                    for j, kt in enumerate(gl):
                        ins = t.transpose(c.pt[slot].t[:, 128 * j:128 * (j + 1)],
                                          attnb[hloc][qt % 2][:, 128 * kt:128 * (kt + 1)],
                                          c.id_bf[:])
                    c.pe.inc(ins)

                def emit_av(u):
                    hloc, gl, slot = u
                    h0 = 64 * hloc
                    wt(t, *avg_w[(hloc, qt, gl[0])])
                    for j, kt in enumerate(gl):
                        ins = t.matmul(c.pb[4].t[h0:h0 + 64, 0:128],
                                       vtok[:, 128 * kt + h0:128 * kt + h0 + 64],
                                       stg[slot][:, 128 * j:128 * (j + 1)],
                                       start=(kt == 0), stop=(kt == qt),
                                       tile_position=(0, h0), skip_group_check=True)
                    c.pe.inc(ins)

                for gi2, u in enumerate(units):
                    emit_tr(u)
                    if gi2 >= 2:
                        emit_av(units[gi2 - 2])
                if len(units) >= 2:
                    emit_av(units[-2])
                emit_av(units[-1])

            for qi, qt in enumerate(reversed(range(NT))):
                kext = 128 * (qt + 1)
                for pi, (k0, k1) in enumerate(passes(qt)):
                    for hloc in range(2):
                        h0 = 64 * hloc
                        wt(t, *sc_w[(hloc, qt, pi)])
                        nch = (k1 - k0 + 511) // 512
                        for ci in range(nch):
                            cs = k0 + 512 * ci
                            n = min(512, k1 - cs)
                            has_diag = cs <= kext - 128 < cs + n
                            ins = t.matmul(c.pbig[:, 1024 * hloc + 512 * ci:1024 * hloc + 512 * ci + n],
                                           qT[h0:h0 + 64, 128 * qt:128 * (qt + 1)],
                                           kT[h0:h0 + 64, cs:cs + n],
                                           start=True, stop=not has_diag,
                                           tile_position=(h0, 0), skip_group_check=True)
                            if has_diag:
                                ins = t.matmul(
                                    c.pbig[:, 1024 * hloc + 512 * ci + (kext - 128 - cs):
                                           1024 * hloc + 512 * ci + (kext - 128 - cs) + 128],
                                    c.id_bf[:], c.mask_bf[:],
                                    start=False, stop=True, skip_group_check=True)
                        c.pe.inc(ins)
                if qi > 0:
                    emit_ktwork(qt + 1)
            emit_ktwork(0)

        @c.block.scalar
        def _(a, hp=hp, qkv_ev_w=qkv_ev_w, exp_w=exp_w, avev_w=avev_w, ktg=ktg):
            gi = 0
            for which in range(2):
                for ch in range(4):
                    bank = c.pb[ch]
                    wt(a, *qkv_ev_w[gi])
                    dst = qT if which == 0 else kT
                    if which == 0:
                        c.sA.inc(a.mul(out=dst[:, 512 * ch:512 * (ch + 1)],
                                       in_=bank.t[:, :], mul=0.125))
                    else:
                        c.sA.inc(a.copy(out=dst[:, 512 * ch:512 * (ch + 1)],
                                        in_=bank.t[:, :]))
                    gi += 1
            for ti in range(NT):
                bank = c.pb[ti % 4]
                wt(a, *qkv_ev_w[gi])
                c.sA.inc(a.copy(out=vtok[:, 128 * ti:128 * (ti + 1)],
                                in_=bank.t[:, 0:128]))
                gi += 1
            def emit_avev(qt):
                for hloc in range(2):
                    h0 = 64 * hloc
                    wt(a, *avev_w[(hloc, qt)])
                    c.sA.inc(a.copy(
                        out=A[h0:h0 + 64, 2048 * hp + 128 * qt:2048 * hp + 128 * (qt + 1)],
                        in_=c.pb[4].t[h0:h0 + 64, 0:128]))

            for qi, qt in enumerate(reversed(range(NT))):
                for pi, (k0, k1) in enumerate(passes(qt)):
                    for hloc in range(2):
                        wt(a, *exp_w[(hloc, qt, pi)])
                        c.sA.inc(a.activation(
                            out=attnb[hloc][qt % 2][:, k0:k1],
                            in_=c.pbig[:, 1024 * hloc:1024 * hloc + (k1 - k0)],
                            func=AF.Exp,
                            accum_out=(rs_col[hloc] if pi == 0 else rs2_col[hloc])))
                if qi > 0:
                    emit_avev(qt + 1)
                if scalar_hook is not None:
                    scalar_hook(a, hp, 15 - qt)
            emit_avev(0)

        @c.block.vector
        def _(v, hp=hp, exp_t=exp_t, nrm_t=nrm_t, trev_w=trev_w, ktg=ktg):
            def emit_ktev(qt):
                for hloc in range(2):
                    for gl, slot in ktg[(hloc, qt)]:
                        wt(v, *trev_w[(hloc, qt, gl[0])])
                        c.sV.inc(v.tensor_copy(out=stg[slot][:, 0:128 * len(gl)],
                                               in_=c.pt[slot].t[:, 0:128 * len(gl)]))

            for qi, qt in enumerate(reversed(range(NT))):
                kext = 128 * (qt + 1)
                npass = len(passes(qt))
                if qi > 0:
                    emit_ktev(qt + 1)
                for hloc in range(2):
                    wt(v, exp_t[(hloc, qt, npass - 1)])
                    if npass == 2:
                        c.sV.inc(v.tensor_tensor(out=rs_col[hloc], in0=rs_col[hloc],
                                                 in1=rs2_col[hloc], op=ALU.add))
                        v.wait_ge(c.sV.h, c.sV.n)
                    c.sV.inc(v.reciprocal(out=rc_col[hloc], in_=rs_col[hloc]))
                    v.wait_ge(c.sV.h, c.sV.n)
                    c.sV.inc(v.tensor_scalar(out=attnb[hloc][qt % 2][:, 0:kext],
                                             in0=attnb[hloc][qt % 2][:, 0:kext],
                                             scalar1=rc_col[hloc], scalar2=None,
                                             op0=ALU.mult))
            emit_ktev(0)

    for i in range(4):
        c.pb[i].free_t = last_exp[i // 2]
    c.pb[4].free_t = pb4_free
    c.pt[0].free_t = pt_free[0]
    c.pt[1].free_t = pt_free[1]

    v0 = c.sV.n
    pe0 = c.pe.n
    wo_sem = [c.dw[8], c.dw[9]]
    wo_n0 = [c.dw[8].n, c.dw[9].n]
    wodma = {j: (wo_sem[j % 2], wo_n0[j % 2] + 16 * (j // 2 + 1)) for j in range(8)}
    womm = {}
    pe_n = pe0
    wo_w = {}
    bank_free = [c.pb[i].free_t for i in range(4)]
    woev_t = {(j, ch): (c.sV, v0 + 4 * j + ch + 1) for j in range(8) for ch in range(4)}
    for j in range(8):
        for ch in range(4):
            u = 4 * j + ch
            wo_w[(j, ch)] = [bank_free[ch] if u < 4 else woev_t[(j - 1, ch)]]
            pe_n += 1
            womm[(j, ch)] = (c.pe, pe_n)

    @c.block.gpsimd
    def _(g, wsl0=wslot_pe[0]):
        for j in range(8):
            if j >= 2:
                wt(g, womm[(j - 2, 3)])
            else:
                wt(g, wsl0)
            wo_sem[j % 2].inc(g.dma_start(
                out=c.wbuf[:, 1024 * (j % 2):1024 * (j % 2 + 1)],
                in_=woutT[:, 128 * j:128 * (j + 1)]
                .rearrange("(k p) f -> p k f", p=128)), 16)

    @c.block.tensor
    def _(t, ctx_last=ctx_last):
        wt(t, ctx_last)
        for j in range(8):
            wsl = c.wbuf[:, 1024 * (j % 2):1024 * (j % 2 + 1)]
            wt(t, wodma[j])
            for ch in range(4):
                wt(t, *wo_w[(j, ch)])
                for e in range(8):
                    ins = t.matmul(c.pb[ch].t[:, :], wsl[:, 128 * e:128 * (e + 1)],
                                   ctxT[:, 2048 * e + 512 * ch:2048 * e + 512 * (ch + 1)],
                                   start=(e == 0), stop=(e == 7))
                c.pe.inc(ins)

    @c.block.vector
    def _(v, xt_prev_t=xt_prev_t):
        wt(v, xt_prev_t)
        for j in range(8):
            for ch in range(4):
                wt(v, womm[(j, ch)])
                dst = c.xT[:, T * j + 512 * ch:T * j + 512 * (ch + 1)]
                c.sV.inc(v.tensor_tensor(out=dst, in0=dst, in1=c.pb[ch].t[:, :],
                                         op=ALU.add))
    for j in range(8):
        for ch in range(4):
            c.pb[ch].mm_t = womm[(j, ch)]
            c.pb[ch].free_t = woev_t[(j, ch)]
    return c.sV.t, womm[(7, 3)], [woev_t[(7, ch)] for ch in range(4)]


def build_l1attn_test():
    nc = bass.Bass("TRN2", target_bir_lowering=False)
    x = nc.dram_tensor("x", [T, D], F32, kind="ExternalInput").ap()
    wqkvT = nc.dram_tensor("wqkvT", [D, 3072], F32, kind="ExternalInput").ap()
    woutT = nc.dram_tensor("woutT", [D, D], F32, kind="ExternalInput").ap()
    y = nc.dram_tensor("y", [T, D], F32, kind="ExternalOutput").ap()
    with ExitStack() as stack:
        block = stack.enter_context(nc.Block())
        c = make_ctx(stack, nc, block)
        const_t = emit_consts(c)
        x_t, _ = emit_load_x_transposed(c, x, const_t)
        ln_t, _ = emit_ln(c, x_t)
        at_t, _, _ = emit_l1attn(c, ln_t, wqkvT, woutT, xt_prev_t=None)
        st = emit_store_x_tokenmajor(c, y, at_t)
        emit_wait_store(c, st)
    return nc



def emit_l0attn_consts(c, ln_t, bkv_hbm):
    af = c.arenaAf
    stage = af[:, 10240:12288]
    A = c.arenaA
    cosB = A[:, 16384:18432]
    sinB = A[:, 18432:20480]

    @c.block.sync
    def _(s):
        c.dx[7].inc(s.dma_start(out=c.cw8[0:1, 0:2048], in_=bkv_hbm), 16)
    bkv_t = c.dx[7].t

    @c.block.gpsimd
    def _(g):
        wt(g, ln_t)
        c.sP.inc(g.iota(stage, pattern=[[1, 2048]], base=0, channel_multiplier=0,
                        allow_small_or_imprecise_dtypes=True))
        c.sP.inc(g.iota(c.tmp0[:, 0:16], pattern=[[128, 16]], base=0,
                        channel_multiplier=1, allow_small_or_imprecise_dtypes=True))
    iota_t = c.sP.t

    @c.block.scalar
    def _(a):
        wt(a, iota_t)
        c.sA.inc(a.activation(out=cosB, in_=stage, func=AF.Sin,
                              scale=-PI / 2 / T, bias=c.bias[:, 56:57]))
        c.sA.inc(a.activation(out=sinB, in_=stage, func=AF.Sin, scale=PI / 2 / T))
        c.sA.inc(a.activation(out=c.poscol[:, 0:16], in_=c.tmp0[:, 0:16], func=AF.Sin,
                              scale=-PI / 2 / T, bias=c.bias[:, 56:57]))
        c.sA.inc(a.activation(out=c.poscol[:, 16:32], in_=c.tmp0[:, 0:16], func=AF.Sin,
                              scale=PI / 2 / T))
    return c.sA.t, c.sA.t, bkv_t


def emit_l0attn(c, xln_t, wqkv, wout, b_dma_t, pos_t, bkv_t, xt_prev_t,
                xln_ch=None):
    A = c.arenaA
    ctxT = A[:, 0:16384]
    cosB = A[:, 16384:18432]
    sinB = A[:, 18432:20480]
    qcT = A[:, 20480:22528]
    qsT = A[:, 22528:24576]
    kc = A[:, 24576:26624]
    ks = A[:, 26624:28672]
    vtok = A[:, 28672:30720]
    kvh = A[:, 30720:30976]
    stg = [A[:, 30976:31104], A[:, 31104:31232]]
    bkv = c.cw8[0:1, 0:2048]

    bank_free = [c.pb[i].free_t for i in range(4)]
    pb4_free = c.pb[4].free_t
    pb5_free = c.pb[5].free_t
    pt_free = [c.pt[0].free_t, c.pt[1].free_t]
    tmp_free = [None, None]
    wslot_pe = [None, None]
    ctx_all = []

    for hp in range(8):
        par = hp % 2
        wq = c.wbuf[:, 3072 * par:3072 * par + 1024]
        wk = c.wbuf[:, 3072 * par + 1024:3072 * par + 2048]
        wv = c.wbuf[:, 3072 * par + 2048:3072 * par + 3072]
        dq, dkv = c.dw[0 + par], c.dw[2 + par]
        wdma = [(dq, dq.n + 16), (dkv, dkv.n + 32)]
        w_reuse = wslot_pe[par] if wslot_pe[par] is not None else xt_prev_t

        pe_n, a_n, v_n = c.pe.n, c.sA.n, c.sV.n
        q_mm, q_w = {}, {}
        for ch in range(4):
            q_w[ch] = [bank_free[ch]]
            if ch == 0:
                q_w[ch].append(wdma[0])
            if hp == 0:
                q_w[ch].append(xln_ch[ch] if xln_ch else xln_t)
            pe_n += 1
            q_mm[ch] = (c.pe, pe_n)
        qev_t, qmul_t = {}, {}
        for ch in range(4):
            a_n += 1
            qev_t[ch] = (c.sA, a_n)
            v_n += 2
            qmul_t[ch] = (c.sV, v_n)
            bank_free[ch] = qev_t[ch]
        qev_w = {}
        for ch in range(4):
            qev_w[ch] = [q_mm[ch], (qmul_t[ch - 2] if ch >= 2 else tmp_free[ch % 2]),
                         b_dma_t if (hp == 0 and ch == 0) else None,
                         pos_t if (hp == 0 and ch < 2) else None]
        kv_mm, kv_w = {}, {}
        kvev_a, kvev_v = {}, {}
        for ti in range(NT):
            b = ti % 4
            kv_w[ti] = [bank_free[b]] + ([wdma[1], bkv_t] if ti == 0 else []) + \
                ([xln_ch[ti // 4]] if (hp == 0 and xln_ch and ti % 4 == 0) else [])
            pe_n += 1
            kv_mm[ti] = (c.pe, pe_n)
            a_n += 2
            kvev_a[ti] = (c.sA, a_n)
            v_n += 2
            kvev_v[ti] = (c.sV, v_n)
            bank_free[b] = kvev_a[ti]
        kvev_aw = {}
        for ti in range(NT):
            kvev_aw[ti] = [kv_mm[ti], (kvev_v[ti - 2] if ti >= 2 else qmul_t[2 + ti % 2])]
        kvacc_mm, kvacc_w = {}, {}
        for ti in range(NT):
            kvacc_w[ti] = [kvev_v[ti], kvev_a[ti]] + \
                ([pb4_free, pb5_free] if ti == 0 else [])
            pe_n += 1
            kvacc_mm[ti] = (c.pe, pe_n)
        a_n += 8
        kvh_t = (c.sA, a_n)
        ctx_mm, ctx_w = {}, {}
        recip_t = {}
        mul_t, mul_w = {}, {}
        tr_mm, tr_w = {}, {}
        trev_t = {}
        for r in range(8):
            t0 = 2 * r
            ctx_w[r] = [kvh_t if r == 0 else None] +                 [bank_free[b] for b in range(4)] if r < 2 else                 [mul_t[(2 * (r - 2) + (b // 2), b % 2)] for b in range(4)]
            if r == 0:
                ctx_w[r] = [kvh_t] + [bank_free[b] for b in range(4)]
            elif r == 1:
                ctx_w[r] = [bank_free[b] for b in range(4)] if False else                     [mul_t.get((2 * (r - 1) + 0, 0))] and [None]
            pe_n += 1
            ctx_mm[r] = (c.pe, pe_n)
            v_n += 2
            recip_t[r] = (c.sV, v_n)
            for u in range(4):
                ti, hloc = t0 + u // 2, u % 2
                mul_w[(ti, hloc)] = [recip_t[r],
                                     (c.pe, tr_mm[ti - 2][1]) if ti >= 2 else None]
                a_n += 1
                mul_t[(ti, hloc)] = (c.sA, a_n)
            for ti in (t0, t0 + 1):
                tr_w[ti] = [mul_t[(ti, 0)], mul_t[(ti, 1)],
                            (pt_free[ti % 2] if ti < 2 else trev_t[ti - 2])]
                pe_n += 1
                tr_mm[ti] = (c.pe, pe_n)
                a_n += 1
                trev_t[ti] = (c.sA, a_n)
                pt_free[ti % 2] = trev_t[ti]
        for r in range(8):
            if r == 0:
                ctx_w[r] = [kvh_t] + [bank_free[b] for b in range(4)]
            elif r == 1:
                ctx_w[r] = []
            else:
                ctx_w[r] = [mul_t[(2 * (r - 2) + 1, 1)]]
        for b in range(4):
            bank_free[b] = mul_t[(2 * 7 + b // 2, b % 2)]
        tmp_free = [kvev_v[NT - 2], kvev_v[NT - 1]]
        wslot_pe[par] = kvacc_mm[NT - 1]
        wslot_pe[par] = kv_mm[NT - 1]
        pb4_free = kvh_t
        pb5_free = kvh_t
        ctx_all.append(trev_t[NT - 1])

        @c.block.gpsimd
        def _(g, hp=hp, wq=wq, wk=wk, wv=wv, w_reuse=w_reuse, dq=dq, dkv=dkv):
            wt(g, w_reuse)
            dq.inc(g.dma_start(out=wq, in_=wqkv[:, 128 * hp:128 * (hp + 1)]
                               .rearrange("(k p) f -> p k f", p=128)), 16)
            dkv.inc(g.dma_start(out=wk, in_=wqkv[:, 1024 + 128 * hp:1024 + 128 * (hp + 1)]
                                .rearrange("(k p) f -> p k f", p=128)), 16)
            dkv.inc(g.dma_start(out=wv, in_=wqkv[:, 2048 + 128 * hp:2048 + 128 * (hp + 1)]
                                .rearrange("(k p) f -> p k f", p=128)), 16)

        @c.block.tensor
        def _(t, hp=hp, wq=wq, wk=wk, wv=wv, q_w=q_w, kv_w=kv_w, kvacc_w=kvacc_w,
              ctx_w=ctx_w, tr_w=tr_w):
            for ch in range(4):
                wt(t, *q_w[ch])
                for k in range(ND):
                    ins = t.matmul(c.pb[ch].t[:, :], wq[:, 128 * k:128 * (k + 1)],
                                   c.xln[:, T * k + 512 * ch:T * k + 512 * (ch + 1)],
                                   start=(k == 0), stop=(k == ND - 1))
                c.pe.inc(ins)
            for ti in range(NT):
                b = ti % 4
                wt(t, *kv_w[ti])
                for k in range(ND):
                    t.matmul(c.pb[b].t[:, 0:128],
                             c.xln[:, T * k + 128 * ti:T * k + 128 * (ti + 1)],
                             wk[:, 128 * k:128 * (k + 1)],
                             start=(k == 0), stop=False)
                t.matmul(c.pb[b].t[:, 0:128], c.ones1[:, :],
                         bkv[0:1, 128 * hp:128 * (hp + 1)], start=False, stop=False)
                for k in range(ND):
                    t.matmul(c.pb[b].t[:, 128:256],
                             c.xln[:, T * k + 128 * ti:T * k + 128 * (ti + 1)],
                             wv[:, 128 * k:128 * (k + 1)],
                             start=False, stop=False)
                ins = t.matmul(c.pb[b].t[:, 128:256], c.ones1[:, :],
                               bkv[0:1, 1024 + 128 * hp:1024 + 128 * (hp + 1)],
                               start=False, stop=True)
                c.pe.inc(ins)
            for ti in range(NT):
                wt(t, *kvacc_w[ti])
                t.matmul(c.pb[4].t[:, 0:128], kc[:, 128 * ti:128 * (ti + 1)],
                         vtok[:, 128 * ti:128 * (ti + 1)],
                         start=(ti == 0), stop=False, skip_group_check=True)
                t.matmul(c.pb[4].t[:, 128:129], kc[:, 128 * ti:128 * (ti + 1)],
                         c.onescol[:, :], start=False, stop=(ti == NT - 1),
                         skip_group_check=True)
                t.matmul(c.pb[5].t[:, 0:128], ks[:, 128 * ti:128 * (ti + 1)],
                         vtok[:, 128 * ti:128 * (ti + 1)],
                         start=(ti == 0), stop=False, skip_group_check=True)
                ins = t.matmul(c.pb[5].t[:, 128:129], ks[:, 128 * ti:128 * (ti + 1)],
                               c.onescol[:, :], start=False, stop=(ti == NT - 1),
                               skip_group_check=True)
                c.pe.inc(ins)
            for r in range(8):
                t0 = 2 * r
                wt(t, *ctx_w[r])
                cb = 65 * (r % 2)
                for phase in range(2):
                    for u in range(4):
                        ti, hloc = t0 + u // 2, u % 2
                        b = 2 * (ti % 2) + hloc
                        h0 = 64 * hloc
                        src = qcT if phase == 0 else qsT
                        rhs = kvh[h0:h0 + 64, 0:65] if phase == 0 else kvh[h0:h0 + 64, 65:130]
                        ins = t.matmul(c.pb[b].t[:, cb:cb + 65],
                                       src[h0:h0 + 64, 128 * ti:128 * (ti + 1)],
                                       rhs, start=(phase == 0), stop=(phase == 1),
                                       tile_position=(h0, 0), skip_group_check=True)
                c.pe.inc(ins)
                for ti in (t0, t0 + 1):
                    wt(t, *tr_w[ti])
                    c.pe.inc(t.transpose(c.pt[ti % 2].t[:, 0:128], stg[ti % 2], c.id_bf[:]))

        @c.block.scalar
        def _(a, hp=hp, qev_w=qev_w, kvev_aw=kvev_aw, tr_mm=tr_mm, trev_t=trev_t,
              kvacc_mm=kvacc_mm):
            for ch in range(4):
                wt(a, *qev_w[ch])
                tdst = c.tmp0 if ch % 2 == 0 else c.tmp1
                c.sA.inc(a.activation(out=tdst[:, :], in_=c.pb[ch].t[:, :], func=AF.Relu,
                                      bias=c.bias[:, 40 + hp:41 + hp]))
            for ti in range(NT):
                b = ti % 4
                wt(a, *kvev_aw[ti])
                tdst = c.tmp0 if ti % 2 == 0 else c.tmp1
                c.sA.inc(a.activation(out=tdst[:, 0:128], in_=c.pb[b].t[:, 0:128],
                                      func=AF.Relu))
                c.sA.inc(a.copy(out=vtok[:, 128 * ti:128 * (ti + 1)],
                                in_=c.pb[b].t[:, 128:256]))
            wt(a, kvacc_mm[NT - 1])
            c.sA.inc(a.copy(out=kvh[0:64, 0:64], in_=c.pb[4].t[0:64, 0:64]))
            c.sA.inc(a.copy(out=kvh[64:128, 0:64], in_=c.pb[4].t[64:128, 64:128]))
            c.sA.inc(a.copy(out=kvh[0:64, 64:65], in_=c.pb[4].t[0:64, 128:129]))
            c.sA.inc(a.copy(out=kvh[64:128, 64:65], in_=c.pb[4].t[64:128, 128:129]))
            c.sA.inc(a.copy(out=kvh[0:64, 65:129], in_=c.pb[5].t[0:64, 0:64]))
            c.sA.inc(a.copy(out=kvh[64:128, 65:129], in_=c.pb[5].t[64:128, 64:128]))
            c.sA.inc(a.copy(out=kvh[0:64, 129:130], in_=c.pb[5].t[0:64, 128:129]))
            c.sA.inc(a.copy(out=kvh[64:128, 129:130], in_=c.pb[5].t[64:128, 128:129]))
            for r in range(8):
                t0 = 2 * r
                cb = 65 * (r % 2)
                rcd = c.cws if r % 2 == 0 else c.gacc
                for u in range(4):
                    ti, hloc = t0 + u // 2, u % 2
                    b = 2 * (ti % 2) + hloc
                    h0 = 64 * hloc
                    wt(a, *mul_w[(ti, hloc)])
                    c.sA.inc(a.mul(out=stg[ti % 2][:, 64 * hloc:64 * hloc + 64],
                                   in_=c.pb[b].t[:, cb:cb + 64], mul=rcd[:, 4 + u:5 + u]))
                for ti in (t0, t0 + 1):
                    wt(a, trev_t[ti] if False else tr_mm[ti])
                    c.sA.inc(a.copy(out=ctxT[:, 2048 * hp + 128 * ti:2048 * hp + 128 * (ti + 1)],
                                    in_=c.pt[ti % 2].t[:, 0:128]))

        @c.block.vector
        def _(v, hp=hp, qev_t=qev_t, kvev_a=kvev_a, ctx_mm=ctx_mm, mul_t=mul_t, pos_t=pos_t):
            if hp == 0:
                wt(v, pos_t)
            for ch in range(4):
                wt(v, qev_t[ch])
                tsrc = c.tmp0 if ch % 2 == 0 else c.tmp1
                c.sV.inc(v.tensor_tensor(out=qcT[:, 512 * ch:512 * (ch + 1)],
                                         in0=tsrc[:, :], in1=cosB[:, 512 * ch:512 * (ch + 1)],
                                         op=ALU.mult))
                c.sV.inc(v.tensor_tensor(out=qsT[:, 512 * ch:512 * (ch + 1)],
                                         in0=tsrc[:, :], in1=sinB[:, 512 * ch:512 * (ch + 1)],
                                         op=ALU.mult))
            for ti in range(NT):
                wt(v, kvev_a[ti])
                tsrc = c.tmp0 if ti % 2 == 0 else c.tmp1
                c.sV.inc(v.tensor_scalar(out=kc[:, 128 * ti:128 * (ti + 1)],
                                         in0=tsrc[:, 0:128], scalar1=c.poscol[:, ti:ti + 1],
                                         scalar2=None, op0=ALU.mult))
                c.sV.inc(v.tensor_scalar(out=ks[:, 128 * ti:128 * (ti + 1)],
                                         in0=tsrc[:, 0:128], scalar1=c.poscol[:, 16 + ti:17 + ti],
                                         scalar2=None, op0=ALU.mult))
            normsv = c.pbig_t[:, :].rearrange("p (b x) -> p b x", x=512)
            for r in range(8):
                cb = 65 * (r % 2)
                rcd = c.cws if r % 2 == 0 else c.gacc
                wt(v, ctx_mm[r])
                if r >= 2:
                    wt(v, mul_t[(2 * (r - 2) + 1, 1)])
                c.sV.inc(v.tensor_scalar(out=rcd[:, 0:4],
                                         in0=normsv[:, :, cb + 64:cb + 65],
                                         scalar1=1e-4, scalar2=None, op0=ALU.add))
                v.wait_ge(c.sV.h, c.sV.n)
                c.sV.inc(v.reciprocal(out=rcd[:, 4:8], in_=rcd[:, 0:4]))

    for i in range(4):
        c.pb[i].free_t = bank_free[i]
    c.pb[4].free_t = pb4_free
    c.pb[5].free_t = pb5_free
    c.pt[0].free_t = pt_free[0]
    c.pt[1].free_t = pt_free[1]
    ctx_last = ctx_all[-1]

    v0 = c.sV.n
    pe0 = c.pe.n
    wo_sem = [c.dw[8], c.dw[9]]
    wo_n0 = [c.dw[8].n, c.dw[9].n]
    wodma = {j: (wo_sem[j % 2], wo_n0[j % 2] + 16 * (j // 2 + 1)) for j in range(8)}
    woev_t = {(j, ch): (c.sV, v0 + 4 * j + ch + 1) for j in range(8) for ch in range(4)}
    womm = {}
    pe_n = pe0
    wo_w = {}
    bank_free = [c.pb[i].free_t for i in range(4)]
    for j in range(8):
        for ch in range(4):
            u = 4 * j + ch
            wo_w[(j, ch)] = [bank_free[ch] if u < 4 else woev_t[(j - 1, ch)]]
            pe_n += 1
            womm[(j, ch)] = (c.pe, pe_n)

    @c.block.gpsimd
    def _(g, wsl0=wslot_pe[0], wsl1=wslot_pe[1]):
        for j in range(8):
            if j >= 2:
                wt(g, womm[(j - 2, 3)])
            else:
                wt(g, wsl0, wsl1)
            wo_sem[j % 2].inc(g.dma_start(
                out=c.wbuf[:, 1024 * (j % 2):1024 * (j % 2 + 1)],
                in_=wout[:, 128 * j:128 * (j + 1)]
                .rearrange("(k p) f -> p k f", p=128)), 16)

    @c.block.tensor
    def _(t, ctx_last=ctx_last):
        wt(t, ctx_last)
        for j in range(8):
            wsl = c.wbuf[:, 1024 * (j % 2):1024 * (j % 2 + 1)]
            wt(t, wodma[j])
            for ch in range(4):
                wt(t, *wo_w[(j, ch)])
                for e in range(8):
                    ins = t.matmul(c.pb[ch].t[:, :], wsl[:, 128 * e:128 * (e + 1)],
                                   ctxT[:, 2048 * e + 512 * ch:2048 * e + 512 * (ch + 1)],
                                   start=(e == 0), stop=(e == 7))
                c.pe.inc(ins)

    @c.block.vector
    def _(v, xt_prev_t=xt_prev_t, b_dma_t=b_dma_t):
        wt(v, xt_prev_t, b_dma_t)
        for j in range(8):
            for ch in range(4):
                wt(v, womm[(j, ch)])
                dst = c.xT[:, T * j + 512 * ch:T * j + 512 * (ch + 1)]
                c.sV.inc(v.scalar_tensor_tensor(out=dst, in0=c.pb[ch].t[:, :],
                                                scalar=c.bias[:, 48 + j:49 + j],
                                                in1=dst, op0=ALU.add, op1=ALU.add))
    for j in range(8):
        for ch in range(4):
            c.pb[ch].mm_t = womm[(j, ch)]
            c.pb[ch].free_t = woev_t[(j, ch)]
    return c.sV.t, womm[(7, 3)], [woev_t[(7, ch)] for ch in range(4)]


def build_l0attn_test():
    nc = bass.Bass("TRN2", target_bir_lowering=False)
    x = nc.dram_tensor("x", [T, D], F32, kind="ExternalInput").ap()
    wqkv = nc.dram_tensor("wqkv", [D, 3072], F32, kind="ExternalInput").ap()
    wout = nc.dram_tensor("wout", [D, D], F32, kind="ExternalInput").ap()
    bq = nc.dram_tensor("bq", [128, 8], F32, kind="ExternalInput").ap()
    bo = nc.dram_tensor("bo", [128, 8], F32, kind="ExternalInput").ap()
    bkv = nc.dram_tensor("bkv", [1, 2048], F32, kind="ExternalInput").ap()
    y = nc.dram_tensor("y", [T, D], F32, kind="ExternalOutput").ap()
    with ExitStack() as stack:
        block = stack.enter_context(nc.Block())
        c = make_ctx(stack, nc, block)
        const_t = emit_consts(c)
        bq_t = emit_bias_cols(c, 40, bq, 8)
        bo_t = emit_bias_cols(c, 48, bo, 8)
        x_t, _ = emit_load_x_transposed(c, x, const_t)
        ln_t, _ = emit_ln(c, x_t)
        cos_t, pos_t, bkv_t = emit_l0attn_consts(c, ln_t, bkv)
        at_t, _, _ = emit_l0attn(c, ln_t, wqkv, wout, bo_t, (cos_t if True else pos_t),
                                 bkv_t, xt_prev_t=None)
        st = emit_store_x_tokenmajor(c, y, at_t)
        emit_wait_store(c, st)
    return nc



def build_full():
    nc = bass.Bass("TRN2", target_bir_lowering=False)
    dt = lambda n, sh: nc.dram_tensor(n, sh, F32, kind="ExternalInput").ap()
    x = dt("x", [T, D])
    wqkv0 = dt("wqkv0", [D, 3072])
    wout0 = dt("wout0", [D, D])
    bq0 = dt("bq0", [128, 8])
    bo0 = dt("bo0", [128, 8])
    bkv0 = dt("bkv0", [1, 2048])
    wrg = dt("wrg", [128, 128])
    w_gate = dt("w_gate", [8, D, 4096])
    w_up = dt("w_up", [8, D, 4096])
    w_down = dt("w_down", [8, 4096, D])
    wqkvT1 = dt("wqkvT1", [D, 3072])
    woutT1 = dt("woutT1", [D, D])
    w1t = dt("w1t", [128, 32768])
    w2t = dt("w2t", [128, 32768])
    b1c = dt("b1c", [128, 32])
    b2c = dt("b2c", [128, 8])
    y = nc.dram_tensor("y", [T, D], F32, kind="ExternalOutput").ap()
    with ExitStack() as stack:
        block = stack.enter_context(nc.Block())
        c = make_ctx(stack, nc, block)
        with nc.named_scope("setup"):
            const_t = emit_consts(c)
            emit_bias_cols(c, 0, b1c, 32)
            emit_bias_cols(c, 32, b2c, 8)
            emit_bias_cols(c, 40, bq0, 8)
            bias_t = emit_bias_cols(c, 48, bo0, 8)
            wrg_t = emit_moe_wrg(c, wrg)
            x_t, x_ch = emit_load_x_transposed(c, x, const_t)
        with nc.named_scope("ln1"):
            ln1_t, ln1_ch = emit_ln(c, x_t, ready_chunk=x_ch,
                                    tscratch=[c.tmp0, c.tmp1, c.tmp0, c.tmp1],
                                    sq_engine="vector", rows_in_quarters=True)
            cos_t, pos_t, bkv_t = emit_l0attn_consts(c, ln1_t, bkv0)
        with nc.named_scope("l0attn"):
            at0_t, l0_pe, at0_ch = emit_l0attn(c, ln1_t, wqkv0, wout0, bias_t, cos_t,
                                               bkv_t, xt_prev_t=None, xln_ch=ln1_ch)
        with nc.named_scope("ln2"):
            ln2_t, ln2_ch = emit_ln(c, at0_t, at0_ch)
        with nc.named_scope("moecw"):
            cw8_t = emit_moe_cw(c, ln2_t, wrg_t, xln_ch=ln2_ch)
        with nc.named_scope("moe"):
            moe_t, moe_pe, moe_ch = emit_l0ffn(c, ln2_t, w_gate, w_up, w_down, cw8_t,
                                               arena_free_pe_t=l0_pe, wbuf_free_t=l0_pe,
                                               wbuf_free_pe_t=l0_pe, xln_ch=ln2_ch)
        with nc.named_scope("ln3"):
            ln3_t, ln3_ch = emit_ln(c, moe_t, moe_ch)
        with nc.named_scope("l1attn"):
            ghook, acc1_t, acc2_t = gamma_stream(c, w1t, w2t, D * 4096,
                                                 wbuf_free_t=moe_t)
            at1_t, l1_pe, at1_ch = emit_l1attn(c, ln3_t, wqkvT1, woutT1, xt_prev_t=moe_t,
                                               wbuf_free_pe_t=moe_pe, scalar_hook=ghook,
                                               xln_ch=ln3_ch)
        with nc.named_scope("gammas"):
            r1_t = emit_gamma_tail(c, D * 4096, 0, acc1_t)
            r2_t = emit_gamma_tail(c, D * 4096, 1, acc2_t)
        with nc.named_scope("ln4"):
            ln4_t, ln4_ch = emit_ln(c, at1_t, at1_ch, apply_extra=r2_t)
        with nc.named_scope("l1ffn"):
            f1_t, _, _, adds = emit_l1ffn(c, ln4_t, w1t, w2t, r1_t, r2_t, bias_t,
                                          wbuf_free_t=l1_pe, wbuf_free_pe_t=l1_pe,
                                          xln_ch=ln4_ch)
        with nc.named_scope("store"):
            st2 = emit_store_overlapped(c, y, adds, r2_t)
            emit_wait_store2(c, st2)
    return nc


def _layouts(inp):
    ac = np.ascontiguousarray
    b_qkv = inp["L0_attn_b_qkv"]
    wrg = np.concatenate([
        inp["L0_ffn_w_router"].reshape(8, 128, 8).transpose(1, 0, 2).reshape(128, 64),
        inp["L0_ffn_w_gating"].reshape(8, 128, 8).transpose(1, 0, 2).reshape(128, 64)],
        axis=1)
    return {
        "wqkv0": ac(inp["L0_attn_W_qkv"]),
        "wout0": ac(inp["L0_attn_W_out"]),
        "bq0": ac(b_qkv[0:1024].reshape(8, 128).T),
        "bo0": ac(inp["L0_attn_b_out"].reshape(8, 128).T),
        "bkv0": ac(b_qkv[1024:3072].reshape(1, 2048)),
        "wrg": ac(wrg),
        "w_gate": ac(inp["L0_ffn_w_gate"]),
        "w_up": ac(inp["L0_ffn_w_up"]),
        "w_down": ac(inp["L0_ffn_w_down"]),
        "wqkvT1": ac(inp["L1_attn_w_qkv"].T),
        "woutT1": ac(inp["L1_attn_w_out"].T),
        "w1t": ac(inp["L1_ffn_W1"].T.reshape(8, 128, 32, 128)
                  .transpose(1, 2, 0, 3).reshape(128, 32768)),
        "w2t": ac(inp["L1_ffn_W2"].T.reshape(32, 128, 8, 128)
                  .transpose(1, 2, 0, 3).reshape(128, 32768)),
        "b1c": ac(inp["L1_ffn_b1"].reshape(32, 128).T),
        "b2c": ac(inp["L1_ffn_b2"].reshape(8, 128).T),
    }


def kernel(**inputs):
    weights = _layouts(inputs)
    xfull = np.ascontiguousarray(inputs["L0_attn_x"], dtype=np.float32)
    nc = build_full()
    in_maps = []
    for b in range(8):
        m = dict(weights)
        m["x"] = np.ascontiguousarray(xfull[b])
        in_maps.append(m)
    res = run_bass_kernel_spmd(nc, in_maps, core_ids=list(range(8)))
    out = np.stack([res.results[b]["y"] for b in range(8)], axis=0)
    return out.astype(np.float32)


def emit_store_overlapped(c, out_hbm, adds, dma_prior_t):
    tmps = [c.tmp0, c.tmp1]
    yr = out_hbm.rearrange("(ti p) f -> p ti f", p=128)
    pe_n, a_n = c.pe.n, c.sA.n
    tr_w, tr_t = {}, {}
    ev_w, ev_t = {}, {}
    dma_t = {}
    bank_free = [c.pb[i].free_t for i in range(4)]
    dsem = [c.dx[0], c.dx[2]]
    d0 = [c.dx[0].n, c.dx[2].n]
    for m in range(8):
        for ti in range(NT):
            u = NT * m + ti
            Q = ti // 4
            prev_bank_ev = None
            if u >= 4:
                pm, pti = (m, ti - 4) if ti >= 4 else (m - 1, ti + 12)
                prev_bank_ev = ev_t[(pm, pti)]
            tr_w[(m, ti)] = [adds[m] if ti == 0 else None,
                             bank_free[ti % 4] if u < 4 else prev_bank_ev]
            pe_n += 1
            tr_t[(m, ti)] = (c.pe, pe_n)
            j = 2 * m + Q // 2
            prev_dma = (dsem[Q % 2], 16 * j + d0[Q % 2]) if j >= 1 else dma_prior_t
            ev_w[(m, ti)] = [tr_t[(m, ti)], prev_dma if ti % 4 == 0 else None]
            a_n += 1
            ev_t[(m, ti)] = (c.sA, a_n)
        for Q in range(4):
            dma_t[(m, Q)] = (dsem[Q % 2], d0[Q % 2] + 16 * (2 * m + Q // 2 + 1))

    @c.block.tensor
    def _(t):
        for m in range(8):
            for ti in range(NT):
                wt(t, *tr_w[(m, ti)])
                c.pe.inc(t.transpose(c.pb[ti % 4].t[:, 0:128],
                                     c.xT[:, T * m + 128 * ti:T * m + 128 * (ti + 1)],
                                     c.id_f32[:]))

    @c.block.scalar
    def _(a):
        for m in range(8):
            for ti in range(NT):
                wt(a, *ev_w[(m, ti)])
                tmp = tmps[(ti // 4) % 2]
                c.sA.inc(a.copy(out=tmp[:, 128 * (ti % 4):128 * (ti % 4 + 1)],
                                in_=c.pb[ti % 4].t[:, 0:128]))

    @c.block.sync
    def _(s):
        if dma_prior_t is not None:
            wt(s, dma_prior_t)
        for m in range(8):
            for Q in range(4):
                wt(s, ev_t[(m, 4 * Q + 3)])
                dsem[Q % 2].inc(s.dma_start(
                    out=yr[:, 4 * Q:4 * (Q + 1), 128 * m:128 * (m + 1)],
                    in_=tmps[Q % 2][:, :].rearrange("p (ti f) -> p ti f", f=128)), 16)
    for i in range(4):
        c.pb[i].free_t = ev_t[(7, 12 + i)]
    return [dma_t[(7, 2)], dma_t[(7, 3)]]


def emit_wait_store2(c, st2):
    @c.block.sync
    def _(s):
        for t in st2:
            wt(s, t)


def gamma_stream(c, w1_hbm, w2_hbm, n_elems, wbuf_free_t):
    chunk = 512
    wf = c.wbuf[:, :].bitcast(F32)
    bufs = [c.tmp0[:, :], c.tmp1[:, :], wf[:, 3072:3584], wf[:, 3584:4096]]
    nck = (n_elems // 128) // chunk
    wrs = [w.rearrange("a b -> (a b)").rearrange("(p n) -> p n", p=128)
           for w in (w1_hbm, w2_hbm)]
    a20 = c.sA2.n
    dma_t = []

    @c.block.sync
    def _(s):
        for i in range(2 * nck):
            if i >= 4:
                s.wait_ge(c.sA2.h, a20 + 2 * (i - 4) + 1)
            else:
                wt(s, wbuf_free_t)
            sem = [c.dx[2], c.dx[3], c.dx[5], c.dx[6]][i % 4]
            w, idx = i // nck, i % nck
            sem.inc(s.dma_start(out=bufs[i % 4],
                                in_=wrs[w][:, chunk * idx:chunk * (idx + 1)]), 16)
            dma_t.append(sem.t)

    def hook(a, hp, qt):
        for i in [16 * hp + qt]:
            which = i // nck
            i_w = i % nck
            wt(a, dma_t[i])
            if i > 0:
                a.wait_ge(c.sA2.h, a20 + 2 * i)
            c.sA2.inc(a.activation(out=bufs[i % 4], in_=bufs[i % 4], func=AF.Abs,
                                   accum_out=c.gacc[:, 4 + which:5 + which]))
            a.wait_ge(c.sA2.h, c.sA2.n)
            if i_w == 0:
                ins = a.copy(out=c.gacc[:, which:which + 1],
                             in_=c.gacc[:, 4 + which:5 + which])
            else:
                ins = a.add(out=c.gacc[:, which:which + 1],
                            in_=c.gacc[:, 4 + which:5 + which],
                            add=c.gacc[:, which:which + 1])
            c.sA2.inc(ins)
    acc1_t = (c.sA2, a20 + 2 * nck)
    acc2_t = (c.sA2, a20 + 4 * nck)
    return hook, acc1_t, acc2_t


def emit_gamma_tail(c, n_elems, which, acc_t):
    bank = c.pb[5]
    freew = bank.free_t

    @c.block.tensor
    def _(t):
        wt(t, acc_t, freew)
        c.pe.inc(t.matmul(bank.t[0:1, 0:1], c.gacc[:, which:which + 1],
                          c.onescol32[:, :], start=True, stop=True))
    mm1_t = c.pe.t

    @c.block.vector
    def _(v):
        wt(v, mm1_t)
        c.sV.inc(v.tensor_scalar(out=c.gacc[0:1, 6 + which:7 + which], in0=bank.t[0:1, 0:1],
                                 scalar1=1.0 / n_elems, scalar2=QEPS,
                                 op0=ALU.mult, op1=ALU.add))
        v.wait_ge(c.sV.h, c.sV.n)
        c.sV.inc(v.reciprocal(out=c.gacc[0:1, 6 + which:7 + which],
                              in_=c.gacc[0:1, 6 + which:7 + which]))
    rec_t = c.sV.t

    @c.block.tensor
    def _(t):
        wt(t, rec_t)
        c.pe.inc(t.matmul(bank.t[:, 0:1], c.ones1[:, :],
                          c.gacc[0:1, 6 + which:7 + which], start=True, stop=True))
    mm2_t = c.pe.t
    bank.mm_t = mm2_t

    @c.block.vector
    def _(v):
        wt(v, mm2_t)
        c.sV.inc(v.tensor_copy(out=c.gacc[:, 2 + which:3 + which], in_=bank.t[:, 0:1]))
    bank.free_t = c.sV.t
    return c.sV.t
